# version 02
# baseline (speedup 1.0000x reference)
import numpy as np
import concourse.bass as bass
import concourse.mybir as mybir
from concourse.bass_utils import run_bass_kernel_spmd

F32 = mybir.dt.float32
BF16 = mybir.dt.bfloat16
I8 = mybir.dt.int8
AL = mybir.AluOpType
AF = mybir.ActivationFunctionType
AX = mybir.AxisListType

C0 = float(np.exp(-0.5))
H, Dh, L, NCH, NTOK, NFC = 16, 64, 128, 16, 2048, 8


class Em:
    ENGS = ("T", "D", "A", "G", "S")

    def __init__(self, nc):
        self.nc = nc
        self.ops = {e: [] for e in self.ENGS}
        self.cnt = {e: 0 for e in self.ENGS}
        self.dcnt = {q: 0 for q in ("S", "G", "A")}
        self.lastw = {e: {} for e in self.ENGS}
        self.pg = {}
        self.sems = {}
        self._ctx = []
        self.drn = 0
        self.mm_sig = None
        self.mm_idx = -1
        self.t_ok = True
        self.cover = {}

    def make_sems(self):
        for name in ("oT", "oD", "oA", "oG", "oS", "qS", "qG", "qA", "cc",
                     "dr"):
            c = self.nc.semaphore(name)
            self.sems[name] = c.__enter__()
            self._ctx.append(c)

    def drain(self, via="S"):
        self.wait(via, "T")
        self.drn += 1
        sem = self.sems["dr"]

        def fn(e, s=sem):
            e.sem_inc(s, 1)
        self.ops[via].append(fn)
        self._wait("T", "dr", sem, self.drn)
        self.t_ok = True

    def op(self, eng, method, *args, **kwargs):
        if "pgkey" in kwargs:
            key = kwargs.pop("pgkey")
            kwargs["start"] = key not in self.pg
            kwargs["stop"] = False
            kwargs["skip_group_check"] = True
            self.pg[key] = kwargs
        if eng == "T" and method == "matmul":
            K = args[1].shape[0]
            sig = 32 if K <= 32 else 64 if K <= 64 else 128
            if self.mm_sig is not None and sig != self.mm_sig \
                    and not self.t_ok:
                self.drain()
            self.mm_sig = sig
        self.cnt[eng] += 1
        if eng == "T" and method == "matmul":
            self.mm_idx = self.cnt["T"]
            self.t_ok = False
        sem = self.sems["o" + eng]

        def fn(e, m=method, a=args, k=kwargs, s=sem):
            getattr(e, m)(*a, **k).then_inc(s, 1)
        self.ops[eng].append(fn)

    def mm_close(self, *keys):
        for k in keys:
            if k in self.pg:
                self.pg.pop(k)["stop"] = True

    def mm_close_all(self):
        self.mm_close(*list(self.pg.keys()))

    def dma(self, q, out, in_, transpose=False, slow=False):
        self.dcnt[q] += 16
        sem = self.sems["q" + q]

        def fn(e, o=out, i=in_, t=transpose, s=sem, sl=slow):
            if t:
                e.dma_start_transpose(o, i).then_inc(s, 16)
            elif sl:
                e.dma_start(out=o, in_=i,
                            allow_slow_non_contiguous=True).then_inc(s, 16)
            else:
                e.dma_start(out=o, in_=i).then_inc(s, 16)
        self.ops[q].append(fn)

    def collective(self, ins, outs, groups):
        self.nc.has_collectives = True
        sem = self.sems["cc"]

        def fn(e, i=ins, o=outs, g=groups, s=sem):
            e.collective_compute("AllGather", AL.bypass, replica_groups=g,
                                 ins=i, outs=o).then_inc(s, 1)
        self.ops["G"].append(fn)

        def fn2(e, s=sem):
            e.wait_ge(s, 1)
        self.ops["G"].append(fn2)

    def _wait(self, eng, key, sem, v):
        if v <= 0 or self.lastw[eng].get(key, 0) >= v:
            return
        self.lastw[eng][key] = v
        if eng != "T" and key == "oT":
            self.cover[eng] = (self.cnt[eng], v)
        elif eng == "T" and key != "oT" and key.startswith("o"):
            c = self.cover.get(key[1:])
            if c and v > c[0] and c[1] >= self.mm_idx:
                self.t_ok = True

        def fn(e, s=sem, vv=v):
            e.wait_ge(s, vv)
        self.ops[eng].append(fn)

    def wait(self, eng, on, val=None):
        v = self.cnt[on] if val is None else val
        self._wait(eng, "o" + on, self.sems["o" + on], v)

    def wait_dma(self, eng, q, val=None):
        v = self.dcnt[q] if val is None else val
        self._wait(eng, "q" + q, self.sems["q" + q], v)

    def barrier(self, engs=None, dmas=None):
        engs = engs or self.ENGS
        dmas = list(self.dcnt.keys()) if dmas is None else dmas
        snap = dict(self.cnt)
        dsnap = dict(self.dcnt)
        for e in engs:
            for o in engs:
                if o != e:
                    self.wait(e, o, snap[o])
            for q in dmas:
                self.wait_dma(e, q, dsnap[q])

    def play(self):
        with self.nc.Block() as block:
            for eng, attr in (("T", "tensor"), ("D", "vector"), ("A", "scalar"),
                              ("G", "gpsimd"), ("S", "sync")):
                ops = self.ops[eng]

                def mk(lst):
                    def body(e):
                        for f in lst:
                            f(e)
                    return body
                getattr(block, attr)(mk(ops))
        for c in reversed(self._ctx):
            c.__exit__(None, None, None)


def build_nc(nchunks=NCH, do_scan=True, do_passb=True, no_cc=False, stage=99, sub=99, debug_dump=()):
    nc = bass.Bass("TRN2", target_bir_lowering=False, debug=False)
    em = Em(nc)

    def din(name, shape, dt=F32):
        return nc.dram_tensor(name, shape, dt, kind="ExternalInput").ap()

    xcf = din("xcf", [1024, NTOK + 1])
    xtok = din("xtok", [NTOK, 1024])
    selq = din("selq", [128, 8])
    wrT = din("wrT", [1024, 1024]); wkT = din("wkT", [1024, 1024])
    wvT = din("wvT", [1024, 1024]); woT = din("woT", [1024, 1024])
    wqT = din("wqT", [1024, 1024], I8)
    maa_w1T = din("maa_w1T", [1024, 128])
    maa_w2 = din("maa_w2", [128, 1024])
    gate_w1T = din("gate_w1T", [1024, 128])
    gate_w2 = din("gate_w2", [128, 1024])
    decay_w1T = din("decay_w1T", [1024, 64])
    decay_w2 = din("decay_w2", [64, 1024])
    s_w1p = din("s_w1p", [1024, 128])
    s_w2p = din("s_w2p", [128, 1024])
    vrows = din("vrows", [16, 1024])
    vfmi = din("vfmi", [128, 128])
    fblk = din("fblk", [1024, 16])
    hblk = din("hblk", [1024, 16])
    hsel = din("hsel", [16, 1024])
    ident = din("ident", [128, 128])
    idlo = din("idlo", [128, 128])
    idhi = din("idhi", [128, 128])
    iddup = din("iddup", [128, 64])
    idblk = din("idblk", [128, 32])
    mask_s = din("mask_s", [128, 128])
    mask_i = din("mask_i", [128, 128])
    onesc = din("onesc", [128, 128])

    out = nc.dram_tensor("out", [NTOK, 1024], F32, kind="ExternalOutput").ap()

    def scr(name, shape, dt=BF16):
        return nc.dram_tensor(name, shape, dt, kind="Internal").ap()
    s_yloc = scr("s_yloc", [NCH, L, 1024])
    s_zf = scr("s_zf", [NCH, 128, 8, 128])
    s_uf = scr("s_uf", [NCH, 128, 8, 128])
    s_bck = scr("s_bck", [NCH, L, 1024])
    s_p1 = scr("s_p1", [NCH, L, 1024])
    s_p2 = scr("s_p2", [NCH, L, 1024])
    cc_in = nc.dram_tensor("cc_in", [2048, 64], F32, kind="Internal").ap()
    cc_out = nc.dram_tensor("cc_out", [8192, 64], F32, kind="Internal").ap()

    SB = {}

    def sb(name, shape, dt):
        SB[name] = nc.alloc_sbuf_tensor(name, shape, dt).ap()
        return SB[name]

    def alias(name, base, shape, off=0):
        ap_ = SB[base]
        if ap_.ndim == 3:
            flat = ap_.rearrange("p a b -> p (a b)")
        else:
            flat = ap_
        n = 1
        for s in shape[1:]:
            n *= s
        v = flat[:shape[0], off:off + n]
        if len(shape) == 3:
            v = v.rearrange("p (a b) -> p a b", b=shape[2])
        SB[name] = v
        return v

    w_bigA = sb("w_bigA", [128, 8, 1024], BF16)
    w_bigB = sb("w_bigB", [128, 8, 1024], BF16)
    w_bigC = sb("w_bigC", [128, 8, 1024], BF16)
    w_maa1 = sb("w_maa1", [128, 8, 128], BF16)
    w_maa2 = sb("w_maa2", [128, 1024], BF16)
    w_g1 = sb("w_g1", [128, 8, 128], BF16)
    w_g2 = sb("w_g2", [128, 1024], BF16)
    w_d1 = sb("w_d1", [128, 8, 64], BF16)
    w_d2 = sb("w_d2", [64, 1024], BF16)
    w_s1 = sb("w_s1", [128, 8, 128], BF16)
    w_s2 = sb("w_s2", [128, 1024], BF16)
    w_fblk = sb("w_fblk", [128, 8, 16], BF16)
    w_hblk = sb("w_hblk", [128, 8, 16], BF16)
    c_hsel = sb("c_hsel", [16, 8, 128], BF16)
    c_id = sb("c_id", [128, 128], BF16)
    c_id32 = sb("c_id32", [128, 128], F32)
    c_idblk = sb("c_idblk", [128, 32], BF16)
    c_ms = sb("c_ms", [128, 128], BF16)
    c_mi = sb("c_mi", [128, 128], BF16)
    c_one = sb("c_one", [128, 128], BF16)
    c_idlo = sb("c_idlo", [128, 128], BF16)
    c_idhi = sb("c_idhi", [128, 128], BF16)
    c_dup = sb("c_dup", [128, 64], BF16)
    c_trid = sb("c_trid", [128, 8, 64], BF16)
    v_fm = sb("v_fm", [128, 8, 16], F32)
    v_neg = sb("v_neg", [128, 8, 4], F32)
    bc_lnw = sb("bc_lnw", [128, 1024], BF16)
    bc_lnb = sb("bc_lnb", [128, 1024], BF16)
    bc_ws = sb("bc_ws", [128, 1024], BF16)
    bc_fb = sb("bc_fb", [128, 1024], BF16)
    sel = sb("sel", [128, 8], F32)
    cb = sb("cb", [128, 4], F32)

    VIDX = {n: i for i, n in enumerate(
        ("tmx", "tmaa0", "tmaa1", "tmaa2", "tmaa3", "td", "taaaaa", "tma",
         "tmk", "lnw", "lnb", "ws", "fbias"))}
    NEGI = {"td": 0, "taaaaa": 1, "tma": 2, "tmk": 3}

    bd_pn = sb("bd_pn", [128, 16, 132], BF16)
    xlc = alias("xlc", "bd_pn", [128, 8, 132])
    xsq = alias("xsq", "bd_pn", [128, 8, 132], off=1056)
    sc_m = sb("sc_m", [128, 132], BF16)
    sc_r = sb("sc_r", [128, 132], BF16)
    sc_f1 = sb("sc_f1", [128, 132], F32)
    sc_f2 = sb("sc_f2", [128, 132], F32)
    u_tk = sb("u_tk", [128, 1024], BF16)
    f_xx = sb("f_xx", [128, 8, L], BF16)
    f_mix = sb("f_mix", [128, 8, L], BF16)
    f_xm = alias("f_xm", "f_mix", [128, 8, L])
    f_r = sb("f_r", [128, 8, L], BF16)
    f_k0 = sb("f_k0", [128, 8, L], BF16)
    f_kk = sb("f_kk", [128, 8, L], BF16)
    f_kkn = sb("f_kkn", [128, 8, L], BF16)
    f_k = sb("f_k", [128, 8, L], BF16)
    f_a = sb("f_a", [128, 8, L], BF16)
    f_ap = sb("f_ap", [128, 8, L], BF16)
    f_ma = sb("f_ma", [128, 8, L], BF16)
    f_mk = sb("f_mk", [128, 8, L], BF16)
    f_sig = sb("f_sig", [128, 8, L], F32)
    f_t1 = sb("f_t1", [128, 8, L], F32)
    f_t2 = sb("f_t2", [128, 8, L], F32)
    f_t3 = sb("f_t3", [128, 8, L], F32)
    f_rkp = sb("f_rkp", [128, 8, L], BF16)
    f_qd = alias("f_qd", "f_rkp", [128, 8, L])
    f_lo = sb("f_lo", [128, L], BF16)
    f_loP = sb("f_loP", [128, 4, L], BF16)
    f_shP = sb("f_shP", [128, 4, L], BF16)
    f_shB = sb("f_shB", [64, L], BF16)
    f_dh = sb("f_dh", [64, L], BF16)
    f_sh = sb("f_sh", [128, L], BF16)
    f_gh = sb("f_gh", [128, L], BF16)
    rn_h = sb("rn_h", [16, L], F32)
    g_buf = sb("g_buf", [128, 8, L + 1], F32)
    eg = sb("eg", [128, 8, L + 1], BF16)
    ek = sb("ek", [128, 8, L], BF16)
    f_kh = sb("f_kh", [128, 8, L], BF16)
    f_bc = sb("f_bc", [128, 8, L], BF16)
    f_kc = sb("f_kc", [128, 8, L], BF16)
    cl_all = sb("cl_all", [128, 8, NCH], F32)
    t_v = sb("t_v", [128, 1024], BF16)
    t_g = sb("t_g", [128, 1024], F32)
    t_b1 = sb("t_b1", [128, 1024], F32)
    t_p = sb("t_p", [128, 1024], BF16)
    t_p2 = sb("t_p2", [128, 1024], BF16)
    t_rk = sb("t_rk", [16, 128], BF16)
    t_rkT = sb("t_rkT", [128, 16], BF16)
    a_ss = sb("a_ss", [128, 16, 128], BF16)
    a_k = sb("a_k", [128, 16, 128], BF16)
    a_q = sb("a_q", [128, 16, 128], BF16)
    a_qk = sb("a_qk", [128, 16, 128], BF16)
    dg_nt = sb("dg_nt", [128, 16, 32], BF16)
    dg_n = sb("dg_n", [128, 16, 32], BF16)
    dg_pt = sb("dg_pt", [128, 16, 32], BF16)
    dg_pn = sb("dg_pn", [128, 16, 32], BF16)
    dg_tt = sb("dg_tt", [128, 16, 32], BF16)
    x_ri = sb("x_ri", [128, 8, 128], BF16)
    x_sol = sb("x_sol", [128, 8, 128], BF16)
    y_lc = sb("y_lc", [128, 1024], BF16)
    z_f = sb("z_f", [128, 8, 128], BF16)
    u_f = sb("u_f", [128, 8, 128], BF16)
    bck_t = sb("bck_t", [128, 1024], BF16)
    kck_t = sb("kck_t", [128, 1024], BF16)
    s_st = sb("s_st", [128, 8, 64], F32)
    s_sb = sb("s_sb", [128, 8, 64], BF16)
    tr_st = sb("tr_st", [128, 8, 64], F32)
    tr_sb = sb("tr_sb", [128, 8, 64], BF16)
    st_t1 = sb("st_t1", [128, 8, 64], F32)
    w_sb = sb("w_sb", [128, 1024], BF16)
    sict = sb("sict", [128, 8, 64], F32)
    sict_b = sb("sict_b", [128, 8, 64], BF16)
    trp = alias("trp", "dg_tt", [128, 8, 64])
    sp_f = alias("sp_f", "dg_nt", [128, 8, 64])
    gat_s = alias("gat_s", "dg_pt", [128, 8, 64])
    gat_t = alias("gat_t", "dg_pn", [128, 8, 64])
    gn_st = alias("gn_st", "sc_f1", [128, 80])
    bn6 = alias("bn6", "sc_f2", [128, 6])
    bn2c = SB["sc_f2"][:, 8:10]
    SB["bn2"] = bn2c
    bn2 = bn2c
    q_c1 = SB["sc_f2"][:, 16:24]
    SB["q_c1"] = q_c1
    y_fin = alias("y_fin", "f_t1", [128, 1024])
    x2_t = alias("x2_t", "f_t2", [128, 1024])
    xt_in = alias("xt_in", "f_t3", [128, 1024])
    lad_u = alias("lad_u", "f_sig", [128, 1024])
    lad_t = alias("lad_t", "t_b1", [128, 1024])
    lad_m = alias("lad_m", "t_g", [128, 1024])
    o_fin = alias("o_fin", "g_buf", [128, 1024])
    t_ld1 = alias("t_ld1", "f_xx", [128, 1024])
    t_ldp1 = alias("t_ldp1", "f_mix", [128, 1024])
    t_ldp2 = alias("t_ldp2", "f_rkp", [128, 1024])
    y2_t = alias("y2_t", "f_r", [128, 1024])
    y2_f = alias("y2_f", "f_k0", [128, 8, 128])
    xq_t = alias("xq_t", "f_kk", [128, 1024])
    xq_f = alias("xq_f", "f_kkn", [128, 8, 128])
    f_khO = alias("f_khO", "f_xx", [128, 8, 128])
    f_qdO = alias("f_qdO", "f_mix", [128, 8, 128])
    u_fO = alias("u_fO", "ek", [128, 8, 128])
    z_fO = alias("z_fO", "eg", [128, 8, 128])
    x_u = alias("x_u", "f_r", [128, 4, 128])
    x_h = alias("x_h", "f_r", [128, 4, 128], off=512)

    PG0 = nc.alloc_psum_tensor("PG0", [128, 8, 128], F32).ap()
    PG1 = nc.alloc_psum_tensor("PG1", [128, 8, 128], F32).ap()
    PG2 = nc.alloc_psum_tensor("PG2", [128, 8, 128], F32).ap()
    PU0 = nc.alloc_psum_tensor("PU0", [128, 512], F32).ap()
    PU1 = nc.alloc_psum_tensor("PU1", [128, 512], F32).ap()

    em.make_sems()

    def r128(ap_):
        return ap_.rearrange("(kc p) n -> p kc n", p=128)

    for dst, src in ((w_bigA, wrT), (w_bigB, wkT), (w_bigC, wvT),
                     (w_maa1, maa_w1T), (w_g1, gate_w1T),
                     (w_d1, decay_w1T), (w_s1, s_w1p), (w_fblk, fblk),
                     (w_hblk, hblk)):
        em.dma("G", dst, r128(src))
    em.dma("G", w_maa2, maa_w2)
    em.dma("G", w_g2, gate_w2)
    em.dma("G", w_d2, decay_w2)
    em.dma("G", w_s2, s_w2p)
    em.dma("G", c_hsel, hsel.rearrange("h (fc p) -> h fc p", p=128))
    em.dma("G", c_id, ident)
    em.dma("G", c_idblk, idblk)
    em.dma("G", c_ms, mask_s)
    em.dma("G", c_mi, mask_i)
    em.dma("G", c_one, onesc)
    em.dma("G", c_idlo, idlo)
    em.dma("G", c_idhi, idhi)
    em.dma("G", c_dup, iddup)
    em.dma("S", c_id32, ident)
    em.dma("S", sel, selq)
    em.dma("S", v_fm, vfmi.rearrange("p (fc i) -> p fc i", i=16))
    em.barrier()

    if stage < 1:
        nchunks = 0
    for n, j in NEGI.items():
        em.op("D", "tensor_scalar_mul", v_neg[:, :, j:j + 1],
              v_fm[:, :, VIDX[n]:VIDX[n] + 1], -1.0)
    for row_i, bc in ((VIDX["lnw"], bc_lnw), (VIDX["lnb"], bc_lnb),
                      (VIDX["ws"], bc_ws), (VIDX["fbias"], bc_fb)):
        for fc in range(8):
            em.op("D", "tensor_scalar_mul", xlc[:, fc, 0:128],
                  c_one, v_fm[:, fc, row_i:row_i + 1])
        em.wait_dma("A", "A")
        em.wait("A", "D")
        for fc in range(8):
            em.dma("A", bc[:, 128 * fc:128 * fc + 128], xlc[:, fc, 0:128],
                   transpose=True)
        em.wait_dma("D", "A")
    em.op("D", "memset", cb[:, 0:1], 1e-5)
    em.op("D", "memset", xlc, 0.0)
    em.op("D", "memset", cl_all, 0.0)
    em.op("D", "memset", x_ri, 0.0)
    em.op("D", "memset", xsq, 0.0)
    em.op("D", "memset", sc_m, 0.0)
    em.op("D", "memset", sc_r, 0.0)
    em.op("D", "memset", sc_f1, 0.0)
    em.op("D", "memset", sc_f2, 0.0)
    em.op("D", "memset", cb[:, 1:2], 1e-24)
    em.op("D", "memset", cb[:, 2:3], 0.00064)
    em.op("D", "memset", s_st, 0.0)
    em.op("D", "memset", s_sb, 0.0)
    em.op("D", "memset", sict, 0.0)
    em.op("D", "memset", sict_b, 0.0)
    em.op("D", "memset", g_buf[:, :, 0:1], 0.0)
    for pr in range(8):
        em.op("D", "tensor_copy", tr_st[0:64, pr, :], c_id32[0:64, 0:64])
        em.op("D", "tensor_copy", tr_st[64:128, pr, :], c_id32[64:128, 64:128])
        em.op("D", "tensor_copy", tr_sb[0:64, pr, :], c_id[0:64, 0:64])
        em.op("D", "tensor_copy", tr_sb[64:128, pr, :], c_id[64:128, 64:128])
        em.op("D", "tensor_copy", c_trid[0:64, pr, :], c_id[0:64, 0:64])
        em.op("D", "tensor_copy", c_trid[64:128, pr, :], c_id[64:128, 64:128])
    em.barrier()

    def head_rows(h):
        return 64 * (h % 2), h // 2

    if stage < 2:
        nchunks = 0

    def _passA_chunk(c):
        t0 = c * L
        em.wait_dma("G", "G")
        em.wait("G", "D")
        em.wait("G", "T")
        em.dma("G", xlc[:, :, 0:129],
               xcf.rearrange("(fc p) t -> p fc t", p=128)[:, :, t0:t0 + 129])
        em.wait_dma("D", "G")
        em.op("D", "tensor_mul", xsq[:, :, 0:129], xlc[:, :, 0:129],
              xlc[:, :, 0:129])
        em.barrier(("T", "D"))
        for fc in range(8):
            em.op("T", "matmul", PU0[:, 0:129], c_one, xlc[:, fc, 0:129],
                  pgkey=("PU0", 0))
        for fc in range(8):
            em.op("T", "matmul", PU0[:, 256:385], c_one, xsq[:, fc, 0:129],
                  pgkey=("PU0", 0))
        em.mm_close(("PU0", 0))
        em.wait("D", "T")
        em.op("D", "tensor_scalar_mul", sc_f1[:, 0:129], PU0[:, 0:129],
              1.0 / 1024.0)
        em.op("D", "tensor_scalar_mul", sc_f2[:, 0:129], PU0[:, 256:385],
              1.0 / 1024.0)
        em.wait("D", "D")
        em.op("D", "tensor_copy", sc_m[:, 0:129], sc_f1[:, 0:129])
        em.op("D", "tensor_mul", sc_f1[:, 0:129], sc_f1[:, 0:129],
              sc_f1[:, 0:129])
        em.wait("D", "D")
        em.op("D", "tensor_sub", sc_f2[:, 0:129], sc_f2[:, 0:129],
              sc_f1[:, 0:129])
        em.wait("A", "D")
        em.op("A", "activation", sc_f1[:, 0:129], sc_f2[:, 0:129], AF.Ln,
              bias=cb[:, 0:1], scale=1.0)
        em.wait("A", "A")
        em.op("A", "activation", sc_f2[:, 0:129], sc_f1[:, 0:129], AF.Exp,
              scale=-0.5)
        em.wait("D", "A")
        em.op("D", "tensor_copy", sc_r[:, 0:129], sc_f2[:, 0:129])
        em.wait("D", "D")
        em.op("D", "tensor_sub", xlc[:, :, 0:129], xlc[:, :, 0:129],
              sc_m[:, 0:129].unsqueeze(1)
              .broadcast_to([128, 8, 129]))
        em.wait("D", "D")
        em.op("D", "tensor_mul", xlc[:, :, 0:129], xlc[:, :, 0:129],
              sc_r[:, 0:129].unsqueeze(1)
              .broadcast_to([128, 8, 129]))
        em.wait("T", "D")
        if sub <= 1:
            em.barrier()
            return
        xl = xlc[:, :, 1:129]
        xls = xlc[:, :, 0:128]
        em.wait("D", "D")
        for fc in range(8):
            em.op("D", "tensor_sub", f_xx[:, fc, :], xls[:, fc, :],
                  xl[:, fc, :])
        em.wait("D", "D")
        for fc in range(8):
            em.op("D", "scalar_tensor_tensor", f_mix[:, fc, :], f_xx[:, fc, :],
                  v_fm[:, fc, VIDX["tmx"]:VIDX["tmx"] + 1], xl[:, fc, :],
                  AL.mult, AL.add)
        if sub == 10:
            em.barrier()
            return
        em.barrier(("T", "D"))
        for kc in range(8):
            em.op("T", "matmul", PU0[:, 0:128], w_maa1[:, kc, :],
                  f_mix[:, kc, :], pgkey=("PU0", 0))
        em.mm_close(("PU0", 0))
        em.wait("A", "T")
        em.op("A", "activation", f_lo, PU0[:, 0:128], AF.Tanh)
        em.wait("D", "A")
        em.op("D", "memset", f_loP, 0.0)
        em.wait("D", "D")
        for gg in range(4):
            em.op("D", "tensor_copy", f_loP[32 * gg:32 * gg + 32, gg, :],
                  f_lo[32 * gg:32 * gg + 32, :])
        em.barrier(("T", "A", "D"))
        if sub == 11:
            em.barrier()
            return

        for g in range(min(4, sub - 11 if 11 < sub < 16 else 4)):
            vi = VIDX["tmaa%d" % g]
            for m in range(8):
                em.op("T", "matmul", PG0[:, m, :],
                      w_maa2[:, 128 * m:128 * m + 128],
                      f_loP[:, g, :], pgkey=("PG0", m // 4))
            em.mm_close(("PG0", 0), ("PG0", 1))
            em.wait("D", "T")
            for fc in range(8):
                em.op("D", "tensor_mul", f_t1[:, fc, :], f_xx[:, fc, :],
                      PG0[:, fc, :])
            for fc in range(8):
                em.op("D", "scalar_tensor_tensor", f_t2[:, fc, :],
                      f_xx[:, fc, :], v_fm[:, fc, vi:vi + 1], xl[:, fc, :],
                      AL.mult, AL.add)
            em.wait("D", "D")
            for fc in range(8):
                em.op("D", "tensor_add", f_xm[:, fc, :], f_t1[:, fc, :],
                      f_t2[:, fc, :])
            em.barrier(("T", "D"))
            if sub == 16 and g == 1:
                em.barrier()
                return
            if g == 0:
                for m in range(8):
                    for kc in range(8):
                        em.op("T", "matmul", PG1[:, m, :],
                              w_bigA[:, kc, 128 * m:128 * m + 128],
                              f_xm[:, kc, :], pgkey=("PG1", m // 4))
                for kc in range(8):
                    em.op("T", "matmul", PU0[:, 0:128], w_g1[:, kc, :],
                          f_xm[:, kc, :], pgkey=("PU0", 0))
                em.mm_close(("PG1", 0), ("PG1", 1), ("PU0", 0))
                em.wait("D", "T")
                em.op("D", "tensor_copy", f_r, PG1)
                em.wait("A", "T")
                em.op("A", "activation", f_gh, PU0[:, 0:128], AF.Tanh)
                em.barrier(("T", "D", "A"))
                for half in range(2):
                    em.op("T", "matmul", PU1[:, 0:512], f_gh,
                          w_g2[:, 512 * half:512 * half + 512],
                          pgkey=("PU1", 0))
                    em.mm_close(("PU1", 0))
                    em.wait("D", "T")
                    em.op("D", "tensor_copy",
                          t_g[:, 512 * half:512 * half + 512], PU1[:, 0:512])
                    em.wait("T", "D")
            elif g == 1:
                for kc in range(8):
                    em.op("T", "matmul", PU0[0:64, 0:128], w_d1[:, kc, :],
                          f_xm[:, kc, :], pgkey=("PU0", 0))
                for kc in range(8):
                    em.op("T", "matmul", PU0[0:64, 128:256], w_s1[:, kc, 0:64],
                          f_xm[:, kc, :], pgkey=("PU0", 0))
                em.mm_close(("PU0", 0))
                em.wait("A", "T")
                em.op("A", "activation", f_dh, PU0[0:64, 0:128], AF.Tanh)
                em.wait("D", "T")
                em.op("D", "memset", f_shP, 0.0)
                em.wait("D", "D")
                em.op("D", "tensor_copy", f_shP[0:32, 0, :],
                      PU0[0:32, 128:256])
                em.op("D", "tensor_copy", f_shP[32:64, 1, :],
                      PU0[32:64, 128:256])
                em.barrier(("T", "D", "A"))
                for m in range(8):
                    em.op("T", "matmul", PG1[:, m, :],
                          w_d2[:, 128 * m:128 * m + 128], f_dh,
                          pgkey=("PG1", m // 4))
                em.mm_close(("PG1", 0), ("PG1", 1))
                for m in range(8):
                    em.op("T", "matmul", PG2[:, m, :],
                          w_s2[:, 128 * m:128 * m + 128], f_shP[:, 0, :],
                          pgkey=("PG2", m // 4))
                for m in range(8):
                    em.op("T", "matmul", PG0[:, m, :],
                          w_s2[:, 128 * m:128 * m + 128], f_shP[:, 1, :],
                          pgkey=("PG0", m // 4))
                em.mm_close_all()
                em.wait("D", "T")
                em.op("D", "tensor_tensor", f_t1, PG1,
                      v_fm[:, :, VIDX["td"]:VIDX["td"] + 1]
                      .broadcast_to([128, 8, L]), AL.add)
                em.op("D", "tensor_tensor", f_t2, PG2,
                      v_fm[:, :, VIDX["taaaaa"]:VIDX["taaaaa"] + 1]
                      .broadcast_to([128, 8, L]), AL.add)
                em.op("D", "tensor_tensor", f_t3, PG0,
                      v_fm[:, :, VIDX["tma"]:VIDX["tma"] + 1]
                      .broadcast_to([128, 8, L]), AL.add)
                em.wait("A", "D")
                em.op("A", "activation", f_sig, f_t1, AF.Sigmoid)
                em.op("A", "activation", f_a, f_t2, AF.Sigmoid)
                em.op("A", "activation", f_ap, f_t2, AF.Sigmoid, scale=-1.0)
                em.op("A", "activation", f_ma, f_t3, AF.Sigmoid)
                em.barrier(("T", "D", "A"))
            elif g == 2:
                for m in range(8):
                    for kc in range(8):
                        em.op("T", "matmul", PG1[:, m, :],
                              w_bigB[:, kc, 128 * m:128 * m + 128],
                              f_xm[:, kc, :], pgkey=("PG1", m // 4))
                em.mm_close(("PG1", 0), ("PG1", 1))
                for kc in range(8):
                    em.op("T", "matmul", PU0[0:64, 256:384],
                          w_s1[:, kc, 64:128], f_xm[:, kc, :],
                          pgkey=("PU0s",))
                em.mm_close(("PU0s",))
                em.wait("D", "T")
                em.op("D", "tensor_copy", f_k0, PG1)
                em.op("D", "tensor_copy", f_shB[0:32, :], PU0[0:32, 256:384])
                em.wait("A", "T")
                em.op("A", "activation", f_shB[32:64, :], PU0[32:64, 256:384],
                      AF.Tanh)
                em.wait_dma("G", "G")
                em.wait("G", "D")
                em.wait("G", "A")
                em.dma("G", f_shP[64:96, 2, :], f_shB[0:32, :])
                em.dma("G", f_shP[96:128, 3, :], f_shB[32:64, :])
                em.wait_dma("T", "G")
                em.barrier(("T", "D", "A"))
                for m in range(8):
                    em.op("T", "matmul", PG2[:, m, :],
                          w_s2[:, 128 * m:128 * m + 128], f_shP[:, 2, :],
                          pgkey=("PG2", m // 4))
                for m in range(8):
                    em.op("T", "matmul", PG0[:, m, :],
                          w_s2[:, 128 * m:128 * m + 128], f_shP[:, 3, :],
                          pgkey=("PG0", m // 4))
                em.mm_close_all()
                em.wait("D", "T")
                em.op("D", "tensor_tensor", f_t1, PG2,
                      v_fm[:, :, VIDX["tmk"]:VIDX["tmk"] + 1]
                      .broadcast_to([128, 8, L]), AL.add)
                em.op("D", "tensor_add", f_kk, f_k0, PG0)
                em.wait("A", "D")
                em.op("A", "activation", f_mk, f_t1, AF.Sigmoid)
                em.barrier(("T", "D", "A"))
            else:
                for half in range(2):
                    for kc in range(8):
                        em.op("T", "matmul", PU1[:, 0:512], f_xm[:, kc, :],
                              w_bigC[:, kc, 512 * half:512 * half + 512],
                              pgkey=("PU1", 0))
                    em.mm_close(("PU1", 0))
                    em.wait("D", "T")
                    em.op("D", "tensor_copy",
                          t_v[:, 512 * half:512 * half + 512], PU1[:, 0:512])
                    em.wait("T", "D")
                em.barrier(("T", "D"))

        if 11 < sub < 16:
            em.barrier()
            return
        if sub <= 2:
            em.barrier()
            return
        em.op("D", "tensor_mul", f_rkp, f_kk, f_kk)
        em.barrier(("T", "D"))
        for fc in range(8):
            em.op("T", "matmul", PU0[0:16, 0:128], w_hblk[:, fc, :],
                  f_rkp[:, fc, :], pgkey=("PU0", 0))
        em.mm_close(("PU0", 0))
        em.wait("A", "T")
        em.op("A", "activation", rn_h, PU0[0:16, 0:128], AF.Ln,
              bias=cb[0:16, 1:2], scale=1.0)
        em.wait("A", "A")
        em.op("A", "activation", rn_h, rn_h, AF.Exp, scale=-0.5)
        em.wait("D", "A")
        em.op("D", "tensor_copy", t_rk[:, 0:128], rn_h)
        em.barrier(("T", "D", "A"))
        for fc in range(8):
            em.op("T", "matmul", PG0[:, fc, :], c_hsel[:, fc, :],
                  t_rk[:, 0:128], pgkey=("PG0", fc // 4))
        em.mm_close(("PG0", 0), ("PG0", 1))
        em.wait("D", "T")
        em.op("D", "tensor_mul", f_kkn, f_kk, PG0)
        em.wait("D", "D")
        em.op("D", "tensor_mul", f_t1, f_ma, f_ap)
        em.wait("D", "D")
        em.op("D", "tensor_add", f_ap, f_a, f_t1)
        em.op("A", "activation", f_t2, f_sig, AF.Ln)
        em.wait("D", "A")
        em.op("D", "scalar_tensor_tensor", f_t2, f_t2, -0.5, f_mk,
              AL.add, AL.mult)
        em.wait("A", "D")
        em.op("A", "activation", f_t2, f_t2, AF.Exp)
        em.wait("D", "A")
        em.op("D", "tensor_mul", f_t1, f_k0, f_ap)
        em.wait("D", "D")
        em.op("D", "tensor_mul", f_k, f_t1, f_t2)
        em.wait("D", "D")
        em.op("D", "scalar_tensor_tensor", f_t2, f_kkn, -1.0, f_a,
              AL.mult, AL.mult)
        em.wait("D", "D")
        em.op("D", "tensor_mul", f_rkp, f_r, f_k)
        em.barrier(("T", "D"))
        for fc in range(8):
            em.op("T", "matmul", PU0[0:16, 128:256], w_fblk[:, fc, :],
                  f_rkp[:, fc, :], pgkey=("PU0", 0))
        em.mm_close(("PU0", 0))
        em.wait("D", "T")
        em.op("D", "tensor_copy", t_rk[:, 0:128], PU0[0:16, 128:256])
        em.wait_dma("A", "A")
        em.wait("A", "D")
        em.dma("A", t_rkT, t_rk[:, 0:128], transpose=True)
        em.wait_dma("D", "A")
        em.op("D", "tensor_tensor", t_b1.rearrange("p (h d) -> p h d", d=64),
              t_v.rearrange("p (h d) -> p h d", d=64),
              t_rkT.unsqueeze(2).broadcast_to([128, 16, 64]),
              AL.mult)
        em.wait("D", "D")
        em.op("D", "tensor_add", t_b1, t_b1, bc_lnb)
        em.wait("D", "D")
        em.op("D", "tensor_mul", t_p, t_b1, t_g)
        em.op("D", "tensor_mul", t_p2, t_g, bc_lnw)
        em.wait_dma("G", "G")
        em.wait("G", "D")
        em.dma("G", s_p2[c], t_p)
        em.dma("G", s_p1[c], t_p2)

        if sub <= 3:
            em.barrier()
            return
        if not do_scan:
            em.barrier()
            return

        for fc in range(8):
            em.op("D", "tensor_tensor_scan", g_buf[:, fc, 1:L + 1],
                  f_sig[:, fc, :], f_sig[:, fc, :], 0.0, AL.add, AL.bypass)
        em.wait("A", "D")
        em.op("A", "activation", eg, g_buf, AF.Exp, scale=-C0)
        em.op("A", "activation", ek, g_buf[:, :, 1:L + 1], AF.Exp,
              scale=C0)
        em.op("A", "activation", cl_all[:, :, c:c + 1],
              g_buf[:, :, L:L + 1], AF.Exp, scale=-C0)
        em.wait("D", "A")
        em.op("D", "tensor_mul", f_qd, f_r, eg[:, :, 1:L + 1])
        em.op("D", "tensor_mul", f_kh, f_kkn, eg[:, :, 0:L])
        em.op("D", "tensor_mul", f_bc, f_t2, ek)
        em.op("D", "tensor_mul", f_kc, f_k, ek)
        em.wait("D", "D")
        em.op("D", "memset", f_khO[0:64, :, :], 0.0)
        em.op("D", "memset", f_qdO[0:64, :, :], 0.0)
        em.wait("D", "D")
        em.op("D", "tensor_copy", f_khO[64:128, :, :], f_kh[64:128, :, :])
        em.op("D", "tensor_copy", f_qdO[64:128, :, :], f_qd[64:128, :, :])
        em.wait_dma("A", "A")
        em.wait("A", "D")
        for fc in range(8):
            em.dma("A", bck_t[:, 128 * fc:128 * fc + 128], f_bc[:, fc, :],
                   transpose=True)
            em.dma("A", kck_t[:, 128 * fc:128 * fc + 128], f_kc[:, fc, :],
                   transpose=True)
        em.barrier(("T", "D", "A"))

        if sub <= 4:
            em.barrier()
            return
        for wv in range(2):
            for pair in range(2):
                rf = f_kh if pair == 0 else f_qd
                rfO = f_khO if pair == 0 else f_qdO
                d0 = a_ss if pair == 0 else a_q
                d1 = a_k if pair == 0 else a_qk
                msk = c_ms if pair == 0 else c_mi
                for j in range(4):
                    pr = 4 * wv + j
                    em.op("T", "matmul", PG0[:, j, :],
                          f_bc[0:64, pr, :], rf[0:64, pr, :],
                          pgkey=("S3", 0))
                    em.op("T", "matmul", PG1[:, j, :],
                          f_kc[0:64, pr, :], rf[0:64, pr, :],
                          pgkey=("S3", 1))
                for j in range(4):
                    pr = 4 * wv + j
                    em.op("T", "matmul", PG0[:, 4 + j, :],
                          f_bc[:, pr, :], rfO[:, pr, :], pgkey=("S3", 2))
                    em.op("T", "matmul", PG1[:, 4 + j, :],
                          f_kc[:, pr, :], rfO[:, pr, :], pgkey=("S3", 3))
                em.mm_close_all()
                em.wait("D", "T")
                d0v = d0.rearrange("p (a two) b -> p a two b", two=2)
                d1v = d1.rearrange("p (a two) b -> p a two b", two=2)
                mb = msk.unsqueeze(1).broadcast_to([128, 4, 128])
                em.op("D", "tensor_tensor", d0v[:, 4 * wv:4 * wv + 4, 0, :],
                      PG0[:, 0:4, :], mb, AL.mult)
                em.op("D", "tensor_tensor", d0v[:, 4 * wv:4 * wv + 4, 1, :],
                      PG0[:, 4:8, :], mb, AL.mult)
                em.op("D", "tensor_tensor", d1v[:, 4 * wv:4 * wv + 4, 0, :],
                      PG1[:, 0:4, :], mb, AL.mult)
                em.op("D", "tensor_tensor", d1v[:, 4 * wv:4 * wv + 4, 1, :],
                      PG1[:, 4:8, :], mb, AL.mult)
                em.wait("T", "D")
        if sub <= 5:
            em.barrier()
            return
        for b in range(4):
            em.op("D", "tensor_copy", dg_nt[32 * b:32 * b + 32, :, :],
                  a_ss[32 * b:32 * b + 32, :, 32 * b:32 * b + 32])
        em.wait("D", "D")
        em.op("D", "transpose", dg_n.rearrange("p h t -> p (h t)"),
              dg_nt.rearrange("p h t -> p (h t)"))
        em.wait("D", "D")
        em.op("D", "tensor_tensor", dg_tt, dg_nt,
              c_idblk.unsqueeze(1).broadcast_to([128, 16, 32]),
              AL.add)
        em.op("D", "tensor_copy", dg_pt, dg_nt)
        em.op("D", "tensor_copy", dg_pn, dg_n)
        em.barrier(("T", "D"))
        em.wait("D", "D")
        em.op("D", "memset", bd_pn[:, :, 0:128], 0.0)
        em.wait("D", "D")
        for b in range(4):
            em.op("D", "tensor_copy",
                  bd_pn[32 * b:32 * b + 32, :, 32 * b:32 * b + 32],
                  dg_pn[32 * b:32 * b + 32, :, :])
        em.barrier(("T", "D"))
        PU0h = PU0.rearrange("p (h t) -> p h t", t=32)
        for rnd in range(4):
            for h in range(16):
                em.op("T", "matmul", PU0[:, 32 * h:32 * h + 32],
                      bd_pn[:, h, 0:128], dg_pt[:, h, :], pgkey=("PT",))
            em.mm_close(("PT",))
            em.wait("D", "T")
            em.op("D", "tensor_copy", dg_pt, PU0h)
            em.wait("D", "D")
            em.op("D", "transpose", dg_pn.rearrange("p h t -> p (h t)"),
                  dg_pt.rearrange("p h t -> p (h t)"))
            em.wait("D", "D")
            for b in range(4):
                em.op("D", "tensor_copy",
                      bd_pn[32 * b:32 * b + 32, :, 32 * b:32 * b + 32],
                      dg_pn[32 * b:32 * b + 32, :, :])
            em.wait("T", "D")
            for h in range(16):
                em.op("T", "matmul", PU0[:, 32 * h:32 * h + 32],
                      bd_pn[:, h, 0:128], dg_tt[:, h, :], pgkey=("PT",))
            em.mm_close(("PT",))
            em.wait("D", "T")
            em.op("D", "tensor_tensor", dg_tt, dg_tt, PU0h, AL.add)
            em.wait("T", "D")

        if sub <= 6:
            em.barrier()
            return
        em.wait_dma("T", "A")
        PU0v = PU0.rearrange("p (a b) -> p a b", b=128)
        PG1zv = PG1.rearrange("p a b -> p (a b)") \
                   .rearrange("p (a b) -> p a b", b=256)
        ywv = y_lc.rearrange("p (w hh d) -> p w hh d", w=2, d=64)
        uwv = u_tk.rearrange("p (w hh d) -> p w hh d", w=2, d=64)
        xs4 = x_sol.rearrange("p (a two) b -> p a two b", two=2)
        em.wait("D", "D")
        for b in range(4):
            em.op("D", "tensor_copy",
                  bd_pn[32 * b:32 * b + 32, :, 32 * b:32 * b + 32],
                  dg_tt[32 * b:32 * b + 32, :, :])
        em.barrier(("T", "D"))
        for wv in range(2):
            for j in range(4):
                pr = 4 * wv + j
                em.op("T", "matmul", PG2[:, 2 * j, 0:64],
                      f_kh[0:64, pr, :], s_sb[0:64, pr, :],
                      pgkey=("RS", wv, j // 2))
                em.op("T", "matmul", PG2[:, 2 * j, 64:128],
                      f_kh[0:64, pr, :], c_id[0:64, 0:64],
                      pgkey=("RS", wv, j // 2))
            for j in range(4):
                pr = 4 * wv + j
                em.op("T", "matmul", PG2[:, 2 * j + 1, 0:64],
                      f_khO[:, pr, :], s_sb[:, pr, :],
                      pgkey=("RS", wv, j // 2))
                em.op("T", "matmul", PG2[:, 2 * j + 1, 64:128],
                      f_khO[:, pr, :], c_id[:, 64:128],
                      pgkey=("RS", wv, j // 2))
            for hh in range(8):
                h = 8 * wv + hh
                em.op("T", "matmul", PG2[:, hh, 0:64],
                      a_k[:, h, :], t_v[:, 64 * h:64 * h + 64],
                      pgkey=("RS", wv, hh // 4))
            em.mm_close(("RS", wv, 0), ("RS", wv, 1))
            if sub == 71:
                em.barrier()
                return
            em.wait("D", "T")
            em.wait("D", "D")
            em.op("D", "memset", x_sol, 0.0)
            for I in range(4):
                if I > 0:
                    em.wait("T", "D")
                    for hh in range(8):
                        h = 8 * wv + hh
                        em.op("T", "matmul", PG1[:, hh, :],
                              a_ss[:, h, :], x_sol[:, hh, :],
                              pgkey=("CR", wv, I, hh // 4))
                    em.mm_close(("CR", wv, I, 0), ("CR", wv, I, 1))
                em.wait("D", "T")
                em.op("D", "tensor_copy", x_ri[32 * I:32 * I + 32, :, :],
                      PG2[32 * I:32 * I + 32, :, :])
                if I > 0:
                    em.wait("D", "D")
                    em.op("D", "tensor_tensor",
                          x_ri[32 * I:32 * I + 32, :, :],
                          x_ri[32 * I:32 * I + 32, :, :],
                          PG1[32 * I:32 * I + 32, :, :], AL.add)
                em.wait("T", "D")
                for hh in range(8):
                    h = 8 * wv + hh
                    em.op("T", "matmul", PG0[:, hh, :],
                          bd_pn[:, h, 0:128], x_ri[:, hh, :],
                          pgkey=("SV", wv, I, hh // 4))
                em.mm_close(("SV", wv, I, 0), ("SV", wv, I, 1))
                em.wait("D", "T")
                em.op("D", "tensor_copy", x_sol[32 * I:32 * I + 32, :, :],
                      PG0[32 * I:32 * I + 32, :, :])
            em.wait("T", "D")
            if sub == 72:
                em.barrier()
                return
            for j in range(4):
                pr = 4 * wv + j
                em.op("T", "matmul", PG0[:, 2 * j, 0:64],
                      f_qd[0:64, pr, :], s_sb[0:64, pr, :],
                      pgkey=("Y", wv, j // 2))
            for j in range(4):
                pr = 4 * wv + j
                em.op("T", "matmul", PG0[:, 2 * j + 1, 0:64],
                      f_qdO[:, pr, :], s_sb[:, pr, :],
                      pgkey=("Y", wv, j // 2))
            for hh in range(8):
                h = 8 * wv + hh
                em.op("T", "matmul", PG0[:, hh, 0:64],
                      a_q[:, h, :], x_sol[:, hh, 0:64],
                      pgkey=("Y", wv, hh // 4))
                em.op("T", "matmul", PG0[:, hh, 0:64],
                      a_qk[:, h, :], t_v[:, 64 * h:64 * h + 64],
                      pgkey=("Y", wv, hh // 4))
            em.mm_close(("Y", wv, 0), ("Y", wv, 1))
            em.wait("D", "T")
            em.op("D", "tensor_copy", ywv[:, wv], PG0[:, :, 0:64])
            em.op("D", "tensor_copy", uwv[:, wv], x_sol[:, :, 64:128])
            em.op("D", "tensor_copy", x_u[:, :, 0:64], xs4[:, :, 0, 64:128])
            em.op("D", "tensor_copy", x_u[:, :, 64:128], xs4[:, :, 1, 64:128])
            em.op("D", "tensor_copy", x_h[:, :, 0:64], xs4[:, :, 0, 0:64])
            em.op("D", "tensor_copy", x_h[:, :, 64:128], xs4[:, :, 1, 0:64])
            em.wait("T", "D")
            if sub == 73:
                em.barrier()
                return
            for j in range(4):
                pr = 4 * wv + j
                em.op("T", "matmul", PG1zv[:, j, 0:256],
                      x_u[:, j, :],
                      a_q[:, 8 * wv + 2 * j:8 * wv + 2 * j + 2, :]
                      .rearrange("p a b -> p (a b)"),
                      pgkey=("Z", wv, j // 2))
                em.op("T", "matmul", PG1zv[:, j, 0:128],
                      c_idlo, f_qd[:, pr, :], pgkey=("Z", wv, j // 2))
                em.op("T", "matmul", PG1zv[:, j, 128:256],
                      c_idhi, f_qd[:, pr, :], pgkey=("Z", wv, j // 2))
            em.mm_close(("Z", wv, 0), ("Z", wv, 1))
            em.wait("D", "T")
            em.op("D", "tensor_copy", z_f[0:64, 4 * wv:4 * wv + 4, :],
                  PG1zv[0:64, :, 0:128])
            em.op("D", "tensor_copy", z_f[64:128, 4 * wv:4 * wv + 4, :],
                  PG1zv[64:128, :, 128:256])
            em.wait("T", "D")
            if sub == 74:
                em.barrier()
                return
            for j in range(4):
                pr = 4 * wv + j
                em.op("T", "matmul", PU0v[:, j, :],
                      bck_t[:, 128 * pr:128 * pr + 128], x_h[:, j, :],
                      pgkey=("SU", wv))
                em.op("T", "matmul", PU0v[:, j, :],
                      kck_t[:, 128 * pr:128 * pr + 128],
                      t_v[:, 128 * pr:128 * pr + 128], pgkey=("SU", wv))
            em.mm_close(("SU", wv))
            em.wait("D", "T")
            em.op("D", "tensor_add", st_t1[0:64, 4 * wv:4 * wv + 4, :],
                  s_st[0:64, 4 * wv:4 * wv + 4, :], PU0v[0:64, :, 0:64])
            em.op("D", "tensor_add", st_t1[64:128, 4 * wv:4 * wv + 4, :],
                  s_st[64:128, 4 * wv:4 * wv + 4, :],
                  PU0v[64:128, :, 64:128])
            em.wait("T", "D")
        em.wait("D", "D")
        em.op("D", "tensor_tensor", s_st, st_t1,
              cl_all[:, :, c:c + 1].broadcast_to([128, 8, 64]), AL.mult)
        em.wait("D", "D")
        em.op("D", "tensor_copy", s_sb, s_st)
        em.wait("T", "D")
        em.wait_dma("A", "A")
        em.wait("A", "D")
        for fc in range(8):
            em.dma("A", u_f[:, fc, :], u_tk[:, 128 * fc:128 * fc + 128],
                   transpose=True)
        em.wait("G", "D")
        em.wait_dma("G", "G")
        em.dma("G", s_yloc[c], y_lc)
        em.dma("G", s_zf[c], z_f)
        em.dma("G", s_bck[c], bck_t)
        em.wait_dma("G", "A")
        em.dma("G", s_uf[c], u_f)
        if sub <= 7:
            em.barrier()
            return
        em.wait_dma("T", "A")
        PU1v = PU1.rearrange("p (a b) -> p a b", b=128)
        em.wait("D", "D")
        em.wait_dma("D", "A")
        em.op("D", "memset", u_fO[0:64, :, :], 0.0)
        em.wait("D", "D")
        em.op("D", "tensor_copy", u_fO[64:128, :, :], u_f[64:128, :, :])
        em.barrier(("T", "D"))
        for pr in range(8):
            em.op("T", "matmul", PG0[:, pr, 0:64],
                  u_f[0:64, pr, :], tr_sb[0:64, pr, :],
                  pgkey=("WT", pr // 4))
        for pr in range(8):
            em.op("T", "matmul", PG0[:, pr, 64:128],
                  u_fO[:, pr, :], tr_sb[:, pr, :], pgkey=("WT", pr // 4))
        em.mm_close(("WT", 0), ("WT", 1))
        em.wait("D", "T")
        em.op("D", "tensor_copy", w_sb, PG0.rearrange("p a b -> p (a b)"))
        em.wait("T", "D")
        for pr in range(4):
            em.op("T", "matmul", PU0v[:, pr, :],
                  bck_t[:, 128 * pr:128 * pr + 128],
                  w_sb[:, 128 * pr:128 * pr + 128], pgkey=("TU", 0))
        for pr in range(4, 8):
            em.op("T", "matmul", PU1v[:, pr - 4, :],
                  bck_t[:, 128 * pr:128 * pr + 128],
                  w_sb[:, 128 * pr:128 * pr + 128], pgkey=("TU", 1))
        em.mm_close(("TU", 0), ("TU", 1))
        em.wait("D", "T")
        em.op("D", "tensor_add", st_t1[0:64, 0:4, :], tr_st[0:64, 0:4, :],
              PU0v[0:64, :, 0:64])
        em.op("D", "tensor_add", st_t1[64:128, 0:4, :],
              tr_st[64:128, 0:4, :], PU0v[64:128, :, 64:128])
        em.op("D", "tensor_add", st_t1[0:64, 4:8, :], tr_st[0:64, 4:8, :],
              PU1v[0:64, :, 0:64])
        em.op("D", "tensor_add", st_t1[64:128, 4:8, :],
              tr_st[64:128, 4:8, :], PU1v[64:128, :, 64:128])
        em.wait("D", "D")
        em.op("D", "tensor_tensor", tr_st, st_t1,
              cl_all[:, :, c:c + 1].broadcast_to([128, 8, 64]), AL.mult)
        em.wait("D", "D")
        em.op("D", "tensor_copy", tr_sb, tr_st)
        em.barrier(("T", "D"))


    passA_chunk = _passA_chunk
    if stage < 2:
        nchunks = 0
    for c in range(nchunks):
        passA_chunk(c)
    em.barrier()

    if do_scan and do_passb and stage >= 3:
        bdS3 = alias("bdS3", "bd_pn", [128, 8, 128])
        PU0s8 = PU0.rearrange("p (a d) -> p a d", d=64)
        em.wait("D", "T")
        em.wait("D", "D")
        em.op("D", "memset", bdS3, 0.0)
        em.wait("D", "D")
        em.op("D", "tensor_copy", bdS3[0:64, :, 0:64], tr_sb[0:64, :, :])
        em.op("D", "tensor_copy", bdS3[64:128, :, 64:128],
              tr_sb[64:128, :, :])
        em.barrier(("T", "D"))
        for pr in range(8):
            em.op("T", "matmul", PG0[:, pr, 0:64],
                  bdS3[:, pr, :], c_dup, pgkey=("SM", pr // 4))
        em.mm_close_all()
        em.wait("D", "T")
        em.op("D", "tensor_copy", st_t1, PG0[:, :, 0:64])
        em.wait("G", "D")
        em.wait_dma("G", "G")
        em.dma("G", cc_in[0:1024].rearrange("(pr p) i -> p pr i", p=128), s_st)
        em.dma("G", cc_in[1024:2048].rearrange("(pr p) i -> p pr i", p=128),
               st_t1)
        em.wait_dma("G", "G")
        if no_cc:
            em.op("D", "memset", gat_s, 0.0)
            em.op("D", "memset", gat_t, 0.0)
            em.wait("D", "D")
        else:
            em.collective([cc_in.opt()], [cc_out.opt()],
                          [[0, 1, 2, 3], [4, 5, 6, 7]])
        em.dma("G", w_bigA, r128(woT))
        em.dma("G", w_bigB, r128(wqT))
        em.barrier()
        for d in range(3):
            if not no_cc:
                em.wait_dma("G", "G")
                em.wait("G", "D")
                em.dma("G", gat_s, cc_out[2048 * d:2048 * d + 1024]
                       .rearrange("(pr p) i -> p pr i", p=128))
                em.dma("G", gat_t, cc_out[2048 * d + 1024:2048 * d + 2048]
                       .rearrange("(pr p) i -> p pr i", p=128))
                em.wait_dma("D", "G")
            em.wait("D", "D")
            em.op("D", "tensor_scalar_mul", trp, gat_t, sel[:, d:d + 1])
            em.op("D", "tensor_scalar_mul", st_t1, c_trid, sel[:, 4 + d:5 + d])
            em.wait("D", "D")
            em.op("D", "tensor_add", trp, trp,
                  st_t1.rearrange("p a b -> p a b"))
            em.op("D", "tensor_scalar_mul", sp_f, gat_s, sel[:, d:d + 1])
            em.wait("D", "T")
            em.wait("D", "D")
            em.op("D", "tensor_copy", bdS3[0:64, :, 0:64], trp[0:64, :, :])
            em.op("D", "tensor_copy", bdS3[64:128, :, 64:128],
                  trp[64:128, :, :])
            em.barrier(("T", "D"))
            for pr in range(8):
                em.op("T", "matmul", PU0s8[:, pr, :],
                      bdS3[:, pr, :], sict_b[:, pr, :], pgkey=("CM", d))
            em.mm_close_all()
            em.wait("D", "T")
            em.op("D", "tensor_add", sict, PU0s8, sp_f)
            em.wait("D", "D")
            em.op("D", "tensor_copy", sict_b, sict)
            em.wait("T", "D")
        em.barrier()

    def passB_chunk(c):
        em.wait_dma("S", "S")
        em.wait_dma("S", "G")
        em.wait("S", "D")
        em.dma("S", t_ld1, s_yloc[c])
        em.dma("S", t_ldp1, s_p1[c])
        em.dma("S", t_ldp2, s_p2[c])
        em.dma("S", z_f, s_zf[c])
        em.dma("S", u_f, s_uf[c])
        em.dma("S", bck_t, s_bck[c])
        em.dma("S", xt_in.rearrange("p f -> p f"), xtok[c * L:(c + 1) * L, :])
        em.wait_dma("T", "S")
        em.wait_dma("D", "S")
        PG0f2 = PG0.rearrange("p a b -> p (a b)")
        PU0v = PU0.rearrange("p (a b) -> p a b", b=128)
        PU1v = PU1.rearrange("p (a b) -> p a b", b=128)
        em.wait("D", "D")
        em.op("D", "memset", z_fO[0:64, :, :], 0.0)
        em.op("D", "memset", u_fO[0:64, :, :], 0.0)
        em.wait("D", "D")
        em.op("D", "tensor_copy", z_fO[64:128, :, :], z_f[64:128, :, :])
        em.op("D", "tensor_copy", u_fO[64:128, :, :], u_f[64:128, :, :])
        em.barrier(("T", "D"))
        for half in range(2):
            em.op("T", "matmul", PG0f2[:, 512 * half:512 * half + 512],
                  c_id, t_ld1[:, 512 * half:512 * half + 512],
                  pgkey=("PG0", half))
        for h in range(0, 16, 2):
            em.op("T", "matmul", PG0f2[:, 64 * h:64 * h + 64],
                  z_f[0:64, h // 2, :], sict_b[0:64, h // 2, :],
                  pgkey=("PG0", h // 8))
        for h in range(1, 16, 2):
            em.op("T", "matmul", PG0f2[:, 64 * h:64 * h + 64],
                  z_fO[:, h // 2, :], sict_b[:, h // 2, :],
                  pgkey=("PG0", h // 8))
        em.mm_close(("PG0", 0), ("PG0", 1))
        em.wait("D", "T")
        em.op("D", "tensor_copy", y_fin, PG0f2)
        em.wait("T", "D")
        for pr in range(8):
            em.op("T", "matmul", PG1[:, pr, 0:64],
                  u_f[0:64, pr, :], sict_b[0:64, pr, :],
                  pgkey=("WB", pr // 4))
        for pr in range(8):
            em.op("T", "matmul", PG1[:, pr, 64:128],
                  u_fO[:, pr, :], sict_b[:, pr, :], pgkey=("WB", pr // 4))
        em.mm_close(("WB", 0), ("WB", 1))
        em.wait("D", "T")
        em.op("D", "tensor_copy", w_sb, PG1.rearrange("p a b -> p (a b)"))
        em.wait("T", "D")
        for pr in range(4):
            em.op("T", "matmul", PU0v[:, pr, :],
                  bck_t[:, 128 * pr:128 * pr + 128],
                  w_sb[:, 128 * pr:128 * pr + 128], pgkey=("TU", 0))
        for pr in range(4, 8):
            em.op("T", "matmul", PU1v[:, pr - 4, :],
                  bck_t[:, 128 * pr:128 * pr + 128],
                  w_sb[:, 128 * pr:128 * pr + 128], pgkey=("TU", 1))
        em.mm_close(("TU", 0), ("TU", 1))
        em.wait("D", "T")
        em.op("D", "tensor_add", st_t1[0:64, 0:4, :], sict[0:64, 0:4, :],
              PU0v[0:64, :, 0:64])
        em.op("D", "tensor_add", st_t1[64:128, 0:4, :],
              sict[64:128, 0:4, :], PU0v[64:128, :, 64:128])
        em.op("D", "tensor_add", st_t1[0:64, 4:8, :], sict[0:64, 4:8, :],
              PU1v[0:64, :, 0:64])
        em.op("D", "tensor_add", st_t1[64:128, 4:8, :],
              sict[64:128, 4:8, :], PU1v[64:128, :, 64:128])
        em.wait("D", "D")
        em.op("D", "tensor_tensor", sict, st_t1,
              cl_all[:, :, c:c + 1].broadcast_to([128, 8, 64]), AL.mult)
        em.wait("D", "D")
        em.op("D", "tensor_copy", sict_b, sict)
        em.wait("T", "D")
        em.op("D", "tensor_mul", lad_m, y_fin, y_fin)
        em.op("D", "tensor_reduce", gn_st[:, 0:16],
              y_fin.rearrange("p (h d) -> p h d", d=64), AX.X, AL.add)
        em.wait("D", "D")
        em.op("D", "tensor_reduce", gn_st[:, 16:32],
              lad_m.rearrange("p (h d) -> p h d", d=64), AX.X, AL.add)
        em.wait("D", "D")
        em.op("D", "tensor_scalar_mul", gn_st[:, 0:16], gn_st[:, 0:16],
              1.0 / 64.0)
        em.op("D", "tensor_scalar_mul", gn_st[:, 16:32], gn_st[:, 16:32],
              1.0 / 64.0)
        em.wait("D", "D")
        em.op("D", "tensor_mul", gn_st[:, 32:48], gn_st[:, 0:16],
              gn_st[:, 0:16])
        em.wait("D", "D")
        em.op("D", "tensor_sub", gn_st[:, 16:32], gn_st[:, 16:32],
              gn_st[:, 32:48])
        em.wait("A", "D")
        em.op("A", "activation", gn_st[:, 32:48], gn_st[:, 16:32], AF.Ln,
              bias=cb[:, 2:3], scale=1.0)
        em.wait("A", "A")
        em.op("A", "activation", gn_st[:, 48:64], gn_st[:, 32:48], AF.Exp,
              scale=-0.5)
        em.wait("D", "A")
        em.op("D", "tensor_sub", y_fin.rearrange("p (h d) -> p h d", d=64),
              y_fin.rearrange("p (h d) -> p h d", d=64),
              gn_st[:, 0:16].unsqueeze(2)
              .broadcast_to([128, 16, 64]))
        em.wait("D", "D")
        em.op("D", "tensor_mul", y_fin.rearrange("p (h d) -> p h d", d=64),
              y_fin.rearrange("p (h d) -> p h d", d=64),
              gn_st[:, 48:64].unsqueeze(2)
              .broadcast_to([128, 16, 64]))
        em.wait("D", "D")
        em.op("D", "tensor_mul", lad_m, y_fin, t_ldp1)
        em.wait("D", "D")
        em.op("D", "tensor_add", y2_t, lad_m, t_ldp2)
        em.wait_dma("A", "A")
        em.wait("A", "D")
        for fc in range(8):
            em.dma("A", y2_f[:, fc, :], y2_t[:, 128 * fc:128 * fc + 128],
                   transpose=True)
        em.wait_dma("T", "A")
        for half in range(2):
            for kc in range(8):
                em.op("T", "matmul", PU1[:, 0:512], y2_f[:, kc, :],
                      w_bigA[:, kc, 512 * half:512 * half + 512],
                      pgkey=("PU1", 0))
            em.mm_close(("PU1", 0))
            em.wait("D", "T")
            em.op("D", "tensor_tensor", x2_t[:, 512 * half:512 * half + 512],
                  PU1[:, 0:512], xt_in[:, 512 * half:512 * half + 512], AL.add)
            em.wait("T", "D")
        em.wait("D", "D")
        em.op("D", "tensor_reduce", bn2[:, 0:1], x2_t, AX.X, AL.add)
        em.wait("D", "D")
        em.op("D", "tensor_scalar_mul", bn2[:, 0:1], bn2[:, 0:1], 1.0 / 1024.0)
        em.wait("D", "D")
        em.op("D", "tensor_scalar", lad_u, x2_t, bn2[:, 0:1], None,
              AL.subtract)
        em.wait("D", "D")
        em.op("D", "tensor_mul", lad_m, lad_u, lad_u)
        em.wait("D", "D")
        em.op("D", "tensor_reduce", bn2[:, 1:2], lad_m, AX.X, AL.add)
        em.wait("D", "D")
        em.op("D", "tensor_scalar_mul", bn2[:, 1:2], bn2[:, 1:2], 1.0 / 1024.0)
        em.wait("A", "D")
        em.op("A", "activation", q_c1[:, 0:1], bn2[:, 1:2], AF.Ln,
              bias=cb[:, 0:1], scale=1.0)
        em.wait("A", "A")
        em.op("A", "activation", q_c1[:, 0:1], q_c1[:, 0:1], AF.Exp,
              scale=-0.5)
        em.wait("D", "D")
        em.op("D", "tensor_reduce", q_c1[:, 1:2], lad_u, AX.X, AL.max,
              apply_absolute_value=True)
        em.wait("D", "D")
        em.wait("D", "A")
        em.op("D", "tensor_mul", q_c1[:, 2:3], q_c1[:, 1:2], q_c1[:, 0:1])
        em.wait("D", "D")
        em.op("D", "tensor_scalar_max", q_c1[:, 2:3], q_c1[:, 2:3], 1e-6)
        em.wait("D", "D")
        em.op("D", "reciprocal", q_c1[:, 3:4], q_c1[:, 2:3])
        em.wait("D", "D")
        em.op("D", "tensor_mul", q_c1[:, 4:5], q_c1[:, 3:4], q_c1[:, 0:1])
        em.op("D", "tensor_scalar_mul", q_c1[:, 5:6], q_c1[:, 2:3], 1.0 / 127.0)
        em.wait("D", "D")
        em.op("D", "tensor_scalar_mul", q_c1[:, 4:5], q_c1[:, 4:5], 127.0)
        em.wait("D", "D")
        em.op("D", "tensor_scalar", lad_u, lad_u, q_c1[:, 4:5], 128.5,
              AL.mult, AL.add)
        em.wait("D", "D")
        em.op("D", "tensor_copy", lad_t, lad_u)
        for k in (7, 6, 5, 4, 3, 2, 1, 0):
            em.wait("D", "D")
            em.op("D", "tensor_scalar", lad_m, lad_t, float(2 ** k),
                  float(-(2 ** k)), AL.is_ge, AL.mult)
            em.wait("D", "D")
            em.op("D", "tensor_add", lad_t, lad_t, lad_m)
        em.wait("D", "D")
        em.op("D", "tensor_sub", lad_u, lad_u, lad_t)
        em.wait("D", "D")
        em.op("D", "tensor_scalar_add", xq_t, lad_u, -128.0)
        em.wait_dma("A", "A")
        em.wait("A", "D")
        for fc in range(8):
            em.dma("A", xq_f[:, fc, :], xq_t[:, 128 * fc:128 * fc + 128],
                   transpose=True)
        em.wait_dma("T", "A")
        for half in range(2):
            for kc in range(8):
                em.op("T", "matmul", PU1[:, 0:512], xq_f[:, kc, :],
                      w_bigB[:, kc, 512 * half:512 * half + 512],
                      pgkey=("PU1", 0))
            em.mm_close(("PU1", 0))
            em.wait("D", "T")
            sl = slice(512 * half, 512 * half + 512)
            em.op("D", "tensor_scalar", lad_m[:, sl], PU1[:, 0:512],
                  q_c1[:, 5:6], None, AL.mult)
            em.wait("T", "D")
        em.wait("D", "D")
        em.op("D", "tensor_mul", lad_m, lad_m, bc_ws)
        em.op("D", "tensor_add", lad_t, x2_t, bc_fb)
        em.wait("D", "D")
        em.wait_dma("D", "G")
        em.op("D", "tensor_add", o_fin, lad_m, lad_t)
        em.wait("G", "D")
        em.wait_dma("G", "G")
        em.dma("G", out[c * L:(c + 1) * L, :], o_fin)

    if do_passb and do_scan and stage >= 4:
        for c in range(nchunks):
            passB_chunk(c)
        em.barrier()
    else:
        em.op("D", "memset", o_fin, 0.0)
        em.wait("G", "D")
        for i in range(16):
            em.dma("G", out[128 * i:128 * i + 128, :], o_fin)
    em.wait_dma("G", "G")

    for name in debug_dump:
        ap_ = SB[name]
        o = nc.dram_tensor("dbg_" + name, list(ap_.shape), ap_.dtype,
                           kind="ExternalOutput").ap()
        em.dma("G", o, ap_)
    em.wait_dma("G", "G")
    em.play()
    return nc


def _host_prep(inputs):
    x = np.asarray(inputs["L0_attn_x"])
    g = {}
    g["wrT"] = np.ascontiguousarray(np.asarray(inputs["L0_attn_Wr"]).T)
    g["wkT"] = np.ascontiguousarray(np.asarray(inputs["L0_attn_Wk"]).T)
    g["wvT"] = np.ascontiguousarray(np.asarray(inputs["L0_attn_Wv"]).T)
    g["woT"] = np.ascontiguousarray(np.asarray(inputs["L0_attn_Wo"]).T)
    g["wqT"] = np.ascontiguousarray(np.asarray(inputs["L0_ffn_weight_q"]).T)
    g["maa_w1T"] = np.ascontiguousarray(np.asarray(inputs["L0_attn_maa_w1"]).T)
    g["maa_w2"] = np.ascontiguousarray(
        np.asarray(inputs["L0_attn_maa_w2"]).reshape(128, 1024))
    g["gate_w1T"] = np.ascontiguousarray(np.asarray(inputs["L0_attn_gate_w1"]).T)
    g["gate_w2"] = np.ascontiguousarray(np.asarray(inputs["L0_attn_gate_w2"]))
    g["decay_w1T"] = np.ascontiguousarray(np.asarray(inputs["L0_attn_decay_w1"]).T)
    g["decay_w2"] = np.ascontiguousarray(np.asarray(inputs["L0_attn_decay_w2"]))
    s1 = np.zeros((1024, 128), np.float32)
    s2 = np.zeros((128, 1024), np.float32)
    for i, n in enumerate(("aaa", "ma", "mk", "kkk")):
        s1[:, 32 * i:32 * i + 16] = np.asarray(inputs["L0_attn_%s_w1" % n]).T
        s2[32 * i:32 * i + 16, :] = np.asarray(inputs["L0_attn_%s_w2" % n])
    g["s_w1p"], g["s_w2p"] = s1, s2
    vr = np.zeros((16, 1024), np.float32)
    vr[0] = np.asarray(inputs["L0_attn_time_maa_x"])
    tm = np.asarray(inputs["L0_attn_time_maa"]).reshape(4, 1024)
    for gg in range(4):
        vr[1 + gg] = tm[gg]
    vr[5] = np.asarray(inputs["L0_attn_time_decay"])
    vr[6] = np.asarray(inputs["L0_attn_time_aaaaa"])
    vr[7] = np.asarray(inputs["L0_attn_time_misc_a"])
    vr[8] = np.asarray(inputs["L0_attn_time_misc_k"])
    vr[9] = np.asarray(inputs["L0_attn_ln_w"])
    vr[10] = np.asarray(inputs["L0_attn_ln_b"])
    vr[11] = np.asarray(inputs["L0_ffn_weight_scale"])
    vr[12] = np.asarray(inputs["L0_ffn_bias"])
    g["vrows"] = vr
    vfm = np.zeros((128, 128), np.float32)
    for i in range(16):
        for fc in range(8):
            vfm[:, fc * 16 + i] = vr[i, fc * 128:fc * 128 + 128]
    g["vfmi"] = vfm
    fa = np.asarray(inputs["L0_attn_time_faaaa"])
    fb = np.zeros((1024, 16), np.float32)
    hb = np.zeros((1024, 16), np.float32)
    idx = np.arange(1024)
    fb[idx, idx // 64] = fa
    hb[idx, idx // 64] = 1.0
    g["fblk"], g["hblk"] = fb, hb
    hs = np.zeros((16, 1024), np.float32)
    hs[idx // 64, idx] = 1.0
    g["hsel"] = hs
    g["ident"] = np.eye(128, dtype=np.float32)
    g["idlo"] = np.diag(np.concatenate([np.ones(64), np.zeros(64)])
                        ).astype(np.float32)
    g["idhi"] = np.diag(np.concatenate([np.zeros(64), np.ones(64)])
                        ).astype(np.float32)
    g["iddup"] = np.concatenate([np.eye(64), np.eye(64)]).astype(np.float32)
    idb = np.zeros((128, 32), np.float32)
    for b in range(4):
        idb[32 * b:32 * b + 32] = np.eye(32)
    g["idblk"] = idb
    sg, tg = np.meshgrid(np.arange(128), np.arange(128), indexing="ij")
    g["mask_s"] = (sg < tg).astype(np.float32)
    g["mask_i"] = (sg <= tg).astype(np.float32)
    g["onesc"] = np.ones((128, 128), np.float32)

    in_maps = []
    for c in range(8):
        b, q = c // 4, c % 4
        t0 = q * NTOK
        m = dict(g)
        xt = np.ascontiguousarray(x[b].T)
        xc = np.zeros((1024, NTOK + 1), np.float32)
        xc[:, 1:] = xt[:, t0:t0 + NTOK]
        if t0 > 0:
            xc[:, 0] = xt[:, t0 - 1]
        m["xcf"] = xc
        m["xtok"] = np.ascontiguousarray(x[b, t0:t0 + NTOK])
        s = np.zeros((128, 8), np.float32)
        s[:, :q] = 1.0
        s[:, 4:8] = 1.0
        s[:, 4:4 + q] = 0.0
        m["selq"] = s
        in_maps.append(m)
    return in_maps


def kernel(**inputs):
    in_maps = _host_prep(inputs)
    nc = build_nc()
    res = run_bass_kernel_spmd(nc, in_maps, core_ids=list(range(8)))
    x = np.asarray(inputs["L0_attn_x"])
    out = np.zeros_like(x, dtype=np.float32)
    for c in range(8):
        b, q = c // 4, c % 4
        out[b, q * NTOK:(q + 1) * NTOK] = res.results[c]["out"]
    return out


# version 03
# speedup vs baseline: 1.0495x; 1.0495x over previous
import numpy as np
import concourse.bass as bass
import concourse.mybir as mybir
from concourse.bass_utils import run_bass_kernel_spmd

F32 = mybir.dt.float32
BF16 = mybir.dt.bfloat16
I8 = mybir.dt.int8
AL = mybir.AluOpType
AF = mybir.ActivationFunctionType
AX = mybir.AxisListType

C0 = float(np.exp(-0.5))
H, Dh, L, NCH, NTOK, NFC = 16, 64, 128, 16, 2048, 8


class Em:
    ENGS = ("T", "D", "A", "G", "S")

    def __init__(self, nc):
        self.nc = nc
        self.ops = {e: [] for e in self.ENGS}
        self.cnt = {e: 0 for e in self.ENGS}
        self.dcnt = {q: 0 for q in ("S", "G", "A")}
        self.lastw = {e: {} for e in self.ENGS}
        self.pg = {}
        self.sems = {}
        self._ctx = []
        self.drn = 0
        self.mm_sig = None
        self.mm_idx = -1
        self.t_ok = True
        self.cover = {}

    def make_sems(self):
        for name in ("oT", "oD", "oA", "oG", "oS", "qS", "qG", "qA", "cc",
                     "dr"):
            c = self.nc.semaphore(name)
            self.sems[name] = c.__enter__()
            self._ctx.append(c)

    def drain(self, via="S"):
        self.wait(via, "T")
        self.drn += 1
        sem = self.sems["dr"]

        def fn(e, s=sem):
            e.sem_inc(s, 1)
        self.ops[via].append(fn)
        self._wait("T", "dr", sem, self.drn)
        self.t_ok = True

    def op(self, eng, method, *args, **kwargs):
        if "pgkey" in kwargs:
            key = kwargs.pop("pgkey")
            kwargs["start"] = key not in self.pg
            kwargs["stop"] = False
            kwargs["skip_group_check"] = True
            self.pg[key] = kwargs
        if eng == "T" and method == "matmul":
            K = args[1].shape[0]
            sig = 32 if K <= 32 else 64 if K <= 64 else 128
            if self.mm_sig is not None and sig != self.mm_sig \
                    and not self.t_ok:
                self.drain()
            self.mm_sig = sig
        self.cnt[eng] += 1
        if eng == "T" and method == "matmul":
            self.mm_idx = self.cnt["T"]
            self.t_ok = False
        sem = self.sems["o" + eng]

        def fn(e, m=method, a=args, k=kwargs, s=sem):
            getattr(e, m)(*a, **k).then_inc(s, 1)
        self.ops[eng].append(fn)

    def mm_close(self, *keys):
        for k in keys:
            if k in self.pg:
                self.pg.pop(k)["stop"] = True

    def mm_close_all(self):
        self.mm_close(*list(self.pg.keys()))

    def dma(self, q, out, in_, transpose=False, slow=False):
        self.dcnt[q] += 16
        sem = self.sems["q" + q]

        def fn(e, o=out, i=in_, t=transpose, s=sem, sl=slow):
            if t:
                e.dma_start_transpose(o, i).then_inc(s, 16)
            elif sl:
                e.dma_start(out=o, in_=i,
                            allow_slow_non_contiguous=True).then_inc(s, 16)
            else:
                e.dma_start(out=o, in_=i).then_inc(s, 16)
        self.ops[q].append(fn)

    def collective(self, ins, outs, groups):
        self.nc.has_collectives = True
        sem = self.sems["cc"]

        def fn(e, i=ins, o=outs, g=groups, s=sem):
            e.collective_compute("AllGather", AL.bypass, replica_groups=g,
                                 ins=i, outs=o).then_inc(s, 1)
        self.ops["G"].append(fn)

        def fn2(e, s=sem):
            e.wait_ge(s, 1)
        self.ops["G"].append(fn2)

    def _wait(self, eng, key, sem, v):
        if v <= 0 or self.lastw[eng].get(key, 0) >= v:
            return
        self.lastw[eng][key] = v
        if eng != "T" and key == "oT":
            self.cover[eng] = (self.cnt[eng], v)
        elif eng == "T" and key != "oT" and key.startswith("o"):
            c = self.cover.get(key[1:])
            if c and v > c[0] and c[1] >= self.mm_idx:
                self.t_ok = True

        def fn(e, s=sem, vv=v):
            e.wait_ge(s, vv)
        self.ops[eng].append(fn)

    def wait(self, eng, on, val=None):
        v = self.cnt[on] if val is None else val
        self._wait(eng, "o" + on, self.sems["o" + on], v)

    def wait_dma(self, eng, q, val=None):
        v = self.dcnt[q] if val is None else val
        self._wait(eng, "q" + q, self.sems["q" + q], v)

    def barrier(self, engs=None, dmas=None):
        engs = engs or self.ENGS
        dmas = list(self.dcnt.keys()) if dmas is None else dmas
        snap = dict(self.cnt)
        dsnap = dict(self.dcnt)
        for e in engs:
            for o in engs:
                if o != e:
                    self.wait(e, o, snap[o])
            for q in dmas:
                self.wait_dma(e, q, dsnap[q])

    def play(self):
        with self.nc.Block() as block:
            for eng, attr in (("T", "tensor"), ("D", "vector"), ("A", "scalar"),
                              ("G", "gpsimd"), ("S", "sync")):
                ops = self.ops[eng]

                def mk(lst):
                    def body(e):
                        for f in lst:
                            f(e)
                    return body
                getattr(block, attr)(mk(ops))
        for c in reversed(self._ctx):
            c.__exit__(None, None, None)


def build_nc(nchunks=NCH, do_scan=True, do_passb=True, no_cc=False, stage=99, sub=99, debug_dump=()):
    nc = bass.Bass("TRN2", target_bir_lowering=False, debug=False)
    em = Em(nc)

    def din(name, shape, dt=F32):
        return nc.dram_tensor(name, shape, dt, kind="ExternalInput").ap()

    xcf = din("xcf", [1024, NTOK + 1])
    xtok = din("xtok", [NTOK, 1024])
    selq = din("selq", [128, 8])
    wrT = din("wrT", [1024, 1024]); wkT = din("wkT", [1024, 1024])
    wvT = din("wvT", [1024, 1024]); woT = din("woT", [1024, 1024])
    wqT = din("wqT", [1024, 1024], I8)
    maa_w1T = din("maa_w1T", [1024, 128])
    maa_w2 = din("maa_w2", [128, 1024])
    gate_w1T = din("gate_w1T", [1024, 128])
    gate_w2 = din("gate_w2", [128, 1024])
    decay_w1T = din("decay_w1T", [1024, 64])
    decay_w2 = din("decay_w2", [64, 1024])
    s_w1p = din("s_w1p", [1024, 128])
    s_w2p = din("s_w2p", [128, 1024])
    vrows = din("vrows", [16, 1024])
    vfmi = din("vfmi", [128, 128])
    fblk = din("fblk", [1024, 16])
    hblk = din("hblk", [1024, 16])
    hsel = din("hsel", [16, 1024])
    ident = din("ident", [128, 128])
    idlo = din("idlo", [128, 128])
    idhi = din("idhi", [128, 128])
    iddup = din("iddup", [128, 64])
    idblk = din("idblk", [128, 32])
    mask_s = din("mask_s", [128, 128])
    mask_i = din("mask_i", [128, 128])
    onesc = din("onesc", [128, 128])

    out = nc.dram_tensor("out", [NTOK, 1024], F32, kind="ExternalOutput").ap()

    def scr(name, shape, dt=BF16):
        return nc.dram_tensor(name, shape, dt, kind="Internal").ap()
    s_yloc = scr("s_yloc", [NCH, L, 1024])
    s_zf = scr("s_zf", [NCH, 128, 8, 128])
    s_uf = scr("s_uf", [NCH, 128, 8, 128])
    s_bck = scr("s_bck", [NCH, L, 1024])
    s_p1 = scr("s_p1", [NCH, L, 1024])
    s_p2 = scr("s_p2", [NCH, L, 1024])
    cc_in = nc.dram_tensor("cc_in", [2048, 64], F32, kind="Internal").ap()
    cc_out = nc.dram_tensor("cc_out", [8192, 64], F32, kind="Internal").ap()

    SB = {}

    def sb(name, shape, dt):
        SB[name] = nc.alloc_sbuf_tensor(name, shape, dt).ap()
        return SB[name]

    def alias(name, base, shape, off=0):
        ap_ = SB[base]
        if ap_.ndim == 3:
            flat = ap_.rearrange("p a b -> p (a b)")
        else:
            flat = ap_
        n = 1
        for s in shape[1:]:
            n *= s
        v = flat[:shape[0], off:off + n]
        if len(shape) == 3:
            v = v.rearrange("p (a b) -> p a b", b=shape[2])
        SB[name] = v
        return v

    w_bigA = sb("w_bigA", [128, 8, 1024], BF16)
    w_bigB = sb("w_bigB", [128, 8, 1024], BF16)
    w_bigC = sb("w_bigC", [128, 8, 1024], BF16)
    w_maa1 = sb("w_maa1", [128, 8, 128], BF16)
    w_maa2 = sb("w_maa2", [128, 1024], BF16)
    w_g1 = sb("w_g1", [128, 8, 128], BF16)
    w_g2 = sb("w_g2", [128, 1024], BF16)
    w_d1 = sb("w_d1", [128, 8, 64], BF16)
    w_d2 = sb("w_d2", [64, 1024], BF16)
    w_s1 = sb("w_s1", [128, 8, 128], BF16)
    w_s2 = sb("w_s2", [128, 1024], BF16)
    w_fblk = sb("w_fblk", [128, 8, 16], BF16)
    w_hblk = sb("w_hblk", [128, 8, 16], BF16)
    c_hsel = sb("c_hsel", [16, 8, 128], BF16)
    c_id = sb("c_id", [128, 128], BF16)
    c_id32 = sb("c_id32", [128, 128], F32)
    c_idblk = sb("c_idblk", [128, 32], BF16)
    c_ms = sb("c_ms", [128, 128], BF16)
    c_mi = sb("c_mi", [128, 128], BF16)
    c_one = sb("c_one", [128, 128], BF16)
    c_idlo = sb("c_idlo", [128, 128], BF16)
    c_idhi = sb("c_idhi", [128, 128], BF16)
    c_dup = sb("c_dup", [128, 64], BF16)
    c_trid = sb("c_trid", [128, 8, 64], BF16)
    v_fm = sb("v_fm", [128, 8, 16], F32)
    v_neg = sb("v_neg", [128, 8, 4], F32)
    bc_lnw = sb("bc_lnw", [128, 1024], BF16)
    bc_lnb = sb("bc_lnb", [128, 1024], BF16)
    bc_ws = sb("bc_ws", [128, 1024], BF16)
    bc_fb = sb("bc_fb", [128, 1024], BF16)
    sel = sb("sel", [128, 8], F32)
    cb = sb("cb", [128, 4], F32)

    VIDX = {n: i for i, n in enumerate(
        ("tmx", "tmaa0", "tmaa1", "tmaa2", "tmaa3", "td", "taaaaa", "tma",
         "tmk", "lnw", "lnb", "ws", "fbias"))}
    NEGI = {"td": 0, "taaaaa": 1, "tma": 2, "tmk": 3}

    bd_pn = sb("bd_pn", [128, 16, 132], BF16)
    xlc = alias("xlc", "bd_pn", [128, 8, 132])
    xsq = alias("xsq", "bd_pn", [128, 8, 132], off=1056)
    sc_m = sb("sc_m", [128, 132], BF16)
    sc_r = sb("sc_r", [128, 132], BF16)
    sc_f1 = sb("sc_f1", [128, 132], F32)
    sc_f2 = sb("sc_f2", [128, 132], F32)
    u_tk = sb("u_tk", [128, 1024], BF16)
    f_xx = sb("f_xx", [128, 8, L], BF16)
    f_mix = sb("f_mix", [128, 8, L], BF16)
    f_xm = alias("f_xm", "f_mix", [128, 8, L])
    f_r = sb("f_r", [128, 8, L], BF16)
    f_k0 = sb("f_k0", [128, 8, L], BF16)
    f_kk = sb("f_kk", [128, 8, L], BF16)
    f_kkn = sb("f_kkn", [128, 8, L], BF16)
    f_k = sb("f_k", [128, 8, L], BF16)
    f_a = sb("f_a", [128, 8, L], BF16)
    f_ap = sb("f_ap", [128, 8, L], BF16)
    f_ma = sb("f_ma", [128, 8, L], BF16)
    f_mk = sb("f_mk", [128, 8, L], BF16)
    f_sig = sb("f_sig", [128, 8, L], F32)
    f_t1 = sb("f_t1", [128, 8, L], F32)
    f_t2 = sb("f_t2", [128, 8, L], F32)
    f_t3 = sb("f_t3", [128, 8, L], F32)
    f_rkp = sb("f_rkp", [128, 8, L], BF16)
    f_qd = alias("f_qd", "f_rkp", [128, 8, L])
    f_lo = sb("f_lo", [128, L], BF16)
    f_loP = sb("f_loP", [128, 4, L], BF16)
    f_shP = sb("f_shP", [128, 4, L], BF16)
    f_shB = sb("f_shB", [64, L], BF16)
    f_dh = sb("f_dh", [64, L], BF16)
    f_sh = sb("f_sh", [128, L], BF16)
    f_gh = sb("f_gh", [128, L], BF16)
    rn_h = sb("rn_h", [16, L], F32)
    g_buf = sb("g_buf", [128, 8, L + 1], F32)
    eg = sb("eg", [128, 8, L + 1], BF16)
    ek = sb("ek", [128, 8, L], BF16)
    f_kh = sb("f_kh", [128, 8, L], BF16)
    f_bc = sb("f_bc", [128, 8, L], BF16)
    f_kc = sb("f_kc", [128, 8, L], BF16)
    cl_all = sb("cl_all", [128, 8, NCH], F32)
    t_v = sb("t_v", [128, 1024], BF16)
    t_g = sb("t_g", [128, 1024], F32)
    t_b1 = sb("t_b1", [128, 1024], F32)
    t_p = sb("t_p", [128, 1024], BF16)
    t_p2 = sb("t_p2", [128, 1024], BF16)
    t_rk = sb("t_rk", [16, 128], BF16)
    t_rkT = sb("t_rkT", [128, 16], BF16)
    a_ss = sb("a_ss", [128, 16, 128], BF16)
    a_k = sb("a_k", [128, 16, 128], BF16)
    a_q = sb("a_q", [128, 16, 128], BF16)
    a_qk = sb("a_qk", [128, 16, 128], BF16)
    dg_nt = sb("dg_nt", [128, 16, 32], BF16)
    dg_n = sb("dg_n", [128, 16, 32], BF16)
    dg_pt = sb("dg_pt", [128, 16, 32], BF16)
    dg_pn = sb("dg_pn", [128, 16, 32], BF16)
    dg_tt = sb("dg_tt", [128, 16, 32], BF16)
    x_ri = sb("x_ri", [128, 8, 128], BF16)
    x_sol = sb("x_sol", [128, 8, 128], BF16)
    y_lc = sb("y_lc", [128, 1024], BF16)
    z_f = sb("z_f", [128, 8, 128], BF16)
    u_f = sb("u_f", [128, 8, 128], BF16)
    bck_t = sb("bck_t", [128, 1024], BF16)
    kck_t = sb("kck_t", [128, 1024], BF16)
    s_st = sb("s_st", [128, 8, 64], F32)
    s_sb = sb("s_sb", [128, 8, 64], BF16)
    tr_st = sb("tr_st", [128, 8, 64], F32)
    tr_sb = sb("tr_sb", [128, 8, 64], BF16)
    st_t1 = sb("st_t1", [128, 8, 64], F32)
    w_sb = sb("w_sb", [128, 1024], BF16)
    sict = sb("sict", [128, 8, 64], F32)
    sict_b = sb("sict_b", [128, 8, 64], BF16)
    trp = alias("trp", "dg_tt", [128, 8, 64])
    sp_f = alias("sp_f", "dg_nt", [128, 8, 64])
    gat_s = alias("gat_s", "dg_pt", [128, 8, 64])
    gat_t = alias("gat_t", "dg_pn", [128, 8, 64])
    gn_st = alias("gn_st", "sc_f1", [128, 80])
    bn6 = alias("bn6", "sc_f2", [128, 6])
    bn2c = SB["sc_f2"][:, 8:10]
    SB["bn2"] = bn2c
    bn2 = bn2c
    q_c1 = SB["sc_f2"][:, 16:24]
    SB["q_c1"] = q_c1
    y_fin = alias("y_fin", "f_t1", [128, 1024])
    x2_t = alias("x2_t", "f_t2", [128, 1024])
    xt_in = alias("xt_in", "f_t3", [128, 1024])
    lad_u = alias("lad_u", "f_sig", [128, 1024])
    lad_t = alias("lad_t", "t_b1", [128, 1024])
    lad_m = alias("lad_m", "t_g", [128, 1024])
    o_fin = alias("o_fin", "g_buf", [128, 1024])
    t_ld1 = alias("t_ld1", "f_xx", [128, 1024])
    t_ldp1 = alias("t_ldp1", "f_mix", [128, 1024])
    t_ldp2 = alias("t_ldp2", "f_rkp", [128, 1024])
    y2_t = alias("y2_t", "f_r", [128, 1024])
    y2_f = alias("y2_f", "f_k0", [128, 8, 128])
    xq_t = alias("xq_t", "f_kk", [128, 1024])
    xq_f = alias("xq_f", "f_kkn", [128, 8, 128])
    f_khO = alias("f_khO", "f_xx", [128, 8, 128])
    f_qdO = alias("f_qdO", "f_mix", [128, 8, 128])
    u_fO = alias("u_fO", "ek", [128, 8, 128])
    z_fO = alias("z_fO", "eg", [128, 8, 128])
    x_u = alias("x_u", "f_r", [128, 4, 128])
    x_h = alias("x_h", "f_r", [128, 4, 128], off=512)

    PG0 = nc.alloc_psum_tensor("PG0", [128, 8, 128], F32).ap()
    PG1 = nc.alloc_psum_tensor("PG1", [128, 8, 128], F32).ap()
    PG2 = nc.alloc_psum_tensor("PG2", [128, 8, 128], F32).ap()
    PU0 = nc.alloc_psum_tensor("PU0", [128, 512], F32).ap()
    PU1 = nc.alloc_psum_tensor("PU1", [128, 512], F32).ap()

    em.make_sems()

    def r128(ap_):
        return ap_.rearrange("(kc p) n -> p kc n", p=128)

    for dst, src in ((w_bigA, wrT), (w_bigB, wkT), (w_bigC, wvT),
                     (w_maa1, maa_w1T), (w_g1, gate_w1T),
                     (w_d1, decay_w1T), (w_s1, s_w1p), (w_fblk, fblk),
                     (w_hblk, hblk)):
        em.dma("G", dst, r128(src))
    em.dma("G", w_maa2, maa_w2)
    em.dma("G", w_g2, gate_w2)
    em.dma("G", w_d2, decay_w2)
    em.dma("G", w_s2, s_w2p)
    em.dma("G", c_hsel, hsel.rearrange("h (fc p) -> h fc p", p=128))
    em.dma("G", c_id, ident)
    em.dma("G", c_idblk, idblk)
    em.dma("G", c_ms, mask_s)
    em.dma("G", c_mi, mask_i)
    em.dma("G", c_one, onesc)
    em.dma("G", c_idlo, idlo)
    em.dma("G", c_idhi, idhi)
    em.dma("G", c_dup, iddup)
    em.dma("S", c_id32, ident)
    em.dma("S", sel, selq)
    em.dma("S", v_fm, vfmi.rearrange("p (fc i) -> p fc i", i=16))
    em.barrier()

    if stage < 1:
        nchunks = 0
    for n, j in NEGI.items():
        em.op("D", "tensor_scalar_mul", v_neg[:, :, j:j + 1],
              v_fm[:, :, VIDX[n]:VIDX[n] + 1], -1.0)
    for row_i, bc in ((VIDX["lnw"], bc_lnw), (VIDX["lnb"], bc_lnb),
                      (VIDX["ws"], bc_ws), (VIDX["fbias"], bc_fb)):
        for fc in range(8):
            em.op("D", "tensor_scalar_mul", xlc[:, fc, 0:128],
                  c_one, v_fm[:, fc, row_i:row_i + 1])
        em.wait_dma("A", "A")
        em.wait("A", "D")
        for fc in range(8):
            em.dma("A", bc[:, 128 * fc:128 * fc + 128], xlc[:, fc, 0:128],
                   transpose=True)
        em.wait_dma("D", "A")
    em.op("D", "memset", cb[:, 0:1], 1e-5)
    em.op("D", "memset", xlc, 0.0)
    em.op("D", "memset", cl_all, 0.0)
    em.op("D", "memset", x_ri, 0.0)
    em.op("D", "memset", xsq, 0.0)
    em.op("D", "memset", sc_m, 0.0)
    em.op("D", "memset", sc_r, 0.0)
    em.op("D", "memset", sc_f1, 0.0)
    em.op("D", "memset", sc_f2, 0.0)
    em.op("D", "memset", cb[:, 1:2], 1e-24)
    em.op("D", "memset", cb[:, 2:3], 0.00064)
    em.op("D", "memset", s_st, 0.0)
    em.op("D", "memset", s_sb, 0.0)
    em.op("D", "memset", sict, 0.0)
    em.op("D", "memset", sict_b, 0.0)
    em.op("D", "memset", g_buf[:, :, 0:1], 0.0)
    for pr in range(8):
        em.op("D", "tensor_copy", tr_st[0:64, pr, :], c_id32[0:64, 0:64])
        em.op("D", "tensor_copy", tr_st[64:128, pr, :], c_id32[64:128, 64:128])
        em.op("D", "tensor_copy", tr_sb[0:64, pr, :], c_id[0:64, 0:64])
        em.op("D", "tensor_copy", tr_sb[64:128, pr, :], c_id[64:128, 64:128])
        em.op("D", "tensor_copy", c_trid[0:64, pr, :], c_id[0:64, 0:64])
        em.op("D", "tensor_copy", c_trid[64:128, pr, :], c_id[64:128, 64:128])
    em.barrier()

    def head_rows(h):
        return 64 * (h % 2), h // 2

    if stage < 2:
        nchunks = 0

    def _passA_chunk(c):
        t0 = c * L
        em.wait_dma("G", "G")
        em.wait("G", "D")
        em.wait("G", "T")
        em.dma("G", xlc[:, :, 0:129],
               xcf.rearrange("(fc p) t -> p fc t", p=128)[:, :, t0:t0 + 129])
        em.wait_dma("D", "G")
        em.op("D", "tensor_mul", xsq[:, :, 0:129], xlc[:, :, 0:129],
              xlc[:, :, 0:129])
        em.barrier(("T", "D"))
        for fc in range(8):
            em.op("T", "matmul", PU0[:, 0:129], c_one, xlc[:, fc, 0:129],
                  pgkey=("PU0", 0))
        for fc in range(8):
            em.op("T", "matmul", PU0[:, 256:385], c_one, xsq[:, fc, 0:129],
                  pgkey=("PU0", 0))
        em.mm_close(("PU0", 0))
        em.wait("D", "T")
        em.op("D", "tensor_scalar_mul", sc_f1[:, 0:129], PU0[:, 0:129],
              1.0 / 1024.0)
        em.op("D", "tensor_scalar_mul", sc_f2[:, 0:129], PU0[:, 256:385],
              1.0 / 1024.0)
        em.wait("D", "D")
        em.op("D", "tensor_copy", sc_m[:, 0:129], sc_f1[:, 0:129])
        em.op("D", "tensor_mul", sc_f1[:, 0:129], sc_f1[:, 0:129],
              sc_f1[:, 0:129])
        em.wait("D", "D")
        em.op("D", "tensor_sub", sc_f2[:, 0:129], sc_f2[:, 0:129],
              sc_f1[:, 0:129])
        em.wait("A", "D")
        em.op("A", "activation", sc_f1[:, 0:129], sc_f2[:, 0:129], AF.Ln,
              bias=cb[:, 0:1], scale=1.0)
        em.wait("A", "A")
        em.op("A", "activation", sc_f2[:, 0:129], sc_f1[:, 0:129], AF.Exp,
              scale=-0.5)
        em.wait("D", "A")
        em.op("D", "tensor_copy", sc_r[:, 0:129], sc_f2[:, 0:129])
        em.wait("D", "D")
        em.op("D", "tensor_sub", xlc[:, :, 0:129], xlc[:, :, 0:129],
              sc_m[:, 0:129].unsqueeze(1)
              .broadcast_to([128, 8, 129]))
        em.wait("D", "D")
        em.op("D", "tensor_mul", xlc[:, :, 0:129], xlc[:, :, 0:129],
              sc_r[:, 0:129].unsqueeze(1)
              .broadcast_to([128, 8, 129]))
        em.wait("T", "D")
        if sub <= 1:
            em.barrier()
            return
        xl = xlc[:, :, 1:129]
        xls = xlc[:, :, 0:128]
        em.wait("D", "D")
        for fc in range(8):
            em.op("D", "tensor_sub", f_xx[:, fc, :], xls[:, fc, :],
                  xl[:, fc, :])
        em.wait("D", "D")
        for fc in range(8):
            em.op("D", "scalar_tensor_tensor", f_mix[:, fc, :], f_xx[:, fc, :],
                  v_fm[:, fc, VIDX["tmx"]:VIDX["tmx"] + 1], xl[:, fc, :],
                  AL.mult, AL.add)
        if sub == 10:
            em.barrier()
            return
        em.barrier(("T", "D"))
        for kc in range(8):
            em.op("T", "matmul", PU0[:, 0:128], w_maa1[:, kc, :],
                  f_mix[:, kc, :], pgkey=("PU0", 0))
        em.mm_close(("PU0", 0))
        em.wait("A", "T")
        em.op("A", "activation", f_lo, PU0[:, 0:128], AF.Tanh)
        em.wait("D", "A")
        em.op("D", "memset", f_loP, 0.0)
        em.wait("D", "D")
        for gg in range(4):
            em.op("D", "tensor_copy", f_loP[32 * gg:32 * gg + 32, gg, :],
                  f_lo[32 * gg:32 * gg + 32, :])
        em.barrier(("T", "A", "D"))
        if sub == 11:
            em.barrier()
            return

        for g in range(min(4, sub - 11 if 11 < sub < 16 else 4)):
            vi = VIDX["tmaa%d" % g]
            for m in range(8):
                em.op("T", "matmul", PG0[:, m, :],
                      w_maa2[:, 128 * m:128 * m + 128],
                      f_loP[:, g, :], pgkey=("PG0", m // 4))
            em.mm_close(("PG0", 0), ("PG0", 1))
            em.wait("D", "T")
            for fc in range(8):
                em.op("D", "tensor_mul", f_t1[:, fc, :], f_xx[:, fc, :],
                      PG0[:, fc, :])
            for fc in range(8):
                em.op("D", "scalar_tensor_tensor", f_t2[:, fc, :],
                      f_xx[:, fc, :], v_fm[:, fc, vi:vi + 1], xl[:, fc, :],
                      AL.mult, AL.add)
            em.wait("D", "D")
            for fc in range(8):
                em.op("D", "tensor_add", f_xm[:, fc, :], f_t1[:, fc, :],
                      f_t2[:, fc, :])
            em.barrier(("T", "D"))
            if sub == 16 and g == 1:
                em.barrier()
                return
            if g == 0:
                for m in range(8):
                    for kc in range(8):
                        em.op("T", "matmul", PG1[:, m, :],
                              w_bigA[:, kc, 128 * m:128 * m + 128],
                              f_xm[:, kc, :], pgkey=("PG1", m // 4))
                for kc in range(8):
                    em.op("T", "matmul", PU0[:, 0:128], w_g1[:, kc, :],
                          f_xm[:, kc, :], pgkey=("PU0", 0))
                em.mm_close(("PG1", 0), ("PG1", 1), ("PU0", 0))
                em.wait("D", "T")
                em.op("D", "tensor_copy", f_r, PG1)
                em.wait("A", "T")
                em.op("A", "activation", f_gh, PU0[:, 0:128], AF.Tanh)
                em.barrier(("T", "D", "A"))
                for half in range(2):
                    em.op("T", "matmul", PU1[:, 0:512], f_gh,
                          w_g2[:, 512 * half:512 * half + 512],
                          pgkey=("PU1", 0))
                    em.mm_close(("PU1", 0))
                    em.wait("D", "T")
                    em.op("D", "tensor_copy",
                          t_g[:, 512 * half:512 * half + 512], PU1[:, 0:512])
                    em.wait("T", "D")
            elif g == 1:
                for kc in range(8):
                    em.op("T", "matmul", PU0[0:64, 0:128], w_d1[:, kc, :],
                          f_xm[:, kc, :], pgkey=("PU0", 0))
                for kc in range(8):
                    em.op("T", "matmul", PU0[0:64, 128:256], w_s1[:, kc, 0:64],
                          f_xm[:, kc, :], pgkey=("PU0", 0))
                em.mm_close(("PU0", 0))
                em.wait("A", "T")
                em.op("A", "activation", f_dh, PU0[0:64, 0:128], AF.Tanh)
                em.wait("D", "T")
                em.op("D", "memset", f_shP, 0.0)
                em.wait("D", "D")
                em.op("D", "tensor_copy", f_shP[0:32, 0, :],
                      PU0[0:32, 128:256])
                em.op("D", "tensor_copy", f_shP[32:64, 1, :],
                      PU0[32:64, 128:256])
                em.barrier(("T", "D", "A"))
                for m in range(8):
                    em.op("T", "matmul", PG1[:, m, :],
                          w_d2[:, 128 * m:128 * m + 128], f_dh,
                          pgkey=("PG1", m // 4))
                em.mm_close(("PG1", 0), ("PG1", 1))
                for m in range(8):
                    em.op("T", "matmul", PG2[:, m, :],
                          w_s2[:, 128 * m:128 * m + 128], f_shP[:, 0, :],
                          pgkey=("PG2", m // 4))
                for m in range(8):
                    em.op("T", "matmul", PG0[:, m, :],
                          w_s2[:, 128 * m:128 * m + 128], f_shP[:, 1, :],
                          pgkey=("PG0", m // 4))
                em.mm_close_all()
                em.wait("D", "T")
                em.op("D", "tensor_tensor", f_t1, PG1,
                      v_fm[:, :, VIDX["td"]:VIDX["td"] + 1]
                      .broadcast_to([128, 8, L]), AL.add)
                em.op("D", "tensor_tensor", f_t2, PG2,
                      v_fm[:, :, VIDX["taaaaa"]:VIDX["taaaaa"] + 1]
                      .broadcast_to([128, 8, L]), AL.add)
                em.op("D", "tensor_tensor", f_t3, PG0,
                      v_fm[:, :, VIDX["tma"]:VIDX["tma"] + 1]
                      .broadcast_to([128, 8, L]), AL.add)
                em.wait("A", "D")
                em.op("A", "activation", f_sig, f_t1, AF.Sigmoid)
                em.op("A", "activation", f_a, f_t2, AF.Sigmoid)
                em.op("A", "activation", f_ap, f_t2, AF.Sigmoid, scale=-1.0)
                em.op("A", "activation", f_ma, f_t3, AF.Sigmoid)
                em.barrier(("T", "D", "A"))
            elif g == 2:
                for m in range(8):
                    for kc in range(8):
                        em.op("T", "matmul", PG1[:, m, :],
                              w_bigB[:, kc, 128 * m:128 * m + 128],
                              f_xm[:, kc, :], pgkey=("PG1", m // 4))
                em.mm_close(("PG1", 0), ("PG1", 1))
                for kc in range(8):
                    em.op("T", "matmul", PU0[0:64, 256:384],
                          w_s1[:, kc, 64:128], f_xm[:, kc, :],
                          pgkey=("PU0s",))
                em.mm_close(("PU0s",))
                em.wait("D", "T")
                em.op("D", "tensor_copy", f_k0, PG1)
                em.op("D", "tensor_copy", f_shB[0:32, :], PU0[0:32, 256:384])
                em.wait("A", "T")
                em.op("A", "activation", f_shB[32:64, :], PU0[32:64, 256:384],
                      AF.Tanh)
                em.wait_dma("G", "G")
                em.wait("G", "D")
                em.wait("G", "A")
                em.dma("G", f_shP[64:96, 2, :], f_shB[0:32, :])
                em.dma("G", f_shP[96:128, 3, :], f_shB[32:64, :])
                em.wait_dma("T", "G")
                em.barrier(("T", "D", "A"))
                for m in range(8):
                    em.op("T", "matmul", PG2[:, m, :],
                          w_s2[:, 128 * m:128 * m + 128], f_shP[:, 2, :],
                          pgkey=("PG2", m // 4))
                for m in range(8):
                    em.op("T", "matmul", PG0[:, m, :],
                          w_s2[:, 128 * m:128 * m + 128], f_shP[:, 3, :],
                          pgkey=("PG0", m // 4))
                em.mm_close_all()
                em.wait("D", "T")
                em.op("D", "tensor_tensor", f_t1, PG2,
                      v_fm[:, :, VIDX["tmk"]:VIDX["tmk"] + 1]
                      .broadcast_to([128, 8, L]), AL.add)
                em.op("D", "tensor_add", f_kk, f_k0, PG0)
                em.wait("A", "D")
                em.op("A", "activation", f_mk, f_t1, AF.Sigmoid)
                em.barrier(("T", "D", "A"))
            else:
                for half in range(2):
                    for kc in range(8):
                        em.op("T", "matmul", PU1[:, 0:512], f_xm[:, kc, :],
                              w_bigC[:, kc, 512 * half:512 * half + 512],
                              pgkey=("PU1", 0))
                    em.mm_close(("PU1", 0))
                    em.wait("D", "T")
                    em.op("D", "tensor_copy",
                          t_v[:, 512 * half:512 * half + 512], PU1[:, 0:512])
                    em.wait("T", "D")
                em.barrier(("T", "D"))

        if 11 < sub < 16:
            em.barrier()
            return
        if sub <= 2:
            em.barrier()
            return
        em.op("D", "tensor_mul", f_rkp, f_kk, f_kk)
        em.barrier(("T", "D"))
        for fc in range(8):
            em.op("T", "matmul", PU0[0:16, 0:128], w_hblk[:, fc, :],
                  f_rkp[:, fc, :], pgkey=("PU0", 0))
        em.mm_close(("PU0", 0))
        em.wait("A", "T")
        em.op("A", "activation", rn_h, PU0[0:16, 0:128], AF.Ln,
              bias=cb[0:16, 1:2], scale=1.0)
        em.wait("A", "A")
        em.op("A", "activation", rn_h, rn_h, AF.Exp, scale=-0.5)
        em.wait("D", "A")
        em.op("D", "tensor_copy", t_rk[:, 0:128], rn_h)
        em.barrier(("T", "D", "A"))
        for fc in range(8):
            em.op("T", "matmul", PG0[:, fc, :], c_hsel[:, fc, :],
                  t_rk[:, 0:128], pgkey=("PG0", fc // 4))
        em.mm_close(("PG0", 0), ("PG0", 1))
        em.wait("D", "T")
        em.op("D", "tensor_mul", f_kkn, f_kk, PG0)
        em.wait("D", "D")
        em.op("D", "tensor_mul", f_t1, f_ma, f_ap)
        em.wait("D", "D")
        em.op("D", "tensor_add", f_ap, f_a, f_t1)
        em.op("A", "activation", f_t2, f_sig, AF.Ln)
        em.wait("D", "A")
        em.op("D", "scalar_tensor_tensor", f_t2, f_t2, -0.5, f_mk,
              AL.add, AL.mult)
        em.wait("A", "D")
        em.op("A", "activation", f_t2, f_t2, AF.Exp)
        em.wait("D", "A")
        em.op("D", "tensor_mul", f_t1, f_k0, f_ap)
        em.wait("D", "D")
        em.op("D", "tensor_mul", f_k, f_t1, f_t2)
        em.wait("D", "D")
        em.op("D", "scalar_tensor_tensor", f_t2, f_kkn, -1.0, f_a,
              AL.mult, AL.mult)
        em.wait("D", "D")
        em.op("D", "tensor_mul", f_rkp, f_r, f_k)
        em.barrier(("T", "D"))
        for fc in range(8):
            em.op("T", "matmul", PU0[0:16, 128:256], w_fblk[:, fc, :],
                  f_rkp[:, fc, :], pgkey=("PU0", 0))
        em.mm_close(("PU0", 0))
        em.wait("D", "T")
        em.op("D", "tensor_copy", t_rk[:, 0:128], PU0[0:16, 128:256])
        em.wait_dma("A", "A")
        em.wait("A", "D")
        em.dma("A", t_rkT, t_rk[:, 0:128], transpose=True)
        em.wait_dma("D", "A")
        em.op("D", "tensor_tensor", t_b1.rearrange("p (h d) -> p h d", d=64),
              t_v.rearrange("p (h d) -> p h d", d=64),
              t_rkT.unsqueeze(2).broadcast_to([128, 16, 64]),
              AL.mult)
        em.wait("D", "D")
        em.op("D", "tensor_add", t_b1, t_b1, bc_lnb)
        em.wait("D", "D")
        em.op("D", "tensor_mul", t_p, t_b1, t_g)
        em.op("D", "tensor_mul", t_p2, t_g, bc_lnw)
        em.wait_dma("G", "G")
        em.wait("G", "D")
        em.dma("G", s_p2[c], t_p)
        em.dma("G", s_p1[c], t_p2)

        if sub <= 3:
            em.barrier()
            return
        if not do_scan:
            em.barrier()
            return

        for fc in range(8):
            em.op("D", "tensor_tensor_scan", g_buf[:, fc, 1:L + 1],
                  f_sig[:, fc, :], f_sig[:, fc, :], 0.0, AL.add, AL.bypass)
        em.wait("A", "D")
        em.op("A", "activation", eg, g_buf, AF.Exp, scale=-C0)
        em.op("A", "activation", ek, g_buf[:, :, 1:L + 1], AF.Exp,
              scale=C0)
        em.op("A", "activation", cl_all[:, :, c:c + 1],
              g_buf[:, :, L:L + 1], AF.Exp, scale=-C0)
        em.wait("D", "A")
        em.op("D", "tensor_mul", f_qd, f_r, eg[:, :, 1:L + 1])
        em.op("D", "tensor_mul", f_kh, f_kkn, eg[:, :, 0:L])
        em.op("D", "tensor_mul", f_bc, f_t2, ek)
        em.op("D", "tensor_mul", f_kc, f_k, ek)
        em.wait("D", "D")
        em.op("D", "memset", f_khO[0:64, :, :], 0.0)
        em.op("D", "memset", f_qdO[0:64, :, :], 0.0)
        em.wait("D", "D")
        em.op("D", "tensor_copy", f_khO[64:128, :, :], f_kh[64:128, :, :])
        em.op("D", "tensor_copy", f_qdO[64:128, :, :], f_qd[64:128, :, :])
        em.wait_dma("A", "A")
        em.wait("A", "D")
        em.wait_dma("S", "S")
        em.wait("S", "D")
        for fc in range(8):
            em.dma("A", bck_t[:, 128 * fc:128 * fc + 128], f_bc[:, fc, :],
                   transpose=True)
            em.dma("S", kck_t[:, 128 * fc:128 * fc + 128], f_kc[:, fc, :],
                   transpose=True)
        em.barrier(("T", "D", "A"))

        if sub <= 4:
            em.barrier()
            return
        for wv in range(2):
            for pair in range(2):
                rf = f_kh if pair == 0 else f_qd
                rfO = f_khO if pair == 0 else f_qdO
                d0 = a_ss if pair == 0 else a_q
                d1 = a_k if pair == 0 else a_qk
                msk = c_ms if pair == 0 else c_mi
                for j in range(4):
                    pr = 4 * wv + j
                    em.op("T", "matmul", PG0[:, j, :],
                          f_bc[0:64, pr, :], rf[0:64, pr, :],
                          pgkey=("S3", 0))
                    em.op("T", "matmul", PG1[:, j, :],
                          f_kc[0:64, pr, :], rf[0:64, pr, :],
                          pgkey=("S3", 1))
                for j in range(4):
                    pr = 4 * wv + j
                    em.op("T", "matmul", PG0[:, 4 + j, :],
                          f_bc[:, pr, :], rfO[:, pr, :], pgkey=("S3", 2))
                    em.op("T", "matmul", PG1[:, 4 + j, :],
                          f_kc[:, pr, :], rfO[:, pr, :], pgkey=("S3", 3))
                em.mm_close_all()
                em.wait("D", "T")
                d0v = d0.rearrange("p (a two) b -> p a two b", two=2)
                d1v = d1.rearrange("p (a two) b -> p a two b", two=2)
                mb = msk.unsqueeze(1).broadcast_to([128, 4, 128])
                em.op("D", "tensor_tensor", d0v[:, 4 * wv:4 * wv + 4, 0, :],
                      PG0[:, 0:4, :], mb, AL.mult)
                em.op("D", "tensor_tensor", d0v[:, 4 * wv:4 * wv + 4, 1, :],
                      PG0[:, 4:8, :], mb, AL.mult)
                em.op("D", "tensor_tensor", d1v[:, 4 * wv:4 * wv + 4, 0, :],
                      PG1[:, 0:4, :], mb, AL.mult)
                em.op("D", "tensor_tensor", d1v[:, 4 * wv:4 * wv + 4, 1, :],
                      PG1[:, 4:8, :], mb, AL.mult)
                em.wait("T", "D")
        if sub <= 5:
            em.barrier()
            return
        for b in range(4):
            em.op("D", "tensor_copy", dg_nt[32 * b:32 * b + 32, :, :],
                  a_ss[32 * b:32 * b + 32, :, 32 * b:32 * b + 32])
        em.wait("D", "D")
        em.op("D", "transpose", dg_n.rearrange("p h t -> p (h t)"),
              dg_nt.rearrange("p h t -> p (h t)"))
        em.wait("D", "D")
        em.op("D", "tensor_tensor", dg_tt, dg_nt,
              c_idblk.unsqueeze(1).broadcast_to([128, 16, 32]),
              AL.add)
        em.op("D", "tensor_copy", dg_pt, dg_nt)
        em.op("D", "tensor_copy", dg_pn, dg_n)
        em.barrier(("T", "D"))
        em.wait("D", "D")
        em.op("D", "memset", bd_pn[:, :, 0:128], 0.0)
        em.wait("D", "D")
        for b in range(4):
            em.op("D", "tensor_copy",
                  bd_pn[32 * b:32 * b + 32, :, 32 * b:32 * b + 32],
                  dg_pn[32 * b:32 * b + 32, :, :])
        em.barrier(("T", "D"))
        PU0h = PU0.rearrange("p (h t) -> p h t", t=32)
        for rnd in range(4):
            for h in range(16):
                em.op("T", "matmul", PU0[:, 32 * h:32 * h + 32],
                      bd_pn[:, h, 0:128], dg_pt[:, h, :], pgkey=("PT",))
            em.mm_close(("PT",))
            em.wait("D", "T")
            em.op("D", "tensor_copy", dg_pt, PU0h)
            em.wait("D", "D")
            em.op("D", "transpose", dg_pn.rearrange("p h t -> p (h t)"),
                  dg_pt.rearrange("p h t -> p (h t)"))
            em.wait("D", "D")
            for b in range(4):
                em.op("D", "tensor_copy",
                      bd_pn[32 * b:32 * b + 32, :, 32 * b:32 * b + 32],
                      dg_pn[32 * b:32 * b + 32, :, :])
            em.wait("T", "D")
            for h in range(16):
                em.op("T", "matmul", PU0[:, 32 * h:32 * h + 32],
                      bd_pn[:, h, 0:128], dg_tt[:, h, :], pgkey=("PT",))
            em.mm_close(("PT",))
            em.wait("D", "T")
            em.op("D", "tensor_tensor", dg_tt, dg_tt, PU0h, AL.add)
            em.wait("T", "D")

        if sub <= 6:
            em.barrier()
            return
        em.wait_dma("T", "A")
        PU0v = PU0.rearrange("p (a b) -> p a b", b=128)
        PG1zv = PG1.rearrange("p a b -> p (a b)") \
                   .rearrange("p (a b) -> p a b", b=256)
        ywv = y_lc.rearrange("p (w hh d) -> p w hh d", w=2, d=64)
        uwv = u_tk.rearrange("p (w hh d) -> p w hh d", w=2, d=64)
        xs4 = x_sol.rearrange("p (a two) b -> p a two b", two=2)
        em.wait("D", "D")
        for b in range(4):
            em.op("D", "tensor_copy",
                  bd_pn[32 * b:32 * b + 32, :, 32 * b:32 * b + 32],
                  dg_tt[32 * b:32 * b + 32, :, :])
        em.barrier(("T", "D"))
        for wv in range(2):
            for j in range(4):
                pr = 4 * wv + j
                em.op("T", "matmul", PG2[:, 2 * j, 0:64],
                      f_kh[0:64, pr, :], s_sb[0:64, pr, :],
                      pgkey=("RS", wv, j // 2))
                em.op("T", "matmul", PG2[:, 2 * j, 64:128],
                      f_kh[0:64, pr, :], c_id[0:64, 0:64],
                      pgkey=("RS", wv, j // 2))
            for j in range(4):
                pr = 4 * wv + j
                em.op("T", "matmul", PG2[:, 2 * j + 1, 0:64],
                      f_khO[:, pr, :], s_sb[:, pr, :],
                      pgkey=("RS", wv, j // 2))
                em.op("T", "matmul", PG2[:, 2 * j + 1, 64:128],
                      f_khO[:, pr, :], c_id[:, 64:128],
                      pgkey=("RS", wv, j // 2))
            for hh in range(8):
                h = 8 * wv + hh
                em.op("T", "matmul", PG2[:, hh, 0:64],
                      a_k[:, h, :], t_v[:, 64 * h:64 * h + 64],
                      pgkey=("RS", wv, hh // 4))
            em.mm_close(("RS", wv, 0), ("RS", wv, 1))
            if sub == 71:
                em.barrier()
                return
            em.wait("D", "T")
            em.wait("D", "D")
            em.op("D", "memset", x_sol, 0.0)
            for I in range(4):
                if I > 0:
                    em.wait("T", "D")
                    for hh in range(8):
                        h = 8 * wv + hh
                        em.op("T", "matmul", PG1[:, hh, :],
                              a_ss[:, h, :], x_sol[:, hh, :],
                              pgkey=("CR", wv, I, hh // 4))
                    em.mm_close(("CR", wv, I, 0), ("CR", wv, I, 1))
                em.wait("D", "T")
                em.op("D", "tensor_copy", x_ri[32 * I:32 * I + 32, :, :],
                      PG2[32 * I:32 * I + 32, :, :])
                if I > 0:
                    em.wait("D", "D")
                    em.op("D", "tensor_tensor",
                          x_ri[32 * I:32 * I + 32, :, :],
                          x_ri[32 * I:32 * I + 32, :, :],
                          PG1[32 * I:32 * I + 32, :, :], AL.add)
                em.wait("T", "D")
                for hh in range(8):
                    h = 8 * wv + hh
                    em.op("T", "matmul", PG0[:, hh, :],
                          bd_pn[:, h, 0:128], x_ri[:, hh, :],
                          pgkey=("SV", wv, I, hh // 4))
                em.mm_close(("SV", wv, I, 0), ("SV", wv, I, 1))
                em.wait("D", "T")
                em.op("D", "tensor_copy", x_sol[32 * I:32 * I + 32, :, :],
                      PG0[32 * I:32 * I + 32, :, :])
            em.wait("T", "D")
            if sub == 72:
                em.barrier()
                return
            for j in range(4):
                pr = 4 * wv + j
                em.op("T", "matmul", PG0[:, 2 * j, 0:64],
                      f_qd[0:64, pr, :], s_sb[0:64, pr, :],
                      pgkey=("Y", wv, j // 2))
            for j in range(4):
                pr = 4 * wv + j
                em.op("T", "matmul", PG0[:, 2 * j + 1, 0:64],
                      f_qdO[:, pr, :], s_sb[:, pr, :],
                      pgkey=("Y", wv, j // 2))
            for hh in range(8):
                h = 8 * wv + hh
                em.op("T", "matmul", PG0[:, hh, 0:64],
                      a_q[:, h, :], x_sol[:, hh, 0:64],
                      pgkey=("Y", wv, hh // 4))
                em.op("T", "matmul", PG0[:, hh, 0:64],
                      a_qk[:, h, :], t_v[:, 64 * h:64 * h + 64],
                      pgkey=("Y", wv, hh // 4))
            em.mm_close(("Y", wv, 0), ("Y", wv, 1))
            em.wait("D", "T")
            em.op("D", "tensor_copy", ywv[:, wv], PG0[:, :, 0:64])
            em.op("D", "tensor_copy", uwv[:, wv], x_sol[:, :, 64:128])
            em.op("D", "tensor_copy", x_u[:, :, 0:64], xs4[:, :, 0, 64:128])
            em.op("D", "tensor_copy", x_u[:, :, 64:128], xs4[:, :, 1, 64:128])
            em.op("D", "tensor_copy", x_h[:, :, 0:64], xs4[:, :, 0, 0:64])
            em.op("D", "tensor_copy", x_h[:, :, 64:128], xs4[:, :, 1, 0:64])
            em.wait("T", "D")
            if sub == 73:
                em.barrier()
                return
            for j in range(4):
                pr = 4 * wv + j
                em.op("T", "matmul", PG1zv[:, j, 0:256],
                      x_u[:, j, :],
                      a_q[:, 8 * wv + 2 * j:8 * wv + 2 * j + 2, :]
                      .rearrange("p a b -> p (a b)"),
                      pgkey=("Z", wv, j // 2))
                em.op("T", "matmul", PG1zv[:, j, 0:128],
                      c_idlo, f_qd[:, pr, :], pgkey=("Z", wv, j // 2))
                em.op("T", "matmul", PG1zv[:, j, 128:256],
                      c_idhi, f_qd[:, pr, :], pgkey=("Z", wv, j // 2))
            em.mm_close(("Z", wv, 0), ("Z", wv, 1))
            em.wait("D", "T")
            em.op("D", "tensor_copy", z_f[0:64, 4 * wv:4 * wv + 4, :],
                  PG1zv[0:64, :, 0:128])
            em.op("D", "tensor_copy", z_f[64:128, 4 * wv:4 * wv + 4, :],
                  PG1zv[64:128, :, 128:256])
            em.wait("T", "D")
            if sub == 74:
                em.barrier()
                return
            for j in range(4):
                pr = 4 * wv + j
                em.op("T", "matmul", PU0v[:, j, :],
                      bck_t[:, 128 * pr:128 * pr + 128], x_h[:, j, :],
                      pgkey=("SU", wv))
                em.op("T", "matmul", PU0v[:, j, :],
                      kck_t[:, 128 * pr:128 * pr + 128],
                      t_v[:, 128 * pr:128 * pr + 128], pgkey=("SU", wv))
            em.mm_close(("SU", wv))
            em.wait("D", "T")
            em.op("D", "tensor_add", st_t1[0:64, 4 * wv:4 * wv + 4, :],
                  s_st[0:64, 4 * wv:4 * wv + 4, :], PU0v[0:64, :, 0:64])
            em.op("D", "tensor_add", st_t1[64:128, 4 * wv:4 * wv + 4, :],
                  s_st[64:128, 4 * wv:4 * wv + 4, :],
                  PU0v[64:128, :, 64:128])
            em.wait("T", "D")
        em.wait("D", "D")
        em.op("D", "tensor_tensor", s_st, st_t1,
              cl_all[:, :, c:c + 1].broadcast_to([128, 8, 64]), AL.mult)
        em.wait("D", "D")
        em.op("D", "tensor_copy", s_sb, s_st)
        em.wait("T", "D")
        em.wait_dma("S", "S")
        em.wait("S", "D")
        for fc in range(8):
            em.dma("S", u_f[:, fc, :], u_tk[:, 128 * fc:128 * fc + 128],
                   transpose=True)
        em.wait("G", "D")
        em.wait_dma("G", "G")
        em.dma("G", s_yloc[c], y_lc)
        em.dma("G", s_zf[c], z_f)
        em.dma("G", s_bck[c], bck_t)
        em.wait_dma("G", "S")
        em.dma("G", s_uf[c], u_f)
        if sub <= 7:
            em.barrier()
            return
        em.wait_dma("T", "S")
        em.wait_dma("T", "A")
        PU1v = PU1.rearrange("p (a b) -> p a b", b=128)
        em.wait("D", "D")
        em.wait_dma("D", "S")
        em.op("D", "memset", u_fO[0:64, :, :], 0.0)
        em.wait("D", "D")
        em.op("D", "tensor_copy", u_fO[64:128, :, :], u_f[64:128, :, :])
        em.barrier(("T", "D"))
        for pr in range(8):
            em.op("T", "matmul", PG0[:, pr, 0:64],
                  u_f[0:64, pr, :], tr_sb[0:64, pr, :],
                  pgkey=("WT", pr // 4))
        for pr in range(8):
            em.op("T", "matmul", PG0[:, pr, 64:128],
                  u_fO[:, pr, :], tr_sb[:, pr, :], pgkey=("WT", pr // 4))
        em.mm_close(("WT", 0), ("WT", 1))
        em.wait("D", "T")
        em.op("D", "tensor_copy", w_sb, PG0.rearrange("p a b -> p (a b)"))
        em.wait("T", "D")
        for pr in range(4):
            em.op("T", "matmul", PU0v[:, pr, :],
                  bck_t[:, 128 * pr:128 * pr + 128],
                  w_sb[:, 128 * pr:128 * pr + 128], pgkey=("TU", 0))
        for pr in range(4, 8):
            em.op("T", "matmul", PU1v[:, pr - 4, :],
                  bck_t[:, 128 * pr:128 * pr + 128],
                  w_sb[:, 128 * pr:128 * pr + 128], pgkey=("TU", 1))
        em.mm_close(("TU", 0), ("TU", 1))
        em.wait("D", "T")
        em.op("D", "tensor_add", st_t1[0:64, 0:4, :], tr_st[0:64, 0:4, :],
              PU0v[0:64, :, 0:64])
        em.op("D", "tensor_add", st_t1[64:128, 0:4, :],
              tr_st[64:128, 0:4, :], PU0v[64:128, :, 64:128])
        em.op("D", "tensor_add", st_t1[0:64, 4:8, :], tr_st[0:64, 4:8, :],
              PU1v[0:64, :, 0:64])
        em.op("D", "tensor_add", st_t1[64:128, 4:8, :],
              tr_st[64:128, 4:8, :], PU1v[64:128, :, 64:128])
        em.wait("D", "D")
        em.op("D", "tensor_tensor", tr_st, st_t1,
              cl_all[:, :, c:c + 1].broadcast_to([128, 8, 64]), AL.mult)
        em.wait("D", "D")
        em.op("D", "tensor_copy", tr_sb, tr_st)
        em.barrier(("T", "D"))


    passA_chunk = _passA_chunk
    if stage < 2:
        nchunks = 0
    for c in range(nchunks):
        passA_chunk(c)
    em.barrier()

    if do_scan and do_passb and stage >= 3:
        bdS3 = alias("bdS3", "bd_pn", [128, 8, 128])
        PU0s8 = PU0.rearrange("p (a d) -> p a d", d=64)
        em.wait("D", "T")
        em.wait("D", "D")
        em.op("D", "memset", bdS3, 0.0)
        em.wait("D", "D")
        em.op("D", "tensor_copy", bdS3[0:64, :, 0:64], tr_sb[0:64, :, :])
        em.op("D", "tensor_copy", bdS3[64:128, :, 64:128],
              tr_sb[64:128, :, :])
        em.barrier(("T", "D"))
        for pr in range(8):
            em.op("T", "matmul", PG0[:, pr, 0:64],
                  bdS3[:, pr, :], c_dup, pgkey=("SM", pr // 4))
        em.mm_close_all()
        em.wait("D", "T")
        em.op("D", "tensor_copy", st_t1, PG0[:, :, 0:64])
        em.wait("G", "D")
        em.wait_dma("G", "G")
        em.dma("G", cc_in[0:1024].rearrange("(pr p) i -> p pr i", p=128), s_st)
        em.dma("G", cc_in[1024:2048].rearrange("(pr p) i -> p pr i", p=128),
               st_t1)
        em.wait_dma("G", "G")
        if no_cc:
            em.op("D", "memset", gat_s, 0.0)
            em.op("D", "memset", gat_t, 0.0)
            em.wait("D", "D")
        else:
            em.collective([cc_in.opt()], [cc_out.opt()],
                          [[0, 1, 2, 3], [4, 5, 6, 7]])
        em.dma("G", w_bigA, r128(woT))
        em.dma("G", w_bigB, r128(wqT))
        em.barrier()
        for d in range(3):
            if not no_cc:
                em.wait_dma("G", "G")
                em.wait("G", "D")
                em.dma("G", gat_s, cc_out[2048 * d:2048 * d + 1024]
                       .rearrange("(pr p) i -> p pr i", p=128))
                em.dma("G", gat_t, cc_out[2048 * d + 1024:2048 * d + 2048]
                       .rearrange("(pr p) i -> p pr i", p=128))
                em.wait_dma("D", "G")
            em.wait("D", "D")
            em.op("D", "tensor_scalar_mul", trp, gat_t, sel[:, d:d + 1])
            em.op("D", "tensor_scalar_mul", st_t1, c_trid, sel[:, 4 + d:5 + d])
            em.wait("D", "D")
            em.op("D", "tensor_add", trp, trp,
                  st_t1.rearrange("p a b -> p a b"))
            em.op("D", "tensor_scalar_mul", sp_f, gat_s, sel[:, d:d + 1])
            em.wait("D", "T")
            em.wait("D", "D")
            em.op("D", "tensor_copy", bdS3[0:64, :, 0:64], trp[0:64, :, :])
            em.op("D", "tensor_copy", bdS3[64:128, :, 64:128],
                  trp[64:128, :, :])
            em.barrier(("T", "D"))
            for pr in range(8):
                em.op("T", "matmul", PU0s8[:, pr, :],
                      bdS3[:, pr, :], sict_b[:, pr, :], pgkey=("CM", d))
            em.mm_close_all()
            em.wait("D", "T")
            em.op("D", "tensor_add", sict, PU0s8, sp_f)
            em.wait("D", "D")
            em.op("D", "tensor_copy", sict_b, sict)
            em.wait("T", "D")
        em.barrier()

    def passB_chunk(c):
        em.wait_dma("S", "S")
        em.wait_dma("S", "G")
        em.wait("S", "D")
        em.dma("S", t_ld1, s_yloc[c])
        em.dma("S", t_ldp1, s_p1[c])
        em.dma("S", t_ldp2, s_p2[c])
        em.dma("S", z_f, s_zf[c])
        em.dma("S", u_f, s_uf[c])
        em.dma("S", bck_t, s_bck[c])
        em.dma("S", xt_in.rearrange("p f -> p f"), xtok[c * L:(c + 1) * L, :])
        em.wait_dma("T", "S")
        em.wait_dma("D", "S")
        PG0f2 = PG0.rearrange("p a b -> p (a b)")
        PU0v = PU0.rearrange("p (a b) -> p a b", b=128)
        PU1v = PU1.rearrange("p (a b) -> p a b", b=128)
        em.wait("D", "D")
        em.op("D", "memset", z_fO[0:64, :, :], 0.0)
        em.op("D", "memset", u_fO[0:64, :, :], 0.0)
        em.wait("D", "D")
        em.op("D", "tensor_copy", z_fO[64:128, :, :], z_f[64:128, :, :])
        em.op("D", "tensor_copy", u_fO[64:128, :, :], u_f[64:128, :, :])
        em.barrier(("T", "D"))
        for half in range(2):
            em.op("T", "matmul", PG0f2[:, 512 * half:512 * half + 512],
                  c_id, t_ld1[:, 512 * half:512 * half + 512],
                  pgkey=("PG0", half))
        for h in range(0, 16, 2):
            em.op("T", "matmul", PG0f2[:, 64 * h:64 * h + 64],
                  z_f[0:64, h // 2, :], sict_b[0:64, h // 2, :],
                  pgkey=("PG0", h // 8))
        for h in range(1, 16, 2):
            em.op("T", "matmul", PG0f2[:, 64 * h:64 * h + 64],
                  z_fO[:, h // 2, :], sict_b[:, h // 2, :],
                  pgkey=("PG0", h // 8))
        em.mm_close(("PG0", 0), ("PG0", 1))
        em.wait("D", "T")
        em.op("D", "tensor_copy", y_fin, PG0f2)
        em.wait("T", "D")
        for pr in range(8):
            em.op("T", "matmul", PG1[:, pr, 0:64],
                  u_f[0:64, pr, :], sict_b[0:64, pr, :],
                  pgkey=("WB", pr // 4))
        for pr in range(8):
            em.op("T", "matmul", PG1[:, pr, 64:128],
                  u_fO[:, pr, :], sict_b[:, pr, :], pgkey=("WB", pr // 4))
        em.mm_close(("WB", 0), ("WB", 1))
        em.wait("D", "T")
        em.op("D", "tensor_copy", w_sb, PG1.rearrange("p a b -> p (a b)"))
        em.wait("T", "D")
        for pr in range(4):
            em.op("T", "matmul", PU0v[:, pr, :],
                  bck_t[:, 128 * pr:128 * pr + 128],
                  w_sb[:, 128 * pr:128 * pr + 128], pgkey=("TU", 0))
        for pr in range(4, 8):
            em.op("T", "matmul", PU1v[:, pr - 4, :],
                  bck_t[:, 128 * pr:128 * pr + 128],
                  w_sb[:, 128 * pr:128 * pr + 128], pgkey=("TU", 1))
        em.mm_close(("TU", 0), ("TU", 1))
        em.wait("D", "T")
        em.op("D", "tensor_add", st_t1[0:64, 0:4, :], sict[0:64, 0:4, :],
              PU0v[0:64, :, 0:64])
        em.op("D", "tensor_add", st_t1[64:128, 0:4, :],
              sict[64:128, 0:4, :], PU0v[64:128, :, 64:128])
        em.op("D", "tensor_add", st_t1[0:64, 4:8, :], sict[0:64, 4:8, :],
              PU1v[0:64, :, 0:64])
        em.op("D", "tensor_add", st_t1[64:128, 4:8, :],
              sict[64:128, 4:8, :], PU1v[64:128, :, 64:128])
        em.wait("D", "D")
        em.op("D", "tensor_tensor", sict, st_t1,
              cl_all[:, :, c:c + 1].broadcast_to([128, 8, 64]), AL.mult)
        em.wait("D", "D")
        em.op("D", "tensor_copy", sict_b, sict)
        em.wait("T", "D")
        em.op("D", "tensor_mul", lad_m, y_fin, y_fin)
        em.op("D", "tensor_reduce", gn_st[:, 0:16],
              y_fin.rearrange("p (h d) -> p h d", d=64), AX.X, AL.add)
        em.wait("D", "D")
        em.op("D", "tensor_reduce", gn_st[:, 16:32],
              lad_m.rearrange("p (h d) -> p h d", d=64), AX.X, AL.add)
        em.wait("D", "D")
        em.op("D", "tensor_scalar_mul", gn_st[:, 0:16], gn_st[:, 0:16],
              1.0 / 64.0)
        em.op("D", "tensor_scalar_mul", gn_st[:, 16:32], gn_st[:, 16:32],
              1.0 / 64.0)
        em.wait("D", "D")
        em.op("D", "tensor_mul", gn_st[:, 32:48], gn_st[:, 0:16],
              gn_st[:, 0:16])
        em.wait("D", "D")
        em.op("D", "tensor_sub", gn_st[:, 16:32], gn_st[:, 16:32],
              gn_st[:, 32:48])
        em.wait("A", "D")
        em.op("A", "activation", gn_st[:, 32:48], gn_st[:, 16:32], AF.Ln,
              bias=cb[:, 2:3], scale=1.0)
        em.wait("A", "A")
        em.op("A", "activation", gn_st[:, 48:64], gn_st[:, 32:48], AF.Exp,
              scale=-0.5)
        em.wait("D", "A")
        em.op("D", "tensor_sub", y_fin.rearrange("p (h d) -> p h d", d=64),
              y_fin.rearrange("p (h d) -> p h d", d=64),
              gn_st[:, 0:16].unsqueeze(2)
              .broadcast_to([128, 16, 64]))
        em.wait("D", "D")
        em.op("D", "tensor_mul", y_fin.rearrange("p (h d) -> p h d", d=64),
              y_fin.rearrange("p (h d) -> p h d", d=64),
              gn_st[:, 48:64].unsqueeze(2)
              .broadcast_to([128, 16, 64]))
        em.wait("D", "D")
        em.op("D", "tensor_mul", lad_m, y_fin, t_ldp1)
        em.wait("D", "D")
        em.op("D", "tensor_add", y2_t, lad_m, t_ldp2)
        em.wait_dma("A", "A")
        em.wait("A", "D")
        em.wait_dma("S", "S")
        em.wait("S", "D")
        for fc in range(8):
            em.dma("A" if fc % 2 == 0 else "S", y2_f[:, fc, :],
                   y2_t[:, 128 * fc:128 * fc + 128], transpose=True)
        em.wait_dma("T", "A")
        em.wait_dma("T", "S")
        for half in range(2):
            for kc in range(8):
                em.op("T", "matmul", PU1[:, 0:512], y2_f[:, kc, :],
                      w_bigA[:, kc, 512 * half:512 * half + 512],
                      pgkey=("PU1", 0))
            em.mm_close(("PU1", 0))
            em.wait("D", "T")
            em.op("D", "tensor_tensor", x2_t[:, 512 * half:512 * half + 512],
                  PU1[:, 0:512], xt_in[:, 512 * half:512 * half + 512], AL.add)
            em.wait("T", "D")
        em.wait("D", "D")
        em.op("D", "tensor_reduce", bn2[:, 0:1], x2_t, AX.X, AL.add)
        em.wait("D", "D")
        em.op("D", "tensor_scalar_mul", bn2[:, 0:1], bn2[:, 0:1], 1.0 / 1024.0)
        em.wait("D", "D")
        em.op("D", "tensor_scalar", lad_u, x2_t, bn2[:, 0:1], None,
              AL.subtract)
        em.wait("D", "D")
        em.op("D", "tensor_mul", lad_m, lad_u, lad_u)
        em.wait("D", "D")
        em.op("D", "tensor_reduce", bn2[:, 1:2], lad_m, AX.X, AL.add)
        em.wait("D", "D")
        em.op("D", "tensor_scalar_mul", bn2[:, 1:2], bn2[:, 1:2], 1.0 / 1024.0)
        em.wait("A", "D")
        em.op("A", "activation", q_c1[:, 0:1], bn2[:, 1:2], AF.Ln,
              bias=cb[:, 0:1], scale=1.0)
        em.wait("A", "A")
        em.op("A", "activation", q_c1[:, 0:1], q_c1[:, 0:1], AF.Exp,
              scale=-0.5)
        em.wait("D", "D")
        em.op("D", "tensor_reduce", q_c1[:, 1:2], lad_u, AX.X, AL.max,
              apply_absolute_value=True)
        em.wait("D", "D")
        em.wait("D", "A")
        em.op("D", "tensor_mul", q_c1[:, 2:3], q_c1[:, 1:2], q_c1[:, 0:1])
        em.wait("D", "D")
        em.op("D", "tensor_scalar_max", q_c1[:, 2:3], q_c1[:, 2:3], 1e-6)
        em.wait("D", "D")
        em.op("D", "reciprocal", q_c1[:, 3:4], q_c1[:, 2:3])
        em.wait("D", "D")
        em.op("D", "tensor_mul", q_c1[:, 4:5], q_c1[:, 3:4], q_c1[:, 0:1])
        em.op("D", "tensor_scalar_mul", q_c1[:, 5:6], q_c1[:, 2:3], 1.0 / 127.0)
        em.wait("D", "D")
        em.op("D", "tensor_scalar_mul", q_c1[:, 4:5], q_c1[:, 4:5], 127.0)
        em.wait("D", "D")
        em.op("D", "tensor_scalar", lad_u, lad_u, q_c1[:, 4:5], 128.5,
              AL.mult, AL.add)
        em.wait("D", "D")
        em.op("D", "tensor_copy", lad_t, lad_u)
        for k in (7, 6, 5, 4, 3, 2, 1, 0):
            em.wait("D", "D")
            em.op("D", "tensor_scalar", lad_m, lad_t, float(2 ** k),
                  float(-(2 ** k)), AL.is_ge, AL.mult)
            em.wait("D", "D")
            em.op("D", "tensor_add", lad_t, lad_t, lad_m)
        em.wait("D", "D")
        em.op("D", "tensor_sub", lad_u, lad_u, lad_t)
        em.wait("D", "D")
        em.op("D", "tensor_scalar_add", xq_t, lad_u, -128.0)
        em.wait_dma("A", "A")
        em.wait("A", "D")
        em.wait_dma("S", "S")
        em.wait("S", "D")
        for fc in range(8):
            em.dma("A" if fc % 2 == 0 else "S", xq_f[:, fc, :],
                   xq_t[:, 128 * fc:128 * fc + 128], transpose=True)
        em.wait_dma("T", "A")
        em.wait_dma("T", "S")
        for half in range(2):
            for kc in range(8):
                em.op("T", "matmul", PU1[:, 0:512], xq_f[:, kc, :],
                      w_bigB[:, kc, 512 * half:512 * half + 512],
                      pgkey=("PU1", 0))
            em.mm_close(("PU1", 0))
            em.wait("D", "T")
            sl = slice(512 * half, 512 * half + 512)
            em.op("D", "tensor_scalar", lad_m[:, sl], PU1[:, 0:512],
                  q_c1[:, 5:6], None, AL.mult)
            em.wait("T", "D")
        em.wait("D", "D")
        em.op("D", "tensor_mul", lad_m, lad_m, bc_ws)
        em.op("D", "tensor_add", lad_t, x2_t, bc_fb)
        em.wait("D", "D")
        em.wait_dma("D", "G")
        em.op("D", "tensor_add", o_fin, lad_m, lad_t)
        em.wait("G", "D")
        em.wait_dma("G", "G")
        em.dma("G", out[c * L:(c + 1) * L, :], o_fin)

    if do_passb and do_scan and stage >= 4:
        for c in range(nchunks):
            passB_chunk(c)
        em.barrier()
    else:
        em.op("D", "memset", o_fin, 0.0)
        em.wait("G", "D")
        for i in range(16):
            em.dma("G", out[128 * i:128 * i + 128, :], o_fin)
    em.wait_dma("G", "G")

    for name in debug_dump:
        ap_ = SB[name]
        o = nc.dram_tensor("dbg_" + name, list(ap_.shape), ap_.dtype,
                           kind="ExternalOutput").ap()
        em.dma("G", o, ap_)
    em.wait_dma("G", "G")
    em.play()
    return nc


def _host_prep(inputs):
    x = np.asarray(inputs["L0_attn_x"])
    g = {}
    g["wrT"] = np.ascontiguousarray(np.asarray(inputs["L0_attn_Wr"]).T)
    g["wkT"] = np.ascontiguousarray(np.asarray(inputs["L0_attn_Wk"]).T)
    g["wvT"] = np.ascontiguousarray(np.asarray(inputs["L0_attn_Wv"]).T)
    g["woT"] = np.ascontiguousarray(np.asarray(inputs["L0_attn_Wo"]).T)
    g["wqT"] = np.ascontiguousarray(np.asarray(inputs["L0_ffn_weight_q"]).T)
    g["maa_w1T"] = np.ascontiguousarray(np.asarray(inputs["L0_attn_maa_w1"]).T)
    g["maa_w2"] = np.ascontiguousarray(
        np.asarray(inputs["L0_attn_maa_w2"]).reshape(128, 1024))
    g["gate_w1T"] = np.ascontiguousarray(np.asarray(inputs["L0_attn_gate_w1"]).T)
    g["gate_w2"] = np.ascontiguousarray(np.asarray(inputs["L0_attn_gate_w2"]))
    g["decay_w1T"] = np.ascontiguousarray(np.asarray(inputs["L0_attn_decay_w1"]).T)
    g["decay_w2"] = np.ascontiguousarray(np.asarray(inputs["L0_attn_decay_w2"]))
    s1 = np.zeros((1024, 128), np.float32)
    s2 = np.zeros((128, 1024), np.float32)
    for i, n in enumerate(("aaa", "ma", "mk", "kkk")):
        s1[:, 32 * i:32 * i + 16] = np.asarray(inputs["L0_attn_%s_w1" % n]).T
        s2[32 * i:32 * i + 16, :] = np.asarray(inputs["L0_attn_%s_w2" % n])
    g["s_w1p"], g["s_w2p"] = s1, s2
    vr = np.zeros((16, 1024), np.float32)
    vr[0] = np.asarray(inputs["L0_attn_time_maa_x"])
    tm = np.asarray(inputs["L0_attn_time_maa"]).reshape(4, 1024)
    for gg in range(4):
        vr[1 + gg] = tm[gg]
    vr[5] = np.asarray(inputs["L0_attn_time_decay"])
    vr[6] = np.asarray(inputs["L0_attn_time_aaaaa"])
    vr[7] = np.asarray(inputs["L0_attn_time_misc_a"])
    vr[8] = np.asarray(inputs["L0_attn_time_misc_k"])
    vr[9] = np.asarray(inputs["L0_attn_ln_w"])
    vr[10] = np.asarray(inputs["L0_attn_ln_b"])
    vr[11] = np.asarray(inputs["L0_ffn_weight_scale"])
    vr[12] = np.asarray(inputs["L0_ffn_bias"])
    g["vrows"] = vr
    vfm = np.zeros((128, 128), np.float32)
    for i in range(16):
        for fc in range(8):
            vfm[:, fc * 16 + i] = vr[i, fc * 128:fc * 128 + 128]
    g["vfmi"] = vfm
    fa = np.asarray(inputs["L0_attn_time_faaaa"])
    fb = np.zeros((1024, 16), np.float32)
    hb = np.zeros((1024, 16), np.float32)
    idx = np.arange(1024)
    fb[idx, idx // 64] = fa
    hb[idx, idx // 64] = 1.0
    g["fblk"], g["hblk"] = fb, hb
    hs = np.zeros((16, 1024), np.float32)
    hs[idx // 64, idx] = 1.0
    g["hsel"] = hs
    g["ident"] = np.eye(128, dtype=np.float32)
    g["idlo"] = np.diag(np.concatenate([np.ones(64), np.zeros(64)])
                        ).astype(np.float32)
    g["idhi"] = np.diag(np.concatenate([np.zeros(64), np.ones(64)])
                        ).astype(np.float32)
    g["iddup"] = np.concatenate([np.eye(64), np.eye(64)]).astype(np.float32)
    idb = np.zeros((128, 32), np.float32)
    for b in range(4):
        idb[32 * b:32 * b + 32] = np.eye(32)
    g["idblk"] = idb
    sg, tg = np.meshgrid(np.arange(128), np.arange(128), indexing="ij")
    g["mask_s"] = (sg < tg).astype(np.float32)
    g["mask_i"] = (sg <= tg).astype(np.float32)
    g["onesc"] = np.ones((128, 128), np.float32)

    in_maps = []
    for c in range(8):
        b, q = c // 4, c % 4
        t0 = q * NTOK
        m = dict(g)
        xt = np.ascontiguousarray(x[b].T)
        xc = np.zeros((1024, NTOK + 1), np.float32)
        xc[:, 1:] = xt[:, t0:t0 + NTOK]
        if t0 > 0:
            xc[:, 0] = xt[:, t0 - 1]
        m["xcf"] = xc
        m["xtok"] = np.ascontiguousarray(x[b, t0:t0 + NTOK])
        s = np.zeros((128, 8), np.float32)
        s[:, :q] = 1.0
        s[:, 4:8] = 1.0
        s[:, 4:4 + q] = 0.0
        m["selq"] = s
        in_maps.append(m)
    return in_maps


def kernel(**inputs):
    in_maps = _host_prep(inputs)
    nc = build_nc()
    res = run_bass_kernel_spmd(nc, in_maps, core_ids=list(range(8)))
    x = np.asarray(inputs["L0_attn_x"])
    out = np.zeros_like(x, dtype=np.float32)
    for c in range(8):
        b, q = c // 4, c % 4
        out[b, q * NTOK:(q + 1) * NTOK] = res.results[c]["out"]
    return out


# version 04
# speedup vs baseline: 1.0511x; 1.0015x over previous
import numpy as np
import concourse.bass as bass
import concourse.mybir as mybir
from concourse.bass_utils import run_bass_kernel_spmd

F32 = mybir.dt.float32
BF16 = mybir.dt.bfloat16
I8 = mybir.dt.int8
AL = mybir.AluOpType
AF = mybir.ActivationFunctionType
AX = mybir.AxisListType

C0 = float(np.exp(-0.5))
H, Dh, L, NCH, NTOK, NFC = 16, 64, 128, 16, 2048, 8


class Em:
    ENGS = ("T", "D", "A", "G", "S")

    def __init__(self, nc):
        self.nc = nc
        self.ops = {e: [] for e in self.ENGS}
        self.cnt = {e: 0 for e in self.ENGS}
        self.dcnt = {q: 0 for q in ("S", "G", "A")}
        self.lastw = {e: {} for e in self.ENGS}
        self.pg = {}
        self.sems = {}
        self._ctx = []
        self.drn = 0
        self.mm_sig = None
        self.mm_idx = -1
        self.t_ok = True
        self.cover = {}

    def make_sems(self):
        for name in ("oT", "oD", "oA", "oG", "oS", "qS", "qG", "qA", "cc",
                     "dr"):
            c = self.nc.semaphore(name)
            self.sems[name] = c.__enter__()
            self._ctx.append(c)

    def drain(self, via="S"):
        self.wait(via, "T")
        self.drn += 1
        sem = self.sems["dr"]

        def fn(e, s=sem):
            e.sem_inc(s, 1)
        self.ops[via].append(fn)
        self._wait("T", "dr", sem, self.drn)
        self.t_ok = True

    def op(self, eng, method, *args, **kwargs):
        if "pgkey" in kwargs:
            key = kwargs.pop("pgkey")
            kwargs["start"] = key not in self.pg
            kwargs["stop"] = False
            kwargs["skip_group_check"] = True
            self.pg[key] = kwargs
        if eng == "T" and method == "matmul":
            K = args[1].shape[0]
            sig = 32 if K <= 32 else 64 if K <= 64 else 128
            if self.mm_sig is not None and sig != self.mm_sig \
                    and not self.t_ok:
                self.drain()
            self.mm_sig = sig
        self.cnt[eng] += 1
        if eng == "T" and method == "matmul":
            self.mm_idx = self.cnt["T"]
            self.t_ok = False
        sem = self.sems["o" + eng]

        def fn(e, m=method, a=args, k=kwargs, s=sem):
            getattr(e, m)(*a, **k).then_inc(s, 1)
        self.ops[eng].append(fn)

    def mm_close(self, *keys):
        for k in keys:
            if k in self.pg:
                self.pg.pop(k)["stop"] = True

    def mm_close_all(self):
        self.mm_close(*list(self.pg.keys()))

    def dma(self, q, out, in_, transpose=False, slow=False):
        self.dcnt[q] += 16
        sem = self.sems["q" + q]

        def fn(e, o=out, i=in_, t=transpose, s=sem, sl=slow):
            if t:
                e.dma_start_transpose(o, i).then_inc(s, 16)
            elif sl:
                e.dma_start(out=o, in_=i,
                            allow_slow_non_contiguous=True).then_inc(s, 16)
            else:
                e.dma_start(out=o, in_=i).then_inc(s, 16)
        self.ops[q].append(fn)

    def collective(self, ins, outs, groups):
        self.nc.has_collectives = True
        sem = self.sems["cc"]

        def fn(e, i=ins, o=outs, g=groups, s=sem):
            e.collective_compute("AllGather", AL.bypass, replica_groups=g,
                                 ins=i, outs=o).then_inc(s, 1)
        self.ops["G"].append(fn)

        def fn2(e, s=sem):
            e.wait_ge(s, 1)
        self.ops["G"].append(fn2)

    def _wait(self, eng, key, sem, v):
        if v <= 0 or self.lastw[eng].get(key, 0) >= v:
            return
        self.lastw[eng][key] = v
        if eng != "T" and key == "oT":
            self.cover[eng] = (self.cnt[eng], v)
        elif eng == "T" and key != "oT" and key.startswith("o"):
            c = self.cover.get(key[1:])
            if c and v > c[0] and c[1] >= self.mm_idx:
                self.t_ok = True

        def fn(e, s=sem, vv=v):
            e.wait_ge(s, vv)
        self.ops[eng].append(fn)

    def wait(self, eng, on, val=None):
        v = self.cnt[on] if val is None else val
        self._wait(eng, "o" + on, self.sems["o" + on], v)

    def wait_dma(self, eng, q, val=None):
        v = self.dcnt[q] if val is None else val
        self._wait(eng, "q" + q, self.sems["q" + q], v)

    def barrier(self, engs=None, dmas=None):
        engs = engs or self.ENGS
        dmas = list(self.dcnt.keys()) if dmas is None else dmas
        snap = dict(self.cnt)
        dsnap = dict(self.dcnt)
        for e in engs:
            for o in engs:
                if o != e:
                    self.wait(e, o, snap[o])
            for q in dmas:
                self.wait_dma(e, q, dsnap[q])

    def play(self):
        with self.nc.Block() as block:
            for eng, attr in (("T", "tensor"), ("D", "vector"), ("A", "scalar"),
                              ("G", "gpsimd"), ("S", "sync")):
                ops = self.ops[eng]

                def mk(lst):
                    def body(e):
                        for f in lst:
                            f(e)
                    return body
                getattr(block, attr)(mk(ops))
        for c in reversed(self._ctx):
            c.__exit__(None, None, None)


def build_nc(nchunks=NCH, do_scan=True, do_passb=True, no_cc=False, stage=99, sub=99, debug_dump=()):
    nc = bass.Bass("TRN2", target_bir_lowering=False, debug=False)
    em = Em(nc)

    def din(name, shape, dt=F32):
        return nc.dram_tensor(name, shape, dt, kind="ExternalInput").ap()

    xcf = din("xcf", [1024, NTOK + 1])
    xtok = din("xtok", [NTOK, 1024])
    selq = din("selq", [128, 8])
    wrT = din("wrT", [1024, 1024]); wkT = din("wkT", [1024, 1024])
    wvT = din("wvT", [1024, 1024]); woT = din("woT", [1024, 1024])
    wqT = din("wqT", [1024, 1024], I8)
    maa_w1T = din("maa_w1T", [1024, 128])
    maa_w2 = din("maa_w2", [128, 1024])
    gate_w1T = din("gate_w1T", [1024, 128])
    gate_w2 = din("gate_w2", [128, 1024])
    decay_w1T = din("decay_w1T", [1024, 64])
    decay_w2 = din("decay_w2", [64, 1024])
    s_w1p = din("s_w1p", [1024, 128])
    s_w2p = din("s_w2p", [128, 1024])
    vrows = din("vrows", [16, 1024])
    vfmi = din("vfmi", [128, 128])
    fblk = din("fblk", [1024, 16])
    hblk = din("hblk", [1024, 16])
    hsel = din("hsel", [16, 1024])
    ident = din("ident", [128, 128])
    idlo = din("idlo", [128, 128])
    idhi = din("idhi", [128, 128])
    iddup = din("iddup", [128, 64])
    idblk = din("idblk", [128, 32])
    mask_s = din("mask_s", [128, 128])
    mask_i = din("mask_i", [128, 128])
    onesc = din("onesc", [128, 128])

    out = nc.dram_tensor("out", [NTOK, 1024], F32, kind="ExternalOutput").ap()

    def scr(name, shape, dt=BF16):
        return nc.dram_tensor(name, shape, dt, kind="Internal").ap()
    s_yloc = scr("s_yloc", [NCH, L, 1024])
    s_zf = scr("s_zf", [NCH, 128, 8, 128])
    s_uf = scr("s_uf", [NCH, 128, 8, 128])
    s_bck = scr("s_bck", [NCH, L, 1024])
    s_p1 = scr("s_p1", [NCH, L, 1024])
    s_p2 = scr("s_p2", [NCH, L, 1024])
    cc_in = nc.dram_tensor("cc_in", [2048, 64], F32, kind="Internal").ap()
    cc_out = nc.dram_tensor("cc_out", [8192, 64], F32, kind="Internal").ap()

    SB = {}

    def sb(name, shape, dt):
        SB[name] = nc.alloc_sbuf_tensor(name, shape, dt).ap()
        return SB[name]

    def alias(name, base, shape, off=0):
        ap_ = SB[base]
        if ap_.ndim == 3:
            flat = ap_.rearrange("p a b -> p (a b)")
        else:
            flat = ap_
        n = 1
        for s in shape[1:]:
            n *= s
        v = flat[:shape[0], off:off + n]
        if len(shape) == 3:
            v = v.rearrange("p (a b) -> p a b", b=shape[2])
        SB[name] = v
        return v

    w_bigA = sb("w_bigA", [128, 8, 1024], BF16)
    w_bigB = sb("w_bigB", [128, 8, 1024], BF16)
    w_bigC = sb("w_bigC", [128, 8, 1024], BF16)
    w_maa1 = sb("w_maa1", [128, 8, 128], BF16)
    w_maa2 = sb("w_maa2", [128, 1024], BF16)
    w_g1 = sb("w_g1", [128, 8, 128], BF16)
    w_g2 = sb("w_g2", [128, 1024], BF16)
    w_d1 = sb("w_d1", [128, 8, 64], BF16)
    w_d2 = sb("w_d2", [64, 1024], BF16)
    w_s1 = sb("w_s1", [128, 8, 128], BF16)
    w_s2 = sb("w_s2", [128, 1024], BF16)
    w_fblk = sb("w_fblk", [128, 8, 16], BF16)
    w_hblk = sb("w_hblk", [128, 8, 16], BF16)
    c_hsel = sb("c_hsel", [16, 8, 128], BF16)
    c_id = sb("c_id", [128, 128], BF16)
    c_id32 = sb("c_id32", [128, 128], F32)
    c_idblk = sb("c_idblk", [128, 32], BF16)
    c_ms = sb("c_ms", [128, 128], BF16)
    c_mi = sb("c_mi", [128, 128], BF16)
    c_one = sb("c_one", [128, 128], BF16)
    c_idlo = sb("c_idlo", [128, 128], BF16)
    c_idhi = sb("c_idhi", [128, 128], BF16)
    c_dup = sb("c_dup", [128, 64], BF16)
    c_trid = sb("c_trid", [128, 8, 64], BF16)
    v_fm = sb("v_fm", [128, 8, 16], F32)
    v_neg = sb("v_neg", [128, 8, 4], F32)
    bc_lnw = sb("bc_lnw", [128, 1024], BF16)
    bc_lnb = sb("bc_lnb", [128, 1024], BF16)
    bc_ws = sb("bc_ws", [128, 1024], BF16)
    bc_fb = sb("bc_fb", [128, 1024], BF16)
    sel = sb("sel", [128, 8], F32)
    cb = sb("cb", [128, 4], F32)

    VIDX = {n: i for i, n in enumerate(
        ("tmx", "tmaa0", "tmaa1", "tmaa2", "tmaa3", "td", "taaaaa", "tma",
         "tmk", "lnw", "lnb", "ws", "fbias"))}
    NEGI = {"td": 0, "taaaaa": 1, "tma": 2, "tmk": 3}

    bd_pn = sb("bd_pn", [128, 16, 132], BF16)
    xlc = alias("xlc", "bd_pn", [128, 8, 132])
    xsq = alias("xsq", "bd_pn", [128, 8, 132], off=1056)
    sc_m = sb("sc_m", [128, 132], BF16)
    sc_r = sb("sc_r", [128, 132], BF16)
    sc_f1 = sb("sc_f1", [128, 132], F32)
    sc_f2 = sb("sc_f2", [128, 132], F32)
    u_tk = sb("u_tk", [128, 1024], BF16)
    f_xx = sb("f_xx", [128, 8, L], BF16)
    f_mix = sb("f_mix", [128, 8, L], BF16)
    f_xm = alias("f_xm", "f_mix", [128, 8, L])
    f_r = sb("f_r", [128, 8, L], BF16)
    f_k0 = sb("f_k0", [128, 8, L], BF16)
    f_kk = sb("f_kk", [128, 8, L], BF16)
    f_kkn = sb("f_kkn", [128, 8, L], BF16)
    f_k = sb("f_k", [128, 8, L], BF16)
    f_a = sb("f_a", [128, 8, L], BF16)
    f_ap = sb("f_ap", [128, 8, L], BF16)
    f_ma = sb("f_ma", [128, 8, L], BF16)
    f_mk = sb("f_mk", [128, 8, L], BF16)
    f_sig = sb("f_sig", [128, 8, L], F32)
    f_t1 = sb("f_t1", [128, 8, L], F32)
    f_t2 = sb("f_t2", [128, 8, L], F32)
    f_t3 = sb("f_t3", [128, 8, L], F32)
    f_rkp = sb("f_rkp", [128, 8, L], BF16)
    f_qd = alias("f_qd", "f_rkp", [128, 8, L])
    f_lo = sb("f_lo", [128, L], BF16)
    f_loP = sb("f_loP", [128, 4, L], BF16)
    f_shP = sb("f_shP", [128, 4, L], BF16)
    f_shB = sb("f_shB", [64, L], BF16)
    f_dh = sb("f_dh", [64, L], BF16)
    f_sh = sb("f_sh", [128, L], BF16)
    f_gh = sb("f_gh", [128, L], BF16)
    rn_h = sb("rn_h", [16, L], F32)
    g_buf = sb("g_buf", [128, 8, L + 1], F32)
    eg = sb("eg", [128, 8, L + 1], BF16)
    ek = sb("ek", [128, 8, L], BF16)
    f_kh = sb("f_kh", [128, 8, L], BF16)
    f_bc = sb("f_bc", [128, 8, L], BF16)
    f_kc = sb("f_kc", [128, 8, L], BF16)
    cl_all = sb("cl_all", [128, 8, NCH], F32)
    t_v = sb("t_v", [128, 1024], BF16)
    t_g = sb("t_g", [128, 1024], F32)
    t_b1 = sb("t_b1", [128, 1024], F32)
    t_p = sb("t_p", [128, 1024], BF16)
    t_p2 = sb("t_p2", [128, 1024], BF16)
    t_rk = sb("t_rk", [16, 128], BF16)
    t_rkT = sb("t_rkT", [128, 16], BF16)
    a_ss = sb("a_ss", [128, 16, 128], BF16)
    a_k = sb("a_k", [128, 16, 128], BF16)
    a_q = sb("a_q", [128, 16, 128], BF16)
    a_qk = sb("a_qk", [128, 16, 128], BF16)
    dg_nt = sb("dg_nt", [128, 16, 32], BF16)
    dg_n = sb("dg_n", [128, 16, 32], BF16)
    dg_pt = sb("dg_pt", [128, 16, 32], BF16)
    dg_pn = sb("dg_pn", [128, 16, 32], BF16)
    dg_tt = sb("dg_tt", [128, 16, 32], BF16)
    x_ri = sb("x_ri", [128, 8, 128], BF16)
    x_sol = sb("x_sol", [128, 8, 128], BF16)
    y_lc = sb("y_lc", [128, 1024], BF16)
    z_f = sb("z_f", [128, 8, 128], BF16)
    u_f = sb("u_f", [128, 8, 128], BF16)
    bck_t = sb("bck_t", [128, 1024], BF16)
    kck_t = sb("kck_t", [128, 1024], BF16)
    s_st = sb("s_st", [128, 8, 64], F32)
    s_sb = sb("s_sb", [128, 8, 64], BF16)
    tr_st = sb("tr_st", [128, 8, 64], F32)
    tr_sb = sb("tr_sb", [128, 8, 64], BF16)
    st_t1 = sb("st_t1", [128, 8, 64], F32)
    w_sb = sb("w_sb", [128, 1024], BF16)
    sict = sb("sict", [128, 8, 64], F32)
    sict_b = sb("sict_b", [128, 8, 64], BF16)
    trp = alias("trp", "dg_tt", [128, 8, 64])
    sp_f = alias("sp_f", "dg_nt", [128, 8, 64])
    gat_s = alias("gat_s", "dg_pt", [128, 8, 64])
    gat_t = alias("gat_t", "dg_pn", [128, 8, 64])
    gn_st = alias("gn_st", "sc_f1", [128, 80])
    bn6 = alias("bn6", "sc_f2", [128, 6])
    bn2c = SB["sc_f2"][:, 8:10]
    SB["bn2"] = bn2c
    bn2 = bn2c
    q_c1 = SB["sc_f2"][:, 16:24]
    SB["q_c1"] = q_c1
    y_fin = alias("y_fin", "f_t1", [128, 1024])
    x2_t = alias("x2_t", "f_t2", [128, 1024])
    xt_in = alias("xt_in", "f_t3", [128, 1024])
    lad_u = alias("lad_u", "f_sig", [128, 1024])
    lad_t = alias("lad_t", "t_b1", [128, 1024])
    lad_m = alias("lad_m", "t_g", [128, 1024])
    o_fin = alias("o_fin", "g_buf", [128, 1024])
    t_ld1 = alias("t_ld1", "f_xx", [128, 1024])
    t_ldp1 = alias("t_ldp1", "f_mix", [128, 1024])
    t_ldp2 = alias("t_ldp2", "f_rkp", [128, 1024])
    y2_t = alias("y2_t", "f_r", [128, 1024])
    y2_f = alias("y2_f", "f_k0", [128, 8, 128])
    xq_t = alias("xq_t", "f_kk", [128, 1024])
    xq_f = alias("xq_f", "f_kkn", [128, 8, 128])
    f_khO = alias("f_khO", "f_xx", [128, 8, 128])
    f_qdO = alias("f_qdO", "f_mix", [128, 8, 128])
    u_fO = alias("u_fO", "ek", [128, 8, 128])
    z_fO = alias("z_fO", "eg", [128, 8, 128])
    x_u = alias("x_u", "f_r", [128, 4, 128])
    x_h = alias("x_h", "f_r", [128, 4, 128], off=512)

    PG0 = nc.alloc_psum_tensor("PG0", [128, 8, 128], F32).ap()
    PG1 = nc.alloc_psum_tensor("PG1", [128, 8, 128], F32).ap()
    PG2 = nc.alloc_psum_tensor("PG2", [128, 8, 128], F32).ap()
    PU0 = nc.alloc_psum_tensor("PU0", [128, 512], F32).ap()
    PU1 = nc.alloc_psum_tensor("PU1", [128, 512], F32).ap()

    em.make_sems()

    def r128(ap_):
        return ap_.rearrange("(kc p) n -> p kc n", p=128)

    for dst, src in ((w_bigA, wrT), (w_bigB, wkT), (w_bigC, wvT),
                     (w_maa1, maa_w1T), (w_g1, gate_w1T),
                     (w_d1, decay_w1T), (w_s1, s_w1p), (w_fblk, fblk),
                     (w_hblk, hblk)):
        em.dma("G", dst, r128(src))
    em.dma("G", w_maa2, maa_w2)
    em.dma("G", w_g2, gate_w2)
    em.dma("G", w_d2, decay_w2)
    em.dma("G", w_s2, s_w2p)
    em.dma("G", c_hsel, hsel.rearrange("h (fc p) -> h fc p", p=128))
    em.dma("G", c_id, ident)
    em.dma("G", c_idblk, idblk)
    em.dma("G", c_ms, mask_s)
    em.dma("G", c_mi, mask_i)
    em.dma("G", c_one, onesc)
    em.dma("G", c_idlo, idlo)
    em.dma("G", c_idhi, idhi)
    em.dma("G", c_dup, iddup)
    em.dma("S", c_id32, ident)
    em.dma("S", sel, selq)
    em.dma("S", v_fm, vfmi.rearrange("p (fc i) -> p fc i", i=16))
    em.barrier()

    if stage < 1:
        nchunks = 0
    for n, j in NEGI.items():
        em.op("D", "tensor_scalar_mul", v_neg[:, :, j:j + 1],
              v_fm[:, :, VIDX[n]:VIDX[n] + 1], -1.0)
    for row_i, bc in ((VIDX["lnw"], bc_lnw), (VIDX["lnb"], bc_lnb),
                      (VIDX["ws"], bc_ws), (VIDX["fbias"], bc_fb)):
        for fc in range(8):
            em.op("D", "tensor_scalar_mul", xlc[:, fc, 0:128],
                  c_one, v_fm[:, fc, row_i:row_i + 1])
        em.wait_dma("A", "A")
        em.wait("A", "D")
        for fc in range(8):
            em.dma("A", bc[:, 128 * fc:128 * fc + 128], xlc[:, fc, 0:128],
                   transpose=True)
        em.wait_dma("D", "A")
    em.op("D", "memset", cb[:, 0:1], 1e-5)
    em.op("D", "memset", xlc, 0.0)
    em.op("D", "memset", cl_all, 0.0)
    em.op("D", "memset", x_ri, 0.0)
    em.op("D", "memset", xsq, 0.0)
    em.op("D", "memset", sc_m, 0.0)
    em.op("D", "memset", sc_r, 0.0)
    em.op("D", "memset", sc_f1, 0.0)
    em.op("D", "memset", sc_f2, 0.0)
    em.op("D", "memset", cb[:, 1:2], 1e-24)
    em.op("D", "memset", cb[:, 2:3], 0.00064)
    em.op("D", "memset", s_st, 0.0)
    em.op("D", "memset", s_sb, 0.0)
    em.op("D", "memset", sict, 0.0)
    em.op("D", "memset", sict_b, 0.0)
    em.op("D", "memset", g_buf[:, :, 0:1], 0.0)
    for pr in range(8):
        em.op("D", "tensor_copy", tr_st[0:64, pr, :], c_id32[0:64, 0:64])
        em.op("D", "tensor_copy", tr_st[64:128, pr, :], c_id32[64:128, 64:128])
        em.op("D", "tensor_copy", tr_sb[0:64, pr, :], c_id[0:64, 0:64])
        em.op("D", "tensor_copy", tr_sb[64:128, pr, :], c_id[64:128, 64:128])
        em.op("D", "tensor_copy", c_trid[0:64, pr, :], c_id[0:64, 0:64])
        em.op("D", "tensor_copy", c_trid[64:128, pr, :], c_id[64:128, 64:128])
    em.barrier()

    def head_rows(h):
        return 64 * (h % 2), h // 2

    if stage < 2:
        nchunks = 0

    def _passA_chunk(c):
        t0 = c * L
        em.wait_dma("G", "G")
        em.wait("G", "D")
        em.wait("G", "T")
        em.dma("G", xlc[:, :, 0:129],
               xcf.rearrange("(fc p) t -> p fc t", p=128)[:, :, t0:t0 + 129])
        em.wait_dma("D", "G")
        em.op("D", "tensor_mul", xsq[:, :, 0:129], xlc[:, :, 0:129],
              xlc[:, :, 0:129])
        em.barrier(("T", "D"))
        for fc in range(8):
            em.op("T", "matmul", PU0[:, 0:129], c_one, xlc[:, fc, 0:129],
                  pgkey=("PU0", 0))
        for fc in range(8):
            em.op("T", "matmul", PU0[:, 256:385], c_one, xsq[:, fc, 0:129],
                  pgkey=("PU0", 0))
        em.mm_close(("PU0", 0))
        em.wait("D", "T")
        em.op("D", "tensor_scalar_mul", sc_f1[:, 0:129], PU0[:, 0:129],
              1.0 / 1024.0)
        em.op("D", "tensor_scalar_mul", sc_f2[:, 0:129], PU0[:, 256:385],
              1.0 / 1024.0)
        em.wait("D", "D")
        em.op("D", "tensor_copy", sc_m[:, 0:129], sc_f1[:, 0:129])
        em.op("D", "tensor_mul", sc_f1[:, 0:129], sc_f1[:, 0:129],
              sc_f1[:, 0:129])
        em.wait("D", "D")
        em.op("D", "tensor_sub", sc_f2[:, 0:129], sc_f2[:, 0:129],
              sc_f1[:, 0:129])
        em.wait("A", "D")
        em.op("A", "activation", sc_f1[:, 0:129], sc_f2[:, 0:129], AF.Ln,
              bias=cb[:, 0:1], scale=1.0)
        em.wait("A", "A")
        em.op("A", "activation", sc_f2[:, 0:129], sc_f1[:, 0:129], AF.Exp,
              scale=-0.5)
        em.wait("D", "A")
        em.op("D", "tensor_copy", sc_r[:, 0:129], sc_f2[:, 0:129])
        em.wait("D", "D")
        em.op("D", "tensor_sub", xlc[:, :, 0:129], xlc[:, :, 0:129],
              sc_m[:, 0:129].unsqueeze(1)
              .broadcast_to([128, 8, 129]))
        em.wait("D", "D")
        em.op("D", "tensor_mul", xlc[:, :, 0:129], xlc[:, :, 0:129],
              sc_r[:, 0:129].unsqueeze(1)
              .broadcast_to([128, 8, 129]))
        em.wait("T", "D")
        if sub <= 1:
            em.barrier()
            return
        xl = xlc[:, :, 1:129]
        xls = xlc[:, :, 0:128]
        em.wait("D", "D")
        for fc in range(8):
            em.op("D", "tensor_sub", f_xx[:, fc, :], xls[:, fc, :],
                  xl[:, fc, :])
        em.wait("D", "D")
        for fc in range(8):
            em.op("D", "scalar_tensor_tensor", f_mix[:, fc, :], f_xx[:, fc, :],
                  v_fm[:, fc, VIDX["tmx"]:VIDX["tmx"] + 1], xl[:, fc, :],
                  AL.mult, AL.add)
        if sub == 10:
            em.barrier()
            return
        em.barrier(("T", "D"))
        for kc in range(8):
            em.op("T", "matmul", PU0[:, 0:128], w_maa1[:, kc, :],
                  f_mix[:, kc, :], pgkey=("PU0", 0))
        em.mm_close(("PU0", 0))
        em.wait("A", "T")
        em.op("A", "activation", f_lo, PU0[:, 0:128], AF.Tanh)
        em.wait("D", "A")
        em.op("D", "memset", f_loP, 0.0)
        em.wait("D", "D")
        for gg in range(4):
            em.op("D", "tensor_copy", f_loP[32 * gg:32 * gg + 32, gg, :],
                  f_lo[32 * gg:32 * gg + 32, :])
        em.barrier(("T", "A", "D"))
        if sub == 11:
            em.barrier()
            return

        for g in range(min(4, sub - 11 if 11 < sub < 16 else 4)):
            vi = VIDX["tmaa%d" % g]
            for m in range(8):
                em.op("T", "matmul", PG0[:, m, :],
                      w_maa2[:, 128 * m:128 * m + 128],
                      f_loP[:, g, :], pgkey=("PG0", m // 4))
            em.mm_close(("PG0", 0), ("PG0", 1))
            em.wait("D", "T")
            for fc in range(8):
                em.op("D", "tensor_mul", f_t1[:, fc, :], f_xx[:, fc, :],
                      PG0[:, fc, :])
            for fc in range(8):
                em.op("D", "scalar_tensor_tensor", f_t2[:, fc, :],
                      f_xx[:, fc, :], v_fm[:, fc, vi:vi + 1], xl[:, fc, :],
                      AL.mult, AL.add)
            em.wait("D", "D")
            for fc in range(8):
                em.op("D", "tensor_add", f_xm[:, fc, :], f_t1[:, fc, :],
                      f_t2[:, fc, :])
            em.barrier(("T", "D"))
            if sub == 16 and g == 1:
                em.barrier()
                return
            if g == 0:
                for m in range(8):
                    for kc in range(8):
                        em.op("T", "matmul", PG1[:, m, :],
                              w_bigA[:, kc, 128 * m:128 * m + 128],
                              f_xm[:, kc, :], pgkey=("PG1", m // 4))
                for kc in range(8):
                    em.op("T", "matmul", PU0[:, 0:128], w_g1[:, kc, :],
                          f_xm[:, kc, :], pgkey=("PU0", 0))
                em.mm_close(("PG1", 0), ("PG1", 1), ("PU0", 0))
                em.wait("D", "T")
                em.op("D", "tensor_copy", f_r, PG1)
                em.wait("A", "T")
                em.op("A", "activation", f_gh, PU0[:, 0:128], AF.Tanh)
                em.barrier(("T", "D", "A"))
                for half in range(2):
                    em.op("T", "matmul", PU1[:, 0:512], f_gh,
                          w_g2[:, 512 * half:512 * half + 512],
                          pgkey=("PU1", 0))
                    em.mm_close(("PU1", 0))
                    em.wait("A", "T")
                    em.op("A", "activation",
                          t_g[:, 512 * half:512 * half + 512], PU1[:, 0:512],
                          AF.Copy)
                    em.wait("T", "A")
            elif g == 1:
                for kc in range(8):
                    em.op("T", "matmul", PU0[0:64, 0:128], w_d1[:, kc, :],
                          f_xm[:, kc, :], pgkey=("PU0", 0))
                for kc in range(8):
                    em.op("T", "matmul", PU0[0:64, 128:256], w_s1[:, kc, 0:64],
                          f_xm[:, kc, :], pgkey=("PU0", 0))
                em.mm_close(("PU0", 0))
                em.wait("A", "T")
                em.op("A", "activation", f_dh, PU0[0:64, 0:128], AF.Tanh)
                em.wait("D", "T")
                em.op("D", "memset", f_shP, 0.0)
                em.wait("D", "D")
                em.op("D", "tensor_copy", f_shP[0:32, 0, :],
                      PU0[0:32, 128:256])
                em.op("D", "tensor_copy", f_shP[32:64, 1, :],
                      PU0[32:64, 128:256])
                em.barrier(("T", "D", "A"))
                for m in range(8):
                    em.op("T", "matmul", PG1[:, m, :],
                          w_d2[:, 128 * m:128 * m + 128], f_dh,
                          pgkey=("PG1", m // 4))
                em.mm_close(("PG1", 0), ("PG1", 1))
                for m in range(8):
                    em.op("T", "matmul", PG2[:, m, :],
                          w_s2[:, 128 * m:128 * m + 128], f_shP[:, 0, :],
                          pgkey=("PG2", m // 4))
                for m in range(8):
                    em.op("T", "matmul", PG0[:, m, :],
                          w_s2[:, 128 * m:128 * m + 128], f_shP[:, 1, :],
                          pgkey=("PG0", m // 4))
                em.mm_close_all()
                em.wait("D", "T")
                em.op("D", "tensor_tensor", f_t1, PG1,
                      v_fm[:, :, VIDX["td"]:VIDX["td"] + 1]
                      .broadcast_to([128, 8, L]), AL.add)
                em.op("D", "tensor_tensor", f_t2, PG2,
                      v_fm[:, :, VIDX["taaaaa"]:VIDX["taaaaa"] + 1]
                      .broadcast_to([128, 8, L]), AL.add)
                em.op("D", "tensor_tensor", f_t3, PG0,
                      v_fm[:, :, VIDX["tma"]:VIDX["tma"] + 1]
                      .broadcast_to([128, 8, L]), AL.add)
                em.wait("A", "D")
                em.op("A", "activation", f_sig, f_t1, AF.Sigmoid)
                em.op("A", "activation", f_a, f_t2, AF.Sigmoid)
                em.op("A", "activation", f_ap, f_t2, AF.Sigmoid, scale=-1.0)
                em.op("A", "activation", f_ma, f_t3, AF.Sigmoid)
                em.barrier(("T", "D", "A"))
            elif g == 2:
                for m in range(8):
                    for kc in range(8):
                        em.op("T", "matmul", PG1[:, m, :],
                              w_bigB[:, kc, 128 * m:128 * m + 128],
                              f_xm[:, kc, :], pgkey=("PG1", m // 4))
                em.mm_close(("PG1", 0), ("PG1", 1))
                for kc in range(8):
                    em.op("T", "matmul", PU0[0:64, 256:384],
                          w_s1[:, kc, 64:128], f_xm[:, kc, :],
                          pgkey=("PU0s",))
                em.mm_close(("PU0s",))
                em.wait("A", "T")
                em.op("A", "activation", f_k0, PG1.rearrange("p a b -> p (a b)"),
                      AF.Copy)
                em.wait("D", "T")
                em.op("D", "tensor_copy", f_shB[0:32, :], PU0[0:32, 256:384])
                em.wait("A", "T")
                em.op("A", "activation", f_shB[32:64, :], PU0[32:64, 256:384],
                      AF.Tanh)
                em.wait_dma("G", "G")
                em.wait("G", "D")
                em.wait("G", "A")
                em.dma("G", f_shP[64:96, 2, :], f_shB[0:32, :])
                em.dma("G", f_shP[96:128, 3, :], f_shB[32:64, :])
                em.wait_dma("T", "G")
                em.barrier(("T", "D", "A"))
                for m in range(8):
                    em.op("T", "matmul", PG2[:, m, :],
                          w_s2[:, 128 * m:128 * m + 128], f_shP[:, 2, :],
                          pgkey=("PG2", m // 4))
                for m in range(8):
                    em.op("T", "matmul", PG0[:, m, :],
                          w_s2[:, 128 * m:128 * m + 128], f_shP[:, 3, :],
                          pgkey=("PG0", m // 4))
                em.mm_close_all()
                em.wait("D", "T")
                em.op("D", "tensor_tensor", f_t1, PG2,
                      v_fm[:, :, VIDX["tmk"]:VIDX["tmk"] + 1]
                      .broadcast_to([128, 8, L]), AL.add)
                em.wait("D", "A")
                em.op("D", "tensor_add", f_kk, f_k0, PG0)
                em.wait("A", "D")
                em.op("A", "activation", f_mk, f_t1, AF.Sigmoid)
                em.barrier(("T", "D", "A"))
            else:
                for half in range(2):
                    for kc in range(8):
                        em.op("T", "matmul", PU1[:, 0:512], f_xm[:, kc, :],
                              w_bigC[:, kc, 512 * half:512 * half + 512],
                              pgkey=("PU1", 0))
                    em.mm_close(("PU1", 0))
                    em.wait("A", "T")
                    em.op("A", "activation",
                          t_v[:, 512 * half:512 * half + 512], PU1[:, 0:512],
                          AF.Copy)
                    em.wait("T", "A")
                em.barrier(("T", "D", "A"))

        if 11 < sub < 16:
            em.barrier()
            return
        if sub <= 2:
            em.barrier()
            return
        em.op("D", "tensor_mul", f_rkp, f_kk, f_kk)
        em.barrier(("T", "D"))
        for fc in range(8):
            em.op("T", "matmul", PU0[0:16, 0:128], w_hblk[:, fc, :],
                  f_rkp[:, fc, :], pgkey=("PU0", 0))
        em.mm_close(("PU0", 0))
        em.wait("A", "T")
        em.op("A", "activation", rn_h, PU0[0:16, 0:128], AF.Ln,
              bias=cb[0:16, 1:2], scale=1.0)
        em.wait("A", "A")
        em.op("A", "activation", rn_h, rn_h, AF.Exp, scale=-0.5)
        em.wait("D", "A")
        em.op("D", "tensor_copy", t_rk[:, 0:128], rn_h)
        em.barrier(("T", "D", "A"))
        for fc in range(8):
            em.op("T", "matmul", PG0[:, fc, :], c_hsel[:, fc, :],
                  t_rk[:, 0:128], pgkey=("PG0", fc // 4))
        em.mm_close(("PG0", 0), ("PG0", 1))
        em.wait("D", "T")
        em.op("D", "tensor_mul", f_kkn, f_kk, PG0)
        em.wait("D", "D")
        em.op("D", "tensor_mul", f_t1, f_ma, f_ap)
        em.wait("D", "D")
        em.op("D", "tensor_add", f_ap, f_a, f_t1)
        em.op("A", "activation", f_t2, f_sig, AF.Ln)
        em.wait("D", "A")
        em.op("D", "scalar_tensor_tensor", f_t2, f_t2, -0.5, f_mk,
              AL.add, AL.mult)
        em.wait("A", "D")
        em.op("A", "activation", f_t2, f_t2, AF.Exp)
        em.wait("D", "A")
        em.op("D", "tensor_mul", f_t1, f_k0, f_ap)
        em.wait("D", "D")
        em.op("D", "tensor_mul", f_k, f_t1, f_t2)
        em.wait("D", "D")
        em.op("D", "scalar_tensor_tensor", f_t2, f_kkn, -1.0, f_a,
              AL.mult, AL.mult)
        em.wait("D", "D")
        em.op("D", "tensor_mul", f_rkp, f_r, f_k)
        em.barrier(("T", "D"))
        for fc in range(8):
            em.op("T", "matmul", PU0[0:16, 128:256], w_fblk[:, fc, :],
                  f_rkp[:, fc, :], pgkey=("PU0", 0))
        em.mm_close(("PU0", 0))
        em.wait("D", "T")
        em.op("D", "tensor_copy", t_rk[:, 0:128], PU0[0:16, 128:256])
        em.wait_dma("A", "A")
        em.wait("A", "D")
        em.dma("A", t_rkT, t_rk[:, 0:128], transpose=True)
        em.wait_dma("D", "A")
        em.op("D", "tensor_tensor", t_b1.rearrange("p (h d) -> p h d", d=64),
              t_v.rearrange("p (h d) -> p h d", d=64),
              t_rkT.unsqueeze(2).broadcast_to([128, 16, 64]),
              AL.mult)
        em.wait("D", "D")
        em.op("D", "tensor_add", t_b1, t_b1, bc_lnb)
        em.wait("D", "D")
        em.op("D", "tensor_mul", t_p, t_b1, t_g)
        em.op("D", "tensor_mul", t_p2, t_g, bc_lnw)
        em.wait_dma("G", "G")
        em.wait("G", "D")
        em.dma("G", s_p2[c], t_p)
        em.dma("G", s_p1[c], t_p2)

        if sub <= 3:
            em.barrier()
            return
        if not do_scan:
            em.barrier()
            return

        for fc in range(8):
            em.op("D", "tensor_tensor_scan", g_buf[:, fc, 1:L + 1],
                  f_sig[:, fc, :], f_sig[:, fc, :], 0.0, AL.add, AL.bypass)
        em.wait("A", "D")
        em.op("A", "activation", eg, g_buf, AF.Exp, scale=-C0)
        em.op("A", "activation", ek, g_buf[:, :, 1:L + 1], AF.Exp,
              scale=C0)
        em.op("A", "activation", cl_all[:, :, c:c + 1],
              g_buf[:, :, L:L + 1], AF.Exp, scale=-C0)
        em.wait("D", "A")
        em.op("D", "tensor_mul", f_qd, f_r, eg[:, :, 1:L + 1])
        em.op("D", "tensor_mul", f_kh, f_kkn, eg[:, :, 0:L])
        em.op("D", "tensor_mul", f_bc, f_t2, ek)
        em.op("D", "tensor_mul", f_kc, f_k, ek)
        em.wait("D", "D")
        em.op("D", "memset", f_khO[0:64, :, :], 0.0)
        em.op("D", "memset", f_qdO[0:64, :, :], 0.0)
        em.wait("D", "D")
        em.op("D", "tensor_copy", f_khO[64:128, :, :], f_kh[64:128, :, :])
        em.op("D", "tensor_copy", f_qdO[64:128, :, :], f_qd[64:128, :, :])
        em.wait_dma("A", "A")
        em.wait("A", "D")
        em.wait_dma("S", "S")
        em.wait("S", "D")
        for fc in range(8):
            em.dma("A", bck_t[:, 128 * fc:128 * fc + 128], f_bc[:, fc, :],
                   transpose=True)
            em.dma("S", kck_t[:, 128 * fc:128 * fc + 128], f_kc[:, fc, :],
                   transpose=True)
        em.barrier(("T", "D", "A"))

        if sub <= 4:
            em.barrier()
            return
        for wv in range(2):
            for pair in range(2):
                rf = f_kh if pair == 0 else f_qd
                rfO = f_khO if pair == 0 else f_qdO
                d0 = a_ss if pair == 0 else a_q
                d1 = a_k if pair == 0 else a_qk
                msk = c_ms if pair == 0 else c_mi
                for j in range(4):
                    pr = 4 * wv + j
                    em.op("T", "matmul", PG0[:, j, :],
                          f_bc[0:64, pr, :], rf[0:64, pr, :],
                          pgkey=("S3", 0))
                    em.op("T", "matmul", PG1[:, j, :],
                          f_kc[0:64, pr, :], rf[0:64, pr, :],
                          pgkey=("S3", 1))
                for j in range(4):
                    pr = 4 * wv + j
                    em.op("T", "matmul", PG0[:, 4 + j, :],
                          f_bc[:, pr, :], rfO[:, pr, :], pgkey=("S3", 2))
                    em.op("T", "matmul", PG1[:, 4 + j, :],
                          f_kc[:, pr, :], rfO[:, pr, :], pgkey=("S3", 3))
                em.mm_close_all()
                em.wait("D", "T")
                d0v = d0.rearrange("p (a two) b -> p a two b", two=2)
                d1v = d1.rearrange("p (a two) b -> p a two b", two=2)
                mb = msk.unsqueeze(1).broadcast_to([128, 4, 128])
                em.op("D", "tensor_tensor", d0v[:, 4 * wv:4 * wv + 4, 0, :],
                      PG0[:, 0:4, :], mb, AL.mult)
                em.op("D", "tensor_tensor", d0v[:, 4 * wv:4 * wv + 4, 1, :],
                      PG0[:, 4:8, :], mb, AL.mult)
                em.op("D", "tensor_tensor", d1v[:, 4 * wv:4 * wv + 4, 0, :],
                      PG1[:, 0:4, :], mb, AL.mult)
                em.op("D", "tensor_tensor", d1v[:, 4 * wv:4 * wv + 4, 1, :],
                      PG1[:, 4:8, :], mb, AL.mult)
                em.wait("T", "D")
        if sub <= 5:
            em.barrier()
            return
        for b in range(4):
            em.op("D", "tensor_copy", dg_nt[32 * b:32 * b + 32, :, :],
                  a_ss[32 * b:32 * b + 32, :, 32 * b:32 * b + 32])
        em.wait("D", "D")
        em.op("D", "transpose", dg_n.rearrange("p h t -> p (h t)"),
              dg_nt.rearrange("p h t -> p (h t)"))
        em.wait("D", "D")
        em.op("D", "tensor_tensor", dg_tt, dg_nt,
              c_idblk.unsqueeze(1).broadcast_to([128, 16, 32]),
              AL.add)
        em.op("D", "tensor_copy", dg_pt, dg_nt)
        em.op("D", "tensor_copy", dg_pn, dg_n)
        em.barrier(("T", "D"))
        em.wait("D", "D")
        em.op("D", "memset", bd_pn[:, :, 0:128], 0.0)
        em.wait("D", "D")
        for b in range(4):
            em.op("D", "tensor_copy",
                  bd_pn[32 * b:32 * b + 32, :, 32 * b:32 * b + 32],
                  dg_pn[32 * b:32 * b + 32, :, :])
        em.barrier(("T", "D"))
        PU0h = PU0.rearrange("p (h t) -> p h t", t=32)
        for rnd in range(4):
            for h in range(16):
                em.op("T", "matmul", PU0[:, 32 * h:32 * h + 32],
                      bd_pn[:, h, 0:128], dg_pt[:, h, :], pgkey=("PT",))
            em.mm_close(("PT",))
            em.wait("D", "T")
            em.op("D", "tensor_copy", dg_pt, PU0h)
            em.wait("D", "D")
            em.op("D", "transpose", dg_pn.rearrange("p h t -> p (h t)"),
                  dg_pt.rearrange("p h t -> p (h t)"))
            em.wait("D", "D")
            for b in range(4):
                em.op("D", "tensor_copy",
                      bd_pn[32 * b:32 * b + 32, :, 32 * b:32 * b + 32],
                      dg_pn[32 * b:32 * b + 32, :, :])
            em.wait("T", "D")
            for h in range(16):
                em.op("T", "matmul", PU0[:, 32 * h:32 * h + 32],
                      bd_pn[:, h, 0:128], dg_tt[:, h, :], pgkey=("PT",))
            em.mm_close(("PT",))
            em.wait("D", "T")
            em.op("D", "tensor_tensor", dg_tt, dg_tt, PU0h, AL.add)
            em.wait("T", "D")

        if sub <= 6:
            em.barrier()
            return
        em.wait_dma("T", "A")
        PU0v = PU0.rearrange("p (a b) -> p a b", b=128)
        PG1zv = PG1.rearrange("p a b -> p (a b)") \
                   .rearrange("p (a b) -> p a b", b=256)
        ywv = y_lc.rearrange("p (w hh d) -> p w hh d", w=2, d=64)
        uwv = u_tk.rearrange("p (w hh d) -> p w hh d", w=2, d=64)
        xs4 = x_sol.rearrange("p (a two) b -> p a two b", two=2)
        em.wait("D", "D")
        for b in range(4):
            em.op("D", "tensor_copy",
                  bd_pn[32 * b:32 * b + 32, :, 32 * b:32 * b + 32],
                  dg_tt[32 * b:32 * b + 32, :, :])
        em.barrier(("T", "D"))
        for wv in range(2):
            for j in range(4):
                pr = 4 * wv + j
                em.op("T", "matmul", PG2[:, 2 * j, 0:64],
                      f_kh[0:64, pr, :], s_sb[0:64, pr, :],
                      pgkey=("RS", wv, j // 2))
                em.op("T", "matmul", PG2[:, 2 * j, 64:128],
                      f_kh[0:64, pr, :], c_id[0:64, 0:64],
                      pgkey=("RS", wv, j // 2))
            for j in range(4):
                pr = 4 * wv + j
                em.op("T", "matmul", PG2[:, 2 * j + 1, 0:64],
                      f_khO[:, pr, :], s_sb[:, pr, :],
                      pgkey=("RS", wv, j // 2))
                em.op("T", "matmul", PG2[:, 2 * j + 1, 64:128],
                      f_khO[:, pr, :], c_id[:, 64:128],
                      pgkey=("RS", wv, j // 2))
            for hh in range(8):
                h = 8 * wv + hh
                em.op("T", "matmul", PG2[:, hh, 0:64],
                      a_k[:, h, :], t_v[:, 64 * h:64 * h + 64],
                      pgkey=("RS", wv, hh // 4))
            em.mm_close(("RS", wv, 0), ("RS", wv, 1))
            if sub == 71:
                em.barrier()
                return
            em.wait("D", "T")
            em.wait("D", "D")
            em.op("D", "memset", x_sol, 0.0)
            for I in range(4):
                if I > 0:
                    em.wait("T", "D")
                    for hh in range(8):
                        h = 8 * wv + hh
                        em.op("T", "matmul", PG1[:, hh, :],
                              a_ss[:, h, :], x_sol[:, hh, :],
                              pgkey=("CR", wv, I, hh // 4))
                    em.mm_close(("CR", wv, I, 0), ("CR", wv, I, 1))
                em.wait("D", "T")
                em.op("D", "tensor_copy", x_ri[32 * I:32 * I + 32, :, :],
                      PG2[32 * I:32 * I + 32, :, :])
                if I > 0:
                    em.wait("D", "D")
                    em.op("D", "tensor_tensor",
                          x_ri[32 * I:32 * I + 32, :, :],
                          x_ri[32 * I:32 * I + 32, :, :],
                          PG1[32 * I:32 * I + 32, :, :], AL.add)
                em.wait("T", "D")
                for hh in range(8):
                    h = 8 * wv + hh
                    em.op("T", "matmul", PG0[:, hh, :],
                          bd_pn[:, h, 0:128], x_ri[:, hh, :],
                          pgkey=("SV", wv, I, hh // 4))
                em.mm_close(("SV", wv, I, 0), ("SV", wv, I, 1))
                em.wait("D", "T")
                em.op("D", "tensor_copy", x_sol[32 * I:32 * I + 32, :, :],
                      PG0[32 * I:32 * I + 32, :, :])
            em.wait("T", "D")
            if sub == 72:
                em.barrier()
                return
            for j in range(4):
                pr = 4 * wv + j
                em.op("T", "matmul", PG0[:, 2 * j, 0:64],
                      f_qd[0:64, pr, :], s_sb[0:64, pr, :],
                      pgkey=("Y", wv, j // 2))
            for j in range(4):
                pr = 4 * wv + j
                em.op("T", "matmul", PG0[:, 2 * j + 1, 0:64],
                      f_qdO[:, pr, :], s_sb[:, pr, :],
                      pgkey=("Y", wv, j // 2))
            for hh in range(8):
                h = 8 * wv + hh
                em.op("T", "matmul", PG0[:, hh, 0:64],
                      a_q[:, h, :], x_sol[:, hh, 0:64],
                      pgkey=("Y", wv, hh // 4))
                em.op("T", "matmul", PG0[:, hh, 0:64],
                      a_qk[:, h, :], t_v[:, 64 * h:64 * h + 64],
                      pgkey=("Y", wv, hh // 4))
            em.mm_close(("Y", wv, 0), ("Y", wv, 1))
            em.wait("D", "T")
            em.wait("D", "A")
            em.op("D", "tensor_copy", ywv[:, wv], PG0[:, :, 0:64])
            em.op("D", "tensor_copy", uwv[:, wv], x_sol[:, :, 64:128])
            em.op("D", "tensor_copy", x_u[:, :, 0:64], xs4[:, :, 0, 64:128])
            em.op("D", "tensor_copy", x_u[:, :, 64:128], xs4[:, :, 1, 64:128])
            em.op("D", "tensor_copy", x_h[:, :, 0:64], xs4[:, :, 0, 0:64])
            em.op("D", "tensor_copy", x_h[:, :, 64:128], xs4[:, :, 1, 0:64])
            em.wait("T", "D")
            if sub == 73:
                em.barrier()
                return
            for j in range(4):
                pr = 4 * wv + j
                em.op("T", "matmul", PG1zv[:, j, 0:256],
                      x_u[:, j, :],
                      a_q[:, 8 * wv + 2 * j:8 * wv + 2 * j + 2, :]
                      .rearrange("p a b -> p (a b)"),
                      pgkey=("Z", wv, j // 2))
                em.op("T", "matmul", PG1zv[:, j, 0:128],
                      c_idlo, f_qd[:, pr, :], pgkey=("Z", wv, j // 2))
                em.op("T", "matmul", PG1zv[:, j, 128:256],
                      c_idhi, f_qd[:, pr, :], pgkey=("Z", wv, j // 2))
            em.mm_close(("Z", wv, 0), ("Z", wv, 1))
            em.wait("D", "T")
            em.op("D", "tensor_copy", z_f[0:64, 4 * wv:4 * wv + 4, :],
                  PG1zv[0:64, :, 0:128])
            em.op("D", "tensor_copy", z_f[64:128, 4 * wv:4 * wv + 4, :],
                  PG1zv[64:128, :, 128:256])
            em.wait("T", "D")
            if sub == 74:
                em.barrier()
                return
            for j in range(4):
                pr = 4 * wv + j
                em.op("T", "matmul", PU0v[:, j, :],
                      bck_t[:, 128 * pr:128 * pr + 128], x_h[:, j, :],
                      pgkey=("SU", wv))
                em.op("T", "matmul", PU0v[:, j, :],
                      kck_t[:, 128 * pr:128 * pr + 128],
                      t_v[:, 128 * pr:128 * pr + 128], pgkey=("SU", wv))
            em.mm_close(("SU", wv))
            em.wait("D", "T")
            em.op("D", "tensor_add", st_t1[0:64, 4 * wv:4 * wv + 4, :],
                  s_st[0:64, 4 * wv:4 * wv + 4, :], PU0v[0:64, :, 0:64])
            em.op("D", "tensor_add", st_t1[64:128, 4 * wv:4 * wv + 4, :],
                  s_st[64:128, 4 * wv:4 * wv + 4, :],
                  PU0v[64:128, :, 64:128])
            em.wait("T", "D")
        em.wait("D", "D")
        em.op("D", "tensor_tensor", s_st, st_t1,
              cl_all[:, :, c:c + 1].broadcast_to([128, 8, 64]), AL.mult)
        em.wait("D", "D")
        em.op("D", "tensor_copy", s_sb, s_st)
        em.wait("T", "D")
        em.wait_dma("S", "S")
        em.wait("S", "D")
        for fc in range(8):
            em.dma("S", u_f[:, fc, :], u_tk[:, 128 * fc:128 * fc + 128],
                   transpose=True)
        em.wait("G", "D")
        em.wait_dma("G", "G")
        em.dma("G", s_yloc[c], y_lc)
        em.dma("G", s_zf[c], z_f)
        em.dma("G", s_bck[c], bck_t)
        em.wait_dma("G", "S")
        em.dma("G", s_uf[c], u_f)
        if sub <= 7:
            em.barrier()
            return
        em.wait_dma("T", "S")
        em.wait_dma("T", "A")
        PU1v = PU1.rearrange("p (a b) -> p a b", b=128)
        em.wait("D", "D")
        em.wait_dma("D", "S")
        em.op("D", "memset", u_fO[0:64, :, :], 0.0)
        em.wait("D", "D")
        em.op("D", "tensor_copy", u_fO[64:128, :, :], u_f[64:128, :, :])
        em.barrier(("T", "D"))
        for pr in range(8):
            em.op("T", "matmul", PG0[:, pr, 0:64],
                  u_f[0:64, pr, :], tr_sb[0:64, pr, :],
                  pgkey=("WT", pr // 4))
        for pr in range(8):
            em.op("T", "matmul", PG0[:, pr, 64:128],
                  u_fO[:, pr, :], tr_sb[:, pr, :], pgkey=("WT", pr // 4))
        em.mm_close(("WT", 0), ("WT", 1))
        em.wait("A", "T")
        em.op("A", "activation", w_sb, PG0.rearrange("p a b -> p (a b)"),
              AF.Copy)
        em.wait("T", "A")
        for pr in range(4):
            em.op("T", "matmul", PU0v[:, pr, :],
                  bck_t[:, 128 * pr:128 * pr + 128],
                  w_sb[:, 128 * pr:128 * pr + 128], pgkey=("TU", 0))
        for pr in range(4, 8):
            em.op("T", "matmul", PU1v[:, pr - 4, :],
                  bck_t[:, 128 * pr:128 * pr + 128],
                  w_sb[:, 128 * pr:128 * pr + 128], pgkey=("TU", 1))
        em.mm_close(("TU", 0), ("TU", 1))
        em.wait("D", "T")
        em.op("D", "tensor_add", st_t1[0:64, 0:4, :], tr_st[0:64, 0:4, :],
              PU0v[0:64, :, 0:64])
        em.op("D", "tensor_add", st_t1[64:128, 0:4, :],
              tr_st[64:128, 0:4, :], PU0v[64:128, :, 64:128])
        em.op("D", "tensor_add", st_t1[0:64, 4:8, :], tr_st[0:64, 4:8, :],
              PU1v[0:64, :, 0:64])
        em.op("D", "tensor_add", st_t1[64:128, 4:8, :],
              tr_st[64:128, 4:8, :], PU1v[64:128, :, 64:128])
        em.wait("D", "D")
        em.op("D", "tensor_tensor", tr_st, st_t1,
              cl_all[:, :, c:c + 1].broadcast_to([128, 8, 64]), AL.mult)
        em.wait("D", "D")
        em.op("D", "tensor_copy", tr_sb, tr_st)
        em.barrier(("T", "D"))


    passA_chunk = _passA_chunk
    if stage < 2:
        nchunks = 0
    for c in range(nchunks):
        passA_chunk(c)
    em.barrier()

    if do_scan and do_passb and stage >= 3:
        bdS3 = alias("bdS3", "bd_pn", [128, 8, 128])
        PU0s8 = PU0.rearrange("p (a d) -> p a d", d=64)
        em.wait("D", "T")
        em.wait("D", "D")
        em.op("D", "memset", bdS3, 0.0)
        em.wait("D", "D")
        em.op("D", "tensor_copy", bdS3[0:64, :, 0:64], tr_sb[0:64, :, :])
        em.op("D", "tensor_copy", bdS3[64:128, :, 64:128],
              tr_sb[64:128, :, :])
        em.barrier(("T", "D"))
        for pr in range(8):
            em.op("T", "matmul", PG0[:, pr, 0:64],
                  bdS3[:, pr, :], c_dup, pgkey=("SM", pr // 4))
        em.mm_close_all()
        em.wait("D", "T")
        em.op("D", "tensor_copy", st_t1, PG0[:, :, 0:64])
        em.wait("G", "D")
        em.wait_dma("G", "G")
        em.dma("G", cc_in[0:1024].rearrange("(pr p) i -> p pr i", p=128), s_st)
        em.dma("G", cc_in[1024:2048].rearrange("(pr p) i -> p pr i", p=128),
               st_t1)
        em.wait_dma("G", "G")
        if no_cc:
            em.op("D", "memset", gat_s, 0.0)
            em.op("D", "memset", gat_t, 0.0)
            em.wait("D", "D")
        else:
            em.collective([cc_in.opt()], [cc_out.opt()],
                          [[0, 1, 2, 3], [4, 5, 6, 7]])
        em.dma("G", w_bigA, r128(woT))
        em.dma("G", w_bigB, r128(wqT))
        em.barrier()
        for d in range(3):
            if not no_cc:
                em.wait_dma("G", "G")
                em.wait("G", "D")
                em.dma("G", gat_s, cc_out[2048 * d:2048 * d + 1024]
                       .rearrange("(pr p) i -> p pr i", p=128))
                em.dma("G", gat_t, cc_out[2048 * d + 1024:2048 * d + 2048]
                       .rearrange("(pr p) i -> p pr i", p=128))
                em.wait_dma("D", "G")
            em.wait("D", "D")
            em.op("D", "tensor_scalar_mul", trp, gat_t, sel[:, d:d + 1])
            em.op("D", "tensor_scalar_mul", st_t1, c_trid, sel[:, 4 + d:5 + d])
            em.wait("D", "D")
            em.op("D", "tensor_add", trp, trp,
                  st_t1.rearrange("p a b -> p a b"))
            em.op("D", "tensor_scalar_mul", sp_f, gat_s, sel[:, d:d + 1])
            em.wait("D", "T")
            em.wait("D", "D")
            em.op("D", "tensor_copy", bdS3[0:64, :, 0:64], trp[0:64, :, :])
            em.op("D", "tensor_copy", bdS3[64:128, :, 64:128],
                  trp[64:128, :, :])
            em.barrier(("T", "D"))
            for pr in range(8):
                em.op("T", "matmul", PU0s8[:, pr, :],
                      bdS3[:, pr, :], sict_b[:, pr, :], pgkey=("CM", d))
            em.mm_close_all()
            em.wait("D", "T")
            em.op("D", "tensor_add", sict, PU0s8, sp_f)
            em.wait("D", "D")
            em.op("D", "tensor_copy", sict_b, sict)
            em.wait("T", "D")
        em.barrier()

    def passB_chunk(c):
        em.wait_dma("S", "S")
        em.wait_dma("S", "G")
        em.wait("S", "D")
        em.dma("S", t_ld1, s_yloc[c])
        em.dma("S", t_ldp1, s_p1[c])
        em.dma("S", t_ldp2, s_p2[c])
        em.dma("S", z_f, s_zf[c])
        em.dma("S", u_f, s_uf[c])
        em.dma("S", bck_t, s_bck[c])
        em.dma("S", xt_in.rearrange("p f -> p f"), xtok[c * L:(c + 1) * L, :])
        em.wait_dma("T", "S")
        em.wait_dma("D", "S")
        PG0f2 = PG0.rearrange("p a b -> p (a b)")
        PU0v = PU0.rearrange("p (a b) -> p a b", b=128)
        PU1v = PU1.rearrange("p (a b) -> p a b", b=128)
        em.wait("D", "D")
        em.op("D", "memset", z_fO[0:64, :, :], 0.0)
        em.op("D", "memset", u_fO[0:64, :, :], 0.0)
        em.wait("D", "D")
        em.op("D", "tensor_copy", z_fO[64:128, :, :], z_f[64:128, :, :])
        em.op("D", "tensor_copy", u_fO[64:128, :, :], u_f[64:128, :, :])
        em.barrier(("T", "D"))
        for half in range(2):
            em.op("T", "matmul", PG0f2[:, 512 * half:512 * half + 512],
                  c_id, t_ld1[:, 512 * half:512 * half + 512],
                  pgkey=("PG0", half))
        for h in range(0, 16, 2):
            em.op("T", "matmul", PG0f2[:, 64 * h:64 * h + 64],
                  z_f[0:64, h // 2, :], sict_b[0:64, h // 2, :],
                  pgkey=("PG0", h // 8))
        for h in range(1, 16, 2):
            em.op("T", "matmul", PG0f2[:, 64 * h:64 * h + 64],
                  z_fO[:, h // 2, :], sict_b[:, h // 2, :],
                  pgkey=("PG0", h // 8))
        em.mm_close(("PG0", 0), ("PG0", 1))
        em.wait("A", "T")
        em.op("A", "activation", y_fin, PG0f2, AF.Copy)
        em.wait("T", "A")
        em.wait("D", "A")
        for pr in range(8):
            em.op("T", "matmul", PG1[:, pr, 0:64],
                  u_f[0:64, pr, :], sict_b[0:64, pr, :],
                  pgkey=("WB", pr // 4))
        for pr in range(8):
            em.op("T", "matmul", PG1[:, pr, 64:128],
                  u_fO[:, pr, :], sict_b[:, pr, :], pgkey=("WB", pr // 4))
        em.mm_close(("WB", 0), ("WB", 1))
        em.wait("A", "T")
        em.op("A", "activation", w_sb, PG1.rearrange("p a b -> p (a b)"),
              AF.Copy)
        em.wait("T", "A")
        for pr in range(4):
            em.op("T", "matmul", PU0v[:, pr, :],
                  bck_t[:, 128 * pr:128 * pr + 128],
                  w_sb[:, 128 * pr:128 * pr + 128], pgkey=("TU", 0))
        for pr in range(4, 8):
            em.op("T", "matmul", PU1v[:, pr - 4, :],
                  bck_t[:, 128 * pr:128 * pr + 128],
                  w_sb[:, 128 * pr:128 * pr + 128], pgkey=("TU", 1))
        em.mm_close(("TU", 0), ("TU", 1))
        em.wait("D", "T")
        em.op("D", "tensor_add", st_t1[0:64, 0:4, :], sict[0:64, 0:4, :],
              PU0v[0:64, :, 0:64])
        em.op("D", "tensor_add", st_t1[64:128, 0:4, :],
              sict[64:128, 0:4, :], PU0v[64:128, :, 64:128])
        em.op("D", "tensor_add", st_t1[0:64, 4:8, :], sict[0:64, 4:8, :],
              PU1v[0:64, :, 0:64])
        em.op("D", "tensor_add", st_t1[64:128, 4:8, :],
              sict[64:128, 4:8, :], PU1v[64:128, :, 64:128])
        em.wait("D", "D")
        em.op("D", "tensor_tensor", sict, st_t1,
              cl_all[:, :, c:c + 1].broadcast_to([128, 8, 64]), AL.mult)
        em.wait("D", "D")
        em.op("D", "tensor_copy", sict_b, sict)
        em.wait("T", "D")
        em.op("D", "tensor_mul", lad_m, y_fin, y_fin)
        em.op("D", "tensor_reduce", gn_st[:, 0:16],
              y_fin.rearrange("p (h d) -> p h d", d=64), AX.X, AL.add)
        em.wait("D", "D")
        em.op("D", "tensor_reduce", gn_st[:, 16:32],
              lad_m.rearrange("p (h d) -> p h d", d=64), AX.X, AL.add)
        em.wait("D", "D")
        em.op("D", "tensor_scalar_mul", gn_st[:, 0:16], gn_st[:, 0:16],
              1.0 / 64.0)
        em.op("D", "tensor_scalar_mul", gn_st[:, 16:32], gn_st[:, 16:32],
              1.0 / 64.0)
        em.wait("D", "D")
        em.op("D", "tensor_mul", gn_st[:, 32:48], gn_st[:, 0:16],
              gn_st[:, 0:16])
        em.wait("D", "D")
        em.op("D", "tensor_sub", gn_st[:, 16:32], gn_st[:, 16:32],
              gn_st[:, 32:48])
        em.wait("A", "D")
        em.op("A", "activation", gn_st[:, 32:48], gn_st[:, 16:32], AF.Ln,
              bias=cb[:, 2:3], scale=1.0)
        em.wait("A", "A")
        em.op("A", "activation", gn_st[:, 48:64], gn_st[:, 32:48], AF.Exp,
              scale=-0.5)
        em.wait("D", "A")
        em.op("D", "tensor_sub", y_fin.rearrange("p (h d) -> p h d", d=64),
              y_fin.rearrange("p (h d) -> p h d", d=64),
              gn_st[:, 0:16].unsqueeze(2)
              .broadcast_to([128, 16, 64]))
        em.wait("D", "D")
        em.op("D", "tensor_mul", y_fin.rearrange("p (h d) -> p h d", d=64),
              y_fin.rearrange("p (h d) -> p h d", d=64),
              gn_st[:, 48:64].unsqueeze(2)
              .broadcast_to([128, 16, 64]))
        em.wait("D", "D")
        em.op("D", "tensor_mul", lad_m, y_fin, t_ldp1)
        em.wait("D", "D")
        em.op("D", "tensor_add", y2_t, lad_m, t_ldp2)
        em.wait_dma("A", "A")
        em.wait("A", "D")
        em.wait_dma("S", "S")
        em.wait("S", "D")
        for fc in range(8):
            em.dma("A" if fc % 2 == 0 else "S", y2_f[:, fc, :],
                   y2_t[:, 128 * fc:128 * fc + 128], transpose=True)
        em.wait_dma("T", "A")
        em.wait_dma("T", "S")
        for half in range(2):
            for kc in range(8):
                em.op("T", "matmul", PU1[:, 0:512], y2_f[:, kc, :],
                      w_bigA[:, kc, 512 * half:512 * half + 512],
                      pgkey=("PU1", 0))
            em.mm_close(("PU1", 0))
            em.wait("D", "T")
            em.op("D", "tensor_tensor", x2_t[:, 512 * half:512 * half + 512],
                  PU1[:, 0:512], xt_in[:, 512 * half:512 * half + 512], AL.add)
            em.wait("T", "D")
        em.wait("D", "D")
        em.op("D", "tensor_reduce", bn2[:, 0:1], x2_t, AX.X, AL.add)
        em.wait("D", "D")
        em.op("D", "tensor_scalar_mul", bn2[:, 0:1], bn2[:, 0:1], 1.0 / 1024.0)
        em.wait("D", "D")
        em.op("D", "tensor_scalar", lad_u, x2_t, bn2[:, 0:1], None,
              AL.subtract)
        em.wait("D", "D")
        em.op("D", "tensor_mul", lad_m, lad_u, lad_u)
        em.wait("D", "D")
        em.op("D", "tensor_reduce", bn2[:, 1:2], lad_m, AX.X, AL.add)
        em.wait("D", "D")
        em.op("D", "tensor_scalar_mul", bn2[:, 1:2], bn2[:, 1:2], 1.0 / 1024.0)
        em.wait("A", "D")
        em.op("A", "activation", q_c1[:, 0:1], bn2[:, 1:2], AF.Ln,
              bias=cb[:, 0:1], scale=1.0)
        em.wait("A", "A")
        em.op("A", "activation", q_c1[:, 0:1], q_c1[:, 0:1], AF.Exp,
              scale=-0.5)
        em.wait("D", "D")
        em.op("D", "tensor_reduce", q_c1[:, 1:2], lad_u, AX.X, AL.max,
              apply_absolute_value=True)
        em.wait("D", "D")
        em.wait("D", "A")
        em.op("D", "tensor_mul", q_c1[:, 2:3], q_c1[:, 1:2], q_c1[:, 0:1])
        em.wait("D", "D")
        em.op("D", "tensor_scalar_max", q_c1[:, 2:3], q_c1[:, 2:3], 1e-6)
        em.wait("D", "D")
        em.op("D", "reciprocal", q_c1[:, 3:4], q_c1[:, 2:3])
        em.wait("D", "D")
        em.op("D", "tensor_mul", q_c1[:, 4:5], q_c1[:, 3:4], q_c1[:, 0:1])
        em.op("D", "tensor_scalar_mul", q_c1[:, 5:6], q_c1[:, 2:3], 1.0 / 127.0)
        em.wait("D", "D")
        em.op("D", "tensor_scalar_mul", q_c1[:, 4:5], q_c1[:, 4:5], 127.0)
        em.wait("D", "D")
        em.op("D", "tensor_scalar", lad_u, lad_u, q_c1[:, 4:5], 128.5,
              AL.mult, AL.add)
        em.wait("D", "D")
        em.op("D", "tensor_copy", lad_t, lad_u)
        for k in (7, 6, 5, 4, 3, 2, 1, 0):
            em.wait("D", "D")
            em.op("D", "tensor_scalar", lad_m, lad_t, float(2 ** k),
                  float(-(2 ** k)), AL.is_ge, AL.mult)
            em.wait("D", "D")
            em.op("D", "tensor_add", lad_t, lad_t, lad_m)
        em.wait("D", "D")
        em.op("D", "tensor_sub", lad_u, lad_u, lad_t)
        em.wait("D", "D")
        em.op("D", "tensor_scalar_add", xq_t, lad_u, -128.0)
        em.wait_dma("A", "A")
        em.wait("A", "D")
        em.wait_dma("S", "S")
        em.wait("S", "D")
        for fc in range(8):
            em.dma("A" if fc % 2 == 0 else "S", xq_f[:, fc, :],
                   xq_t[:, 128 * fc:128 * fc + 128], transpose=True)
        em.wait_dma("T", "A")
        em.wait_dma("T", "S")
        for half in range(2):
            for kc in range(8):
                em.op("T", "matmul", PU1[:, 0:512], xq_f[:, kc, :],
                      w_bigB[:, kc, 512 * half:512 * half + 512],
                      pgkey=("PU1", 0))
            em.mm_close(("PU1", 0))
            em.wait("D", "T")
            sl = slice(512 * half, 512 * half + 512)
            em.op("D", "tensor_scalar", lad_m[:, sl], PU1[:, 0:512],
                  q_c1[:, 5:6], None, AL.mult)
            em.wait("T", "D")
        em.wait("D", "D")
        em.op("D", "tensor_mul", lad_m, lad_m, bc_ws)
        em.op("D", "tensor_add", lad_t, x2_t, bc_fb)
        em.wait("D", "D")
        em.wait_dma("D", "G")
        em.op("D", "tensor_add", o_fin, lad_m, lad_t)
        em.wait("G", "D")
        em.wait_dma("G", "G")
        em.dma("G", out[c * L:(c + 1) * L, :], o_fin)

    if do_passb and do_scan and stage >= 4:
        for c in range(nchunks):
            passB_chunk(c)
        em.barrier()
    else:
        em.op("D", "memset", o_fin, 0.0)
        em.wait("G", "D")
        for i in range(16):
            em.dma("G", out[128 * i:128 * i + 128, :], o_fin)
    em.wait_dma("G", "G")

    for name in debug_dump:
        ap_ = SB[name]
        o = nc.dram_tensor("dbg_" + name, list(ap_.shape), ap_.dtype,
                           kind="ExternalOutput").ap()
        em.dma("G", o, ap_)
    em.wait_dma("G", "G")
    em.play()
    return nc


def _host_prep(inputs):
    x = np.asarray(inputs["L0_attn_x"])
    g = {}
    g["wrT"] = np.ascontiguousarray(np.asarray(inputs["L0_attn_Wr"]).T)
    g["wkT"] = np.ascontiguousarray(np.asarray(inputs["L0_attn_Wk"]).T)
    g["wvT"] = np.ascontiguousarray(np.asarray(inputs["L0_attn_Wv"]).T)
    g["woT"] = np.ascontiguousarray(np.asarray(inputs["L0_attn_Wo"]).T)
    g["wqT"] = np.ascontiguousarray(np.asarray(inputs["L0_ffn_weight_q"]).T)
    g["maa_w1T"] = np.ascontiguousarray(np.asarray(inputs["L0_attn_maa_w1"]).T)
    g["maa_w2"] = np.ascontiguousarray(
        np.asarray(inputs["L0_attn_maa_w2"]).reshape(128, 1024))
    g["gate_w1T"] = np.ascontiguousarray(np.asarray(inputs["L0_attn_gate_w1"]).T)
    g["gate_w2"] = np.ascontiguousarray(np.asarray(inputs["L0_attn_gate_w2"]))
    g["decay_w1T"] = np.ascontiguousarray(np.asarray(inputs["L0_attn_decay_w1"]).T)
    g["decay_w2"] = np.ascontiguousarray(np.asarray(inputs["L0_attn_decay_w2"]))
    s1 = np.zeros((1024, 128), np.float32)
    s2 = np.zeros((128, 1024), np.float32)
    for i, n in enumerate(("aaa", "ma", "mk", "kkk")):
        s1[:, 32 * i:32 * i + 16] = np.asarray(inputs["L0_attn_%s_w1" % n]).T
        s2[32 * i:32 * i + 16, :] = np.asarray(inputs["L0_attn_%s_w2" % n])
    g["s_w1p"], g["s_w2p"] = s1, s2
    vr = np.zeros((16, 1024), np.float32)
    vr[0] = np.asarray(inputs["L0_attn_time_maa_x"])
    tm = np.asarray(inputs["L0_attn_time_maa"]).reshape(4, 1024)
    for gg in range(4):
        vr[1 + gg] = tm[gg]
    vr[5] = np.asarray(inputs["L0_attn_time_decay"])
    vr[6] = np.asarray(inputs["L0_attn_time_aaaaa"])
    vr[7] = np.asarray(inputs["L0_attn_time_misc_a"])
    vr[8] = np.asarray(inputs["L0_attn_time_misc_k"])
    vr[9] = np.asarray(inputs["L0_attn_ln_w"])
    vr[10] = np.asarray(inputs["L0_attn_ln_b"])
    vr[11] = np.asarray(inputs["L0_ffn_weight_scale"])
    vr[12] = np.asarray(inputs["L0_ffn_bias"])
    g["vrows"] = vr
    vfm = np.zeros((128, 128), np.float32)
    for i in range(16):
        for fc in range(8):
            vfm[:, fc * 16 + i] = vr[i, fc * 128:fc * 128 + 128]
    g["vfmi"] = vfm
    fa = np.asarray(inputs["L0_attn_time_faaaa"])
    fb = np.zeros((1024, 16), np.float32)
    hb = np.zeros((1024, 16), np.float32)
    idx = np.arange(1024)
    fb[idx, idx // 64] = fa
    hb[idx, idx // 64] = 1.0
    g["fblk"], g["hblk"] = fb, hb
    hs = np.zeros((16, 1024), np.float32)
    hs[idx // 64, idx] = 1.0
    g["hsel"] = hs
    g["ident"] = np.eye(128, dtype=np.float32)
    g["idlo"] = np.diag(np.concatenate([np.ones(64), np.zeros(64)])
                        ).astype(np.float32)
    g["idhi"] = np.diag(np.concatenate([np.zeros(64), np.ones(64)])
                        ).astype(np.float32)
    g["iddup"] = np.concatenate([np.eye(64), np.eye(64)]).astype(np.float32)
    idb = np.zeros((128, 32), np.float32)
    for b in range(4):
        idb[32 * b:32 * b + 32] = np.eye(32)
    g["idblk"] = idb
    sg, tg = np.meshgrid(np.arange(128), np.arange(128), indexing="ij")
    g["mask_s"] = (sg < tg).astype(np.float32)
    g["mask_i"] = (sg <= tg).astype(np.float32)
    g["onesc"] = np.ones((128, 128), np.float32)

    in_maps = []
    for c in range(8):
        b, q = c // 4, c % 4
        t0 = q * NTOK
        m = dict(g)
        xt = np.ascontiguousarray(x[b].T)
        xc = np.zeros((1024, NTOK + 1), np.float32)
        xc[:, 1:] = xt[:, t0:t0 + NTOK]
        if t0 > 0:
            xc[:, 0] = xt[:, t0 - 1]
        m["xcf"] = xc
        m["xtok"] = np.ascontiguousarray(x[b, t0:t0 + NTOK])
        s = np.zeros((128, 8), np.float32)
        s[:, :q] = 1.0
        s[:, 4:8] = 1.0
        s[:, 4:4 + q] = 0.0
        m["selq"] = s
        in_maps.append(m)
    return in_maps


def kernel(**inputs):
    in_maps = _host_prep(inputs)
    nc = build_nc()
    res = run_bass_kernel_spmd(nc, in_maps, core_ids=list(range(8)))
    x = np.asarray(inputs["L0_attn_x"])
    out = np.zeros_like(x, dtype=np.float32)
    for c in range(8):
        b, q = c // 4, c % 4
        out[b, q * NTOK:(q + 1) * NTOK] = res.results[c]["out"]
    return out
